# version 31
# baseline (speedup 1.0000x reference)
import contextlib
import numpy as np
import concourse.bass as bass
import concourse.mybir as mybir

F32 = mybir.dt.float32
BF16 = mybir.dt.bfloat16
I8 = mybir.dt.int8
F8 = mybir.dt.float8e4
AF = mybir.ActivationFunctionType
ALU = mybir.AluOpType
AX = mybir.AxisListType

D = 1024
DH = 4096
S = 2048
BPC = 4
T = 256
NSEQT = S // T
NTILES = BPC * NSEQT
KC = D // 128
CC = DH // 128
NB = T // 128
TH = T + 2
EPS = 1e-5
INV_D = 1.0 / D
MAGIC = 12582912.0


class Ctx:

    def __init__(self, eng, sem, base, sems_h):
        self._eng = eng
        self._semname = sem
        self._val = base
        self._sems_h = sems_h
        self._last = None
        self.nbar = 0

    def __getattr__(self, name):
        fn = getattr(self._eng, name)
        if callable(fn):
            def wrap(*a, **k):
                ins = fn(*a, **k)
                self._last = ins
                return ins
            return wrap
        return fn

    def barrier(self):
        assert self._last is not None, "barrier with no prior instruction"
        self._val += 1
        self.nbar += 1
        self._last.then_inc(self._sems_h[self._semname], 1)
        self._eng.wait_ge(self._sems_h[self._semname], self._val)
        self._last = None


class Sched:
    def __init__(self, sems):
        self.ops = []
        self.cnt = {s: 0 for s in sems}
        self.sems = sems

    def op(self, eng, emitfn, waits=(), barriers=0, n_dma=0):
        waits = tuple(w for w in waits if w is not None)
        if n_dma:
            self.cnt[eng] += 16 * n_dma
            tok = (eng, self.cnt[eng])
            self.ops.append(("dma", eng, emitfn, n_dma, waits, None))
            return tok
        base = self.cnt[eng]
        self.cnt[eng] += barriers + 1
        tok = (eng, self.cnt[eng])
        self.ops.append(("cmp", eng, emitfn, barriers, waits, base))
        return tok


def host_prepare(inputs, core):
    x = np.asarray(inputs["L0_attn_x"])
    xT = np.ascontiguousarray(np.transpose(x[core * BPC:(core + 1) * BPC], (0, 2, 1)))
    cT = lambda a: np.ascontiguousarray(np.asarray(a).T).astype(np.float32)
    col = lambda a, n: np.ascontiguousarray(np.asarray(a).astype(np.float32).reshape(n, 128).T)
    return {
        "x": xT.astype(np.float32),
        "wq": cT(inputs["L0_attn_Wq"]), "wv": cT(inputs["L0_attn_Wv"]),
        "wg": cT(inputs["L0_attn_Wg"]), "wo": cT(inputs["L0_attn_Wo"]),
        "w1t": cT(inputs["L0_ffn_w1"]), "w2t": cT(inputs["L0_ffn_w2"]),
        "beta_col": col(np.repeat(np.asarray(inputs["L0_attn_beta"]).astype(np.float32), 64), KC),
        "lng_col": col(inputs["L0_attn_ln_gamma"], KC),
        "lnb_col": col(inputs["L0_attn_ln_beta"], KC),
        "cw_col": np.ascontiguousarray(
            np.asarray(inputs["L0_ffn_conv_w"]).astype(np.float32).reshape(CC, 128, 3)
            .transpose(1, 0, 2).reshape(128, CC * 3)),
        "cb_col": col(inputs["L0_ffn_conv_b"], CC),
        "ident": np.eye(128, dtype=np.float32),
        "ones1row": np.ones((1, 128), dtype=np.float32),
        "onescol_bf": np.ones((128, 1), dtype=np.float32),
        "onescol_f": np.ones((128, 1), dtype=np.float32),
        "onesT_h": np.ones((128, T), dtype=np.float32),
        "zpad_h": np.zeros((BPC, D, 2), dtype=np.float32),
        "zcol_h": np.zeros((128, KC), dtype=np.float32),
    }


def build_nc(phases="AB", ntiles_a=NTILES, ntiles_b=NTILES):
    nc = bass.Bass("TRN2", target_bir_lowering=False)
    x_h = nc.dram_tensor("x", [BPC, D, S], F32, kind="ExternalInput").ap()
    w_h = {nm: nc.dram_tensor(nm, [D, D], F32, kind="ExternalInput").ap()
           for nm in ("wq", "wv", "wg", "wo")}
    w1t_h = nc.dram_tensor("w1t", [D, DH], F32, kind="ExternalInput").ap()
    w2t_h = nc.dram_tensor("w2t", [DH, D], F32, kind="ExternalInput").ap()
    beta_h = nc.dram_tensor("beta_col", [128, KC], F32, kind="ExternalInput").ap()
    lng_h = nc.dram_tensor("lng_col", [128, KC], F32, kind="ExternalInput").ap()
    lnb_h = nc.dram_tensor("lnb_col", [128, KC], F32, kind="ExternalInput").ap()
    cw_h = nc.dram_tensor("cw_col", [128, CC * 3], F32, kind="ExternalInput").ap()
    cb_h = nc.dram_tensor("cb_col", [128, CC], F32, kind="ExternalInput").ap()
    id_h = nc.dram_tensor("ident", [128, 128], F32, kind="ExternalInput").ap()
    ones1_h = nc.dram_tensor("ones1row", [1, 128], F32, kind="ExternalInput").ap()
    onescol_h = nc.dram_tensor("onescol_bf", [128, 1], F32, kind="ExternalInput").ap()
    onescolf_h = nc.dram_tensor("onescol_f", [128, 1], F32, kind="ExternalInput").ap()
    onesT_hh = nc.dram_tensor("onesT_h", [128, T], F32, kind="ExternalInput").ap()
    zpad_hh = nc.dram_tensor("zpad_h", [BPC, D, 2], F32, kind="ExternalInput").ap()
    zcol_hh = nc.dram_tensor("zcol_h", [128, KC], F32, kind="ExternalInput").ap()
    out_h = nc.dram_tensor("out", [BPC, D, S], F32, kind="ExternalOutput").ap()
    x1s_h = nc.dram_tensor("x1s", [BPC, D, S], F32, kind="Internal").ap()

    es = contextlib.ExitStack()
    with es:
        def sb(name, shape, dt, stack=None):
            return (stack or es).enter_context(nc.sbuf_tensor(name, shape, dt))
        ident = sb("ident_sb", [128, 128], F32)
        ones1 = sb("ones1_sb", [1, 128], F32)
        onescol = sb("onescol_sb", [128, 1], BF16)
        lamcol = sb("lamcol", [128, KC], F32)
        betac = sb("betac", [128, KC], F32)
        lngc = sb("lngc", [128, KC], F32)
        lnbc = sb("lnbc", [128, KC], F32)
        cwc = sb("cwc", [128, CC * 3], F32)
        cbc = sb("cbc", [128, CC], F32)
        carry = sb("carry", [128, KC], F32)
        mrow_o = sb("mrow_o", [128, KC], BF16)
        mrow_f = sb("mrow_f", [128, CC], F32)
        mrow2 = sb("mrow2", [128, CC], BF16)
        srow = sb("srow", [1, 2 * TH], F32)
        stok = sb("stok", [128, 24], F32)
        stokm = sb("stokm", [128, 12], F32)
        srowm = sb("srowm", [1, 2 * TH], F32)
        srowmB = sb("srowmB", [1, 2 * TH], F32)
        rtmp = sb("rtmp", [1, T], F32)
        rtmpB = sb("rtmpB", [1, T], F32)
        sqb = sb("sqb", [128, KC * T], BF16)
        swcol = sb("swcol", [128, 4], F32)
        zsb = sb("zsb", [128, KC], F32)
        onescolf = sb("onescolf", [128, 1], F32)
        tinyrow = sb("tinyrow", [1, 8], F32)
        wtmp1 = [sb(f"wtmp1_{i}", [128, 1024], F32) for i in range(2)]
        wacc1 = sb("wacc1", [128, 2], F32)
        wsum1 = sb("wsum1", [128, 2], F32)

        semnames = ["t", "v", "s", "g", "qp", "qw", "qx0", "qx1", "qx2", "qr2", "gx0", "gx1",
                    "qr0", "qr1", "go0", "go1", "qz", "pw0", "pw1", "pw2", "pw3"]
        sems = {n: es.enter_context(nc.semaphore("sem_" + n)) for n in semnames}
        ps = [es.enter_context(nc.psum_tensor(f"ps{i}", [128, 512], F32)) for i in range(8)]
        blk = es.enter_context(nc.Block())
        sch = Sched(sems)

        def p0(e):
            return [e.dma_start(out=ident[:, :], in_=id_h[:, :]),
                    e.dma_start(out=ones1[:, :], in_=ones1_h[:, :]),
                    e.dma_start(out=betac[:, :], in_=beta_h[:, :]),
                    e.dma_start(out=lngc[:, :], in_=lng_h[:, :]),
                    e.dma_start(out=lnbc[:, :], in_=lnb_h[:, :]),
                    e.dma_start(out=cwc[:, :], in_=cw_h[:, :]),
                    e.dma_start(out=cbc[:, :], in_=cb_h[:, :]),
                    e.dma_start(out=carry[:, :], in_=zcol_hh[:, :]),
                    e.dma_start(out=zsb[:, :], in_=zcol_hh[:, :]),
                    e.dma_start(out=onescolf[:, :], in_=onescolf_h[:, :])]
        t_p0 = sch.op("qp", p0, n_dma=10)
        t_onescol = sch.op("qw", lambda e: e.dma_start(out=onescol[:, :], in_=onescol_h[:, :]),
                           n_dma=1)
        t_lam = sch.op("s", lambda c: c.activation(lamcol[:, :], betac[:, :], AF.Sigmoid),
                       waits=[t_p0])

        scopeA = contextlib.ExitStack()
        wsb = {nm: sb(nm + "_sb", [128, KC * D], BF16, scopeA) for nm in ("wq", "wv", "wg", "wo")}
        lamtile = sb("lamtile", [128, KC * T], F32, scopeA)
        onesT = sb("onesT", [128, T], F32, scopeA)
        t_onesT = sch.op("qp", lambda e: e.dma_start(out=onesT[:, :], in_=onesT_hh[:, :]),
                         n_dma=1, waits=[t_p0])

        def mk_lamtile(c):
            r = None
            for k in range(KC):
                r = c.tensor_scalar(lamtile[:, k * T:(k + 1) * T], onesT[:, :],
                                    lamcol[:, k:k + 1], None, ALU.mult)
            return r
        t_lamtile = sch.op("v", mk_lamtile, waits=[t_lam, t_onesT])

        def wload(e):
            out = []
            for nm in ("wq", "wv", "wg", "wo"):
                src = w_h[nm][:, :].rearrange("(k p) m -> p k m", p=128)
                dst = wsb[nm][:, :].rearrange("p (k m) -> p k m", k=KC)
                out.append(e.dma_start(out=dst, in_=src))
            return out
        t_w = sch.op("qw", wload, n_dma=4, waits=[t_onescol])

        def mk_mrow(c):
            for k in range(KC):
                c.reduce_sum(mrow_f[:, k:k + 1], wsb["wo"][:, k * D:(k + 1) * D], axis=AX.X)
            c.barrier()
            return c.tensor_copy(mrow_o[:, :], mrow_f[:, 0:KC])
        t_mrow = sch.op("v", mk_mrow, barriers=1, waits=[t_w])

        def ln_machinery(t_statrow, btr, bbc):
            def rowmath(c):
                c.tensor_scalar(srowm[0:1, 0:T], srowm[0:1, 0:T], INV_D, None, ALU.mult)
                c.tensor_scalar(srowm[0:1, TH:TH + T], srowm[0:1, TH:TH + T], INV_D, None, ALU.mult)
                c.barrier()
                c.tensor_tensor(srowm[0:1, TH + T:TH + T + T] if False else rtmp[0:1, 0:T],
                                srowm[0:1, 0:T], srowm[0:1, 0:T], ALU.mult)
                c.barrier()
                c.tensor_tensor(srowm[0:1, TH:TH + T], srowm[0:1, TH:TH + T], rtmp[0:1, 0:T],
                                ALU.subtract)
                c.barrier()
                c.tensor_scalar(srowm[0:1, TH:TH + T], srowm[0:1, TH:TH + T], EPS, None, ALU.add)
                c.barrier()
                return c.reciprocal(rtmp[0:1, 0:T], srowm[0:1, TH:TH + T])
            t_rm = sch.op("v", rowmath, barriers=4, waits=[t_statrow])
            t_sq = sch.op("s", lambda c: c.activation(srowm[0:1, TH:TH + T], rtmp[0:1, 0:T],
                                                      AF.Sqrt), waits=[t_rm])
            def negmu(c):
                c.tensor_tensor(srowm[0:1, 0:T], srowm[0:1, 0:T], srowm[0:1, TH:TH + T], ALU.mult)
                c.barrier()
                return c.tensor_scalar(srowm[0:1, 0:T], srowm[0:1, 0:T], -1.0, None, ALU.mult)
            t_nm = sch.op("v", negmu, barriers=1, waits=[t_sq])

            def bcast(c):
                c.matmul(ps[bbc][:, 0:T], ones1[0:1, :], srowm[0:1, TH:TH + T],
                         start=True, stop=True)
                return c.matmul(ps[bbc][:, 256:256 + T], ones1[0:1, :], srowm[0:1, 0:T],
                                start=True, stop=True)
            return sch.op("t", bcast, waits=[t_nm])

        xt = [sb(f"xt{i}", [128, KC * T], F32, scopeA) for i in range(3)]
        xb = [sb(f"xb{i}", [128, KC * T], BF16, scopeA) for i in range(2)]
        qbf = [sb(f"qbf{i}", [128, KC * T], BF16, scopeA) for i in range(2)]
        vfull = [sb(f"vfull{i}", [128, KC * T], F32, scopeA) for i in range(2)]
        state = [sb(f"state{i}", [128, KC * T], F32, scopeA) for i in range(2)]
        gate = [sb(f"gate{i}", [128, KC * T], BF16, scopeA) for i in range(2)]
        ybf = sb("ybf", [128, KC * T], BF16, scopeA)
        z1 = sb("z1", [128, KC * T], F32, scopeA)
        y1b = sb("y1b", [128, KC * T], BF16, scopeA)
        x1t = [sb(f"x1t{i}", [128, KC * T], F32, scopeA) for i in range(2)]

        xt_free = [None, None, None]
        xb_free = [None, None]
        x1_dma = [None, None]
        prev_scan = [None]
        bankA = {}
        bankWo = {}
        sty_tok = [None]
        st1_tok = [None]
        y1_par = [None, None]
        a1_prev = [None]
        row1_prev = [None]
        pend_tail = [[]]
        aft = {}

        def a_ld(ti):
            if ti >= ntiles_a:
                return
            buf3 = ti % 3
            qx = f"qx{ti % 3}"
            b, si = divmod(ti, NSEQT)
            s0 = si * T
            def ld(e, b=b, s0=s0, buf3=buf3):
                srcap = x_h[b, :, s0:s0 + T].rearrange("(k p) t -> p k t", p=128)
                dst = xt[buf3][:, :].rearrange("p (k t) -> p k t", k=KC)
                return e.dma_start(out=dst, in_=srcap)
            aft[("x", ti)] = sch.op(qx, ld, n_dma=1, waits=[xt_free[buf3]])

        def front_emitters(ti):
            if ti >= ntiles_a:
                return []
            buf = ti % 2
            fs = {}
            aft[ti] = fs
            out = []
            buf3 = ti % 3
            def e_xb():
                t_x = aft.pop(("x", ti))
                fs["xb"] = sch.op("s", lambda c, buf=buf, buf3=buf3: c.activation(
                    xb[buf][:, :], xt[buf3][:, :], AF.Identity),
                    waits=[t_x, xb_free[buf]])
            out.append(e_xb)
            evq, evv, evg = [], [], []
            for wname, dstl, evl in (("wq", qbf, evq), ("wv", vfull, evv), ("wg", gate, evg)):
                for mp in range(4):
                    def e_mm(wname=wname, mp=mp):
                        bank = ps[mp]
                        def mm(c, wname=wname, mp=mp, bank=bank, buf=buf):
                            r = None
                            for k in range(KC):
                                for half in range(2):
                                    m = 2 * mp + half
                                    r = c.matmul(bank[:, half * 256:half * 256 + T],
                                                 wsb[wname][:, k * D + m * 128:k * D + (m + 1) * 128],
                                                 xb[buf][:, k * T:(k + 1) * T],
                                                 start=(k == 0 and half == 0),
                                                 stop=(k == KC - 1 and half == 1))
                            return r
                        fs[("mm", wname, mp)] = sch.op("t", mm,
                                                       waits=[fs["xb"], t_w, bankA.get(mp)])
                    def e_ev(wname=wname, mp=mp, dstl=dstl, evl=evl):
                        bank = ps[mp]
                        if wname == "wg":
                            emitf = lambda c, dstl=dstl, mp=mp, bank=bank, buf=buf: c.activation(
                                dstl[buf][:, 2 * mp * T:(2 * mp + 2) * T], bank[:, 0:2 * T],
                                AF.Silu)
                            w2_ = [y1_par[buf]]
                        else:
                            emitf = lambda c, dstl=dstl, mp=mp, bank=bank, buf=buf: c.activation(
                                dstl[buf][:, 2 * mp * T:(2 * mp + 2) * T], bank[:, 0:2 * T],
                                AF.Identity)
                            w2_ = []
                        t = sch.op("s", emitf, waits=[fs[("mm", wname, mp)]] + w2_)
                        evl.append(t)
                        bankA[mp] = t
                    out += [e_mm, e_ev]
            def e_scan():
                b, si = divmod(ti, NSEQT)
                def scans(c, si=si, buf=buf):
                    if si == 0:
                        c.tensor_scalar(carry[:, :], carry[:, :], 0.0, None, ALU.mult)
                        c.barrier()
                    for k in range(KC):
                        c.tensor_tensor_scan(state[buf][:, k * T:(k + 1) * T],
                                             lamtile[:, k * T:(k + 1) * T],
                                             vfull[buf][:, k * T:(k + 1) * T],
                                             carry[:, k:k + 1], ALU.mult, ALU.add)
                    c.barrier()
                    r = None
                    for k in range(KC):
                        r = c.tensor_copy(carry[:, k:k + 1],
                                          state[buf][:, (k + 1) * T - 1:(k + 1) * T])
                    return r
                fs["scan"] = sch.op("v", scans, barriers=(2 if si == 0 else 1),
                                    waits=evv + [t_lamtile, prev_scan[0], t_p0])
                prev_scan[0] = fs["scan"]
            out.append(e_scan)
            fs["evq"] = evq
            fs["evg"] = evg
            return out

        def back_emitters(ti):
            buf = ti % 2
            buf3 = ti % 3
            b, si = divmod(ti, NSEQT)
            s0 = si * T
            ft = aft.pop(ti)
            bt = {}
            def w_ty():
                bt["ty"] = sch.op("v", lambda c, buf=buf: c.tensor_tensor(
                    ybf[:, :], qbf[buf][:, :], state[buf][:, :], ALU.mult),
                    waits=[ft["scan"], sty_tok[0]] + ft["evq"])
            def w_ysq():
                bt["ysq"] = sch.op("s", lambda c: c.activation(sqb[:, :], ybf[:, :], AF.Square),
                                   waits=[bt["ty"], st1_tok[0]])
            def w_statsy():
                def stats_y(c):
                    for k in range(KC):
                        c.matmul(ps[7][0:1, 0:T], onescol[:, 0:1], ybf[:, k * T:(k + 1) * T],
                                 start=(k == 0), stop=(k == KC - 1))
                    r = None
                    for k in range(KC):
                        r = c.matmul(ps[7][0:1, 256:256 + T], onescol[:, 0:1],
                                     sqb[:, k * T:(k + 1) * T],
                                     start=(k == 0), stop=(k == KC - 1))
                    return r
                bt["sty"] = sch.op("t", stats_y,
                                   waits=[bt["ysq"], t_onescol, bt["ty"], row1_prev[0]])
                sty_tok[0] = bt["sty"]
            def w_rowy():
                def rowy(c):
                    c.tensor_copy(srowm[0:1, 0:T], ps[7][0:1, 0:T])
                    return c.tensor_copy(srowm[0:1, TH:TH + T], ps[7][0:1, 256:256 + T])
                bt["rowy"] = sch.op("v", rowy, waits=[bt["sty"]])
            def mk_rm(px, prev_key, srow, rt):
                def w_rm1():
                    def rm1(c):
                        c.tensor_scalar(srow[0:1, 0:T], srow[0:1, 0:T], INV_D, None, ALU.mult)
                        return c.tensor_scalar(srow[0:1, TH:TH + T], srow[0:1, TH:TH + T],
                                               INV_D, None, ALU.mult)
                    bt[px + "rm1"] = sch.op("v", rm1, waits=[bt[prev_key]])
                def w_rm2():
                    bt[px + "rm2"] = sch.op("v", lambda c: c.tensor_tensor(
                        rt[0:1, 0:T], srow[0:1, 0:T], srow[0:1, 0:T], ALU.mult),
                        waits=[bt[px + "rm1"]])
                def w_rm3():
                    bt[px + "rm3"] = sch.op("v", lambda c: c.scalar_tensor_tensor(
                        srow[0:1, TH:TH + T], srow[0:1, TH:TH + T], EPS, rt[0:1, 0:T],
                        ALU.add, ALU.subtract), waits=[bt[px + "rm2"]])
                def w_rm4():
                    bt[px + "rm4"] = sch.op("v", lambda c: c.reciprocal(
                        rt[0:1, 0:T], srow[0:1, TH:TH + T]), waits=[bt[px + "rm3"]])
                def w_sq():
                    bt[px + "sq"] = sch.op("s", lambda c: c.activation(
                        srow[0:1, TH:TH + T], rt[0:1, 0:T], AF.Sqrt),
                        waits=[bt[px + "rm4"]])
                def w_nm():
                    bt[px + "nm"] = sch.op("v", lambda c: c.scalar_tensor_tensor(
                        srow[0:1, 0:T], srow[0:1, 0:T], -1.0, srow[0:1, TH:TH + T],
                        ALU.mult, ALU.mult), waits=[bt[px + "sq"]])
                return [w_rm1, w_rm2, w_rm3, w_rm4, w_sq, w_nm]
            def mk_bc(px, srow, extra=()):
                def w_bc():
                    def bcast(c):
                        c.matmul(ps[6][:, 0:T], ones1[0:1, :], srow[0:1, TH:TH + T],
                                 start=True, stop=True)
                        return c.matmul(ps[6][:, 256:256 + T], ones1[0:1, :], srow[0:1, 0:T],
                                        start=True, stop=True)
                    bt[px + "bc"] = sch.op("t", bcast,
                                           waits=[bt[px + "nm"]] + [x[0] for x in extra])
                return w_bc
            def w_apym():
                def apym(c):
                    abc = ps[6][:, 0:T].unsqueeze(1).broadcast_to([128, KC, T])
                    y3 = ybf[:, :].rearrange("p (k t) -> p k t", k=KC)
                    z3 = z1[:, :].rearrange("p (k t) -> p k t", k=KC)
                    return c.tensor_tensor(z3, y3, abc, ALU.mult)
                bt["apym"] = sch.op("v", apym, waits=[bt["ybc"]])
            def w_apya():
                def apya(c):
                    bbc = ps[6][:, 256:256 + T].unsqueeze(1).broadcast_to([128, KC, T])
                    z3 = z1[:, :].rearrange("p (k t) -> p k t", k=KC)
                    return c.tensor_tensor(z3, z3, bbc, ALU.add)
                bt["apya"] = sch.op("v", apya, waits=[bt["apym"]])
            def w_gb():
                def gb(c):
                    r = None
                    for k in range(KC):
                        r = c.tensor_scalar(z1[:, k * T:(k + 1) * T], z1[:, k * T:(k + 1) * T],
                                            lngc[:, k:k + 1], lnbc[:, k:k + 1], ALU.mult, ALU.add)
                    return r
                bt["gb"] = sch.op("v", gb, waits=[bt["apya"]])
            def w_y1():
                bt["y1"] = sch.op("v", lambda c, buf=buf: c.tensor_tensor(
                    y1b[:, :], z1[:, :], gate[buf][:, :], ALU.mult),
                    waits=[bt["gb"]] + ft["evg"])
                y1_par[buf] = bt["y1"]
            def mk_wo(mp):
                def w_wo(mp=mp):
                    bank = ps[4 + (mp % 2)]
                    def mm(c, mp=mp, bank=bank):
                        r = None
                        for k in range(KC):
                            for half in range(2):
                                m = 2 * mp + half
                                r = c.matmul(bank[:, half * 256:half * 256 + T],
                                             wsb["wo"][:, k * D + m * 128:k * D + (m + 1) * 128],
                                             y1b[:, k * T:(k + 1) * T],
                                             start=(k == 0 and half == 0),
                                             stop=(k == KC - 1 and half == 1))
                        return r
                    bt[("wo", mp)] = sch.op("t", mm, waits=[bt["y1"], bankWo.get(mp % 2)])
                return w_wo
            def mk_evr(mp):
                def w_evr(mp=mp):
                    bank = ps[4 + (mp % 2)]
                    t = sch.op("v", lambda c, mp=mp, bank=bank, buf3=buf3: c.tensor_tensor(
                        xt[buf3][:, 2 * mp * T:(2 * mp + 2) * T], bank[:, 0:2 * T],
                        xt[buf3][:, 2 * mp * T:(2 * mp + 2) * T], ALU.add),
                        waits=[bt[("wo", mp)], bt.get(("evr", mp - 1))])
                    bt[("evr", mp)] = t
                    bankWo[mp % 2] = t
                return w_evr
            def w_rsq():
                bt["rsq"] = sch.op("s", lambda c, buf3=buf3: c.activation(
                    sqb[:, :], xt[buf3][:, :], AF.Square), waits=[bt[("evr", 3)]])
            def w_stats1():
                def stats_1(c, buf=buf):
                    for k in range(KC):
                        c.matmul(ps[7][0:1, 0:T], onescol[:, 0:1], xb[buf][:, k * T:(k + 1) * T],
                                 start=(k == 0), stop=False)
                    for k in range(KC):
                        c.matmul(ps[7][0:1, 0:T], mrow_o[:, k:k + 1], y1b[:, k * T:(k + 1) * T],
                                 start=False, stop=(k == KC - 1))
                    r = None
                    for k in range(KC):
                        r = c.matmul(ps[7][0:1, 256:256 + T], onescol[:, 0:1],
                                     sqb[:, k * T:(k + 1) * T],
                                     start=(k == 0), stop=(k == KC - 1))
                    return r
                bt["st1"] = sch.op("t", stats_1, waits=[bt["rsq"], t_mrow])
                st1_tok[0] = bt["st1"]
                xb_free[buf] = bt["st1"]
            def w_row1():
                def row1(c):
                    c.tensor_copy(srowmB[0:1, 0:T], ps[7][0:1, 0:T])
                    return c.tensor_copy(srowmB[0:1, TH:TH + T], ps[7][0:1, 256:256 + T])
                bt["row1"] = sch.op("v", row1, waits=[bt["st1"]])
                row1_prev[0] = bt["row1"]
            def w_a1m():
                def a1m(c, buf=buf, buf3=buf3):
                    abc = ps[6][:, 0:T].unsqueeze(1).broadcast_to([128, KC, T])
                    xt3 = xt[buf3][:, :].rearrange("p (k t) -> p k t", k=KC)
                    x13 = x1t[buf][:, :].rearrange("p (k t) -> p k t", k=KC)
                    return c.tensor_tensor(x13, xt3, abc, ALU.mult)
                bt["a1m"] = sch.op("v", a1m, waits=[bt["1bc"], x1_dma[buf]])
            def w_a1a():
                def a1a(c, buf=buf):
                    bbc = ps[6][:, 256:256 + T].unsqueeze(1).broadcast_to([128, KC, T])
                    x13 = x1t[buf][:, :].rearrange("p (k t) -> p k t", k=KC)
                    return c.tensor_tensor(x13, x13, bbc, ALU.add)
                bt["a1a"] = sch.op("v", a1a, waits=[bt["a1m"]])
                xt_free[buf3] = bt["a1a"]
                a1_prev[0] = bt["a1a"]
            def w_stx1():
                gx = "gx0" if buf == 0 else "gx1"
                def stx1(e, b=b, s0=s0, buf=buf):
                    srcap = x1t[buf][:, :].rearrange("p (k t) -> p k t", k=KC)
                    dst = x1s_h[b, :, s0:s0 + T].rearrange("(k p) t -> p k t", p=128)
                    return e.dma_start(out=dst, in_=srcap)
                x1_dma[buf] = sch.op(gx, stx1, n_dma=1, waits=[bt["a1a"]])
            yrm = mk_rm("y", "rowy", srowm, rtmp)
            orm = mk_rm("1", "row1", srowmB, rtmpB)
            bp = dict(ty=w_ty, ysq=w_ysq, statsy=w_statsy, rowy=w_rowy,
                      yrm1=yrm[0], yrm2=yrm[1], yrm3=yrm[2], yrm4=yrm[3],
                      ysqrt=yrm[4], ynm=yrm[5],
                      ybc=mk_bc("y", srowm, extra=[a1_prev]),
                      apym=w_apym, apya=w_apya, gb=w_gb, y1=w_y1,
                      wo0=mk_wo(0), evr0=mk_evr(0), wo1=mk_wo(1), evr1=mk_evr(1),
                      wo2=mk_wo(2), evr2=mk_evr(2), wo3=mk_wo(3), evr3=mk_evr(3))
            tail = ([w_rsq, w_stats1, w_row1] + orm +
                    [mk_bc("1", srowmB), w_a1m, w_a1a, w_stx1])
            return bp, tail

        pp1l = []
        if "B" in phases:
            t_z1 = sch.op("v", lambda c: c.tensor_scalar(
                wsum1[:, 0:2], lamcol[:, 0:2], 0.0, None, ALU.mult), waits=[t_p0])
            halves = []
            for ch in range(KC):
                for h in range(4):
                    halves.append((w1t_h, ch, h, 0))
            for ch in range(CC):
                halves.append((w2t_h, ch, 0, 1))
            pp_ld = [None, None]
            pp_abs = [None]
            pp_add = [t_z1]
            def mk_pp(i):
                hh, ch, h, col_i = halves[i]
                pb = i % 2
                wid = 1024
                def e_ld():
                    def ldw(e, hh=hh, ch=ch, h=h, pb=pb, wid=wid):
                        return e.dma_start(out=wtmp1[pb][:, 0:wid],
                                           in_=hh[ch * 128:(ch + 1) * 128, h * wid:(h + 1) * wid])
                    tok = sch.op(f"pw{pb}", ldw, n_dma=1, waits=[pp_ld[pb]])
                    pp_ld[pb] = tok
                    mk_pp.lt[i] = tok
                def e_abs():
                    t = sch.op("s", lambda c, pb=pb, wid=wid: c.activation(
                        wtmp1[pb][:, 0:wid], wtmp1[pb][:, 0:wid], AF.Abs,
                        accum_out=wacc1[:, pb:pb + 1]),
                        waits=[mk_pp.lt[i], pp_abs[0], pp_add[0]])
                    pp_abs[0] = t
                    pp_ld[pb] = t
                    mk_pp.at[i] = t
                def e_add():
                    t = sch.op("v", lambda c, col_i=col_i, pb=pb: c.tensor_tensor(
                        wsum1[:, col_i:col_i + 1], wsum1[:, col_i:col_i + 1],
                        wacc1[:, pb:pb + 1], ALU.add), waits=[mk_pp.at[i], pp_add[0]])
                    pp_add[0] = t
                return [e_ld, e_abs, e_add]
            mk_pp.lt = {}
            mk_pp.at = {}
            for i in range(len(halves)):
                pp1l.extend(mk_pp(i))
        if "A" in phases:
            a_ld(0)
            for f in front_emitters(0):
                f()
            for ti in range(ntiles_a):
                a_ld(ti + 1)
                bp, tail = back_emitters(ti)
                fq = front_emitters(ti + 1)
                tl = pend_tail[0]
                def tpull(n=1, tl=tl):
                    for _ in range(n):
                        if tl:
                            tl.pop(0)()
                def fpull(n=1, fq=fq):
                    for _ in range(n):
                        if fq:
                            fq.pop(0)()
                tpull(2)
                bp["ty"](); fpull()
                tpull()
                bp["ysq"](); fpull()
                tpull()
                bp["statsy"](); fpull()
                tpull()
                bp["rowy"](); fpull()
                tpull(); bp["yrm1"](); fpull()
                tpull(); bp["yrm2"](); fpull()
                tpull(); bp["yrm3"](); fpull()
                tpull(); bp["yrm4"](); fpull()
                tpull(); bp["ysqrt"](); fpull()
                tpull(); bp["ynm"](); fpull()
                tpull()
                bp["ybc"](); fpull()
                tpull(99)
                bp["apym"](); fpull(2)
                bp["apya"](); fpull(2)
                bp["gb"](); fpull(2)
                bp["y1"](); fpull(2)
                bp["wo0"](); fpull(2); bp["evr0"](); fpull(2)
                bp["wo1"](); fpull(2); bp["evr1"](); fpull(2)
                bp["wo2"](); fpull(2); bp["evr2"](); fpull(2)
                bp["wo3"](); fpull(99); bp["evr3"]()
                for _ in range(10):
                    if pp1l:
                        pp1l.pop(0)()
                pend_tail[0] = tail
            for f in pend_tail[0]:
                f()
        while pp1l:
            pp1l.pop(0)()
        pa_done = [x1_dma[0], x1_dma[1], prev_scan[0], xt_free[0], xt_free[1]]

        scopeA.close()

        kw1 = sb("kw1", [128, KC * DH], F8)
        kw2 = sb("kw2", [128, CC * D], F8)
        t_kw = t_mrow2 = None
        if "B" in phases:
            scopeP = contextlib.ExitStack()
            wtmp = [sb(f"wtmp{i}", [128, DH], F32, scopeP) for i in range(4)]
            wi8 = [sb(f"wi8{i}", [128, DH], I8, scopeP) for i in range(2)]

            def total_mm(c):
                c.matmul(ps[6][0:1, 0:1], wsum1[:, 0:1], onescolf[:, 0:1], start=True, stop=True)
                return c.matmul(ps[6][0:1, 1:2], wsum1[:, 1:2], onescolf[:, 0:1], start=True, stop=True)
            t_tot = sch.op("t", total_mm, waits=[pp_add[0], t_p0] + pa_done)

            def scalemath(c):
                c.tensor_copy(tinyrow[0:1, 0:2], ps[6][0:1, 0:2])
                c.barrier()
                c.tensor_scalar(tinyrow[0:1, 0:2], tinyrow[0:1, 0:2], 1.0 / (DH * D), None, ALU.mult)
                c.barrier()
                c.tensor_scalar(tinyrow[0:1, 0:2], tinyrow[0:1, 0:2], 1e-5, None, ALU.max)
                c.barrier()
                return c.reciprocal(tinyrow[0:1, 2:4], tinyrow[0:1, 0:2])
            t_sm = sch.op("v", scalemath, barriers=3, waits=[t_tot])

            def bc_scales(c):
                c.matmul(ps[6][:, 4:5], ones1[0:1, :], tinyrow[0:1, 2:3], start=True, stop=True)
                c.matmul(ps[6][:, 5:6], ones1[0:1, :], tinyrow[0:1, 3:4], start=True, stop=True)
                c.matmul(ps[6][:, 6:7], ones1[0:1, :], tinyrow[0:1, 1:2], start=True, stop=True)
                return c.matmul(ps[6][:, 7:8], ones1[0:1, :], tinyrow[0:1, 0:1], start=True, stop=True)
            t_bs = sch.op("t", bc_scales, waits=[t_sm])
            t_swc = sch.op("v", lambda c: c.tensor_copy(swcol[:, 0:4], ps[6][:, 4:8]), waits=[t_bs])
            t_cw = sch.op("v", lambda c: c.tensor_scalar(cwc[:, :], cwc[:, :],
                                                         swcol[:, 3:4], None, ALU.mult),
                          waits=[t_swc, t_p0])

            chunks2 = [("w1", w1t_h, ch, DH, 0, kw1) for ch in range(KC)] + \
                      [("w2", w2t_h, ch, D, 1, kw2) for ch in range(CC)]
            ld2 = [None, None, None, None]
            qa = {}
            qb = {}
            qc = {}
            qd = {}
            def eld(i):
                if i >= len(chunks2):
                    return
                nm, hh, ch, chw, scol, dst = chunks2[i]
                pb = i % 4
                pw = f"pw{pb}"
                def ldw(e, hh=hh, ch=ch, chw=chw, pb=pb):
                    return e.dma_start(out=wtmp[pb][:, 0:chw], in_=hh[ch * 128:(ch + 1) * 128, :])
                qa[("l", i)] = sch.op(pw, ldw, n_dma=1,
                                      waits=[ld2[pb], t_swc, t_cw] +
                                            (pa_done if i < 4 else []))
            def eq1(i):
                nm, hh, ch, chw, scol, dst = chunks2[i]
                pb = i % 4
                qa[i] = sch.op("v", lambda c, pb=pb, chw=chw, scol=scol: c.tensor_scalar(
                    wtmp[pb][:, 0:chw], wtmp[pb][:, 0:chw],
                    swcol[:, scol:scol + 1], 1.0, ALU.mult, ALU.min),
                    waits=[qa[("l", i)], qd.get(i - 4)])
            def eq2(i):
                nm, hh, ch, chw, scol, dst = chunks2[i]
                pb = i % 4
                qb[i] = sch.op("v", lambda c, pb=pb, chw=chw: c.tensor_scalar(
                    wtmp[pb][:, 0:chw], wtmp[pb][:, 0:chw], -1.0, None, ALU.max),
                    waits=[qa[i]])
            def eq3(i):
                nm, hh, ch, chw, scol, dst = chunks2[i]
                pb = i % 4
                wb = i % 2
                qc[i] = sch.op("v", lambda c, pb=pb, chw=chw, wb=wb: c.tensor_scalar(
                    wi8[wb][:, 0:chw], wtmp[pb][:, 0:chw], MAGIC, -MAGIC, ALU.add, ALU.add),
                    waits=[qb[i], qd.get(i - 2)])
            def eq4(i):
                nm, hh, ch, chw, scol, dst = chunks2[i]
                wb = i % 2
                qd[i] = sch.op("v", lambda c, chw=chw, wb=wb, dst=dst, ch=ch: c.tensor_copy(
                    dst[:, ch * chw:(ch + 1) * chw], wi8[wb][:, 0:chw]), waits=[qc[i]])
                ld2[i % 4] = qd[i]
            for i in range(4):
                eld(i)
            eq1(0)
            eq2(0)
            eq1(1)
            for i in range(len(chunks2)):
                eq3(i)
                if i + 2 < len(chunks2):
                    eq1(i + 2)
                eq4(i)
                eld(i + 4)
                if i + 1 < len(chunks2):
                    eq2(i + 1)
            t_kw = qd[len(chunks2) - 1]

            def mr2(c):
                for ch in range(CC):
                    c.reduce_sum(mrow_f[:, ch:ch + 1], kw2[:, ch * D:(ch + 1) * D], axis=AX.X)
                c.barrier()
                return c.tensor_copy(mrow2[:, :], mrow_f[:, 0:CC])
            t_mrow2 = sch.op("v", mr2, barriers=1, waits=[t_kw])
            scopeP.close()

        if "B" in phases:
            x1ld = [sb(f"x1ld{i}", [128, KC * TH], F32) for i in range(3)]
            xq = sb("xq", [128, KC * TH], BF16)
            convA = [sb(f"convA{i}", [128, T], F32) for i in range(2)]
            convB1 = sb("convB1", [128, T], F32)
            hsil = sb("hsil", [128, CC * T], F32)
            k2bf = [sb(f"k2bf{i}", [128, 2 * T], BF16) for i in range(2)]
            k2i8 = [sb(f"k2i8{i}", [128, 2 * T], I8) for i in range(2)]
            res2 = sb("res2", [128, KC * T], F32)
            habs1 = sb("habs1", [128, 2 * T], F32)
            outb = sb("outb", [128, KC * T], F32)
            q1m = sb("q1m", [128, 3 * TH], F32)
            zq1 = sb("zq1", [128, 4 * TH], F32)
            zqi1 = sb("zqi1", [128, 4 * TH], I8)
            srowq1 = sb("srowq1", [1, 2 * TH], F32)
            e2bc2 = sb("e2bc2", [128, T], F32)
            MB0 = 2 * TH
            HB0 = 2 * TH + 2 * TH
            CB0 = HB0 + 2 * T
            mbuf = res2
            convB = res2

            x1_free = [None, None, None]
            out_prev = [None]
            a2_prev = [None]
            fin_prev = [None]
            m1q_free = [None]
            stok1_free = [None]
            stok2_free = [None]
            srowq_free = [None]
            xqfree = [None, None]
            ln2row_tok = [None]
            res2q_free = [None, None]
            evm_prev = [None, None]
            xq_free_warm = [None]
            hsilfree = [None, None]
            mf1_tok = {}
            qx_tok = {}
            trq1_tok = {}
            r1_tok = {}
            bsq_tok = {}
            bcq1_tok = {}
            stageB = {}

            class Weaver:
                def __init__(self):
                    self.q = []
                def add(self, fn):
                    self.q.append(fn)
                def pull(self, n=1):
                    for _ in range(n):
                        if self.q:
                            self.q.pop(0)()
                def drain(self):
                    while self.q:
                        self.q.pop(0)()

            def b_load(ti):
                if ti >= ntiles_b:
                    return
                b, si = divmod(ti, NSEQT)
                s0 = si * T
                bufl = ti % 3
                qr = f"qr{bufl}"
                lo = max(s0 - 1, 0)
                hi = min(s0 - 1 + TH, S)
                doff = lo - (s0 - 1)
                ncols = hi - lo
                def ld(e, b=b, lo=lo, hi=hi, doff=doff, ncols=ncols, bufl=bufl):
                    srcap = x1s_h[b, :, lo:hi].rearrange("(k p) t -> p k t", p=128)
                    dst = x1ld[bufl][:, :].rearrange("p (k t) -> p k t", k=KC)[:, :, doff:doff + ncols]
                    return e.dma_start(out=dst, in_=srcap)
                wx = [x1_free[bufl]] + ([x1_dma[0], x1_dma[1], t_p0, t_kw, t_mrow2] if ti <= 2 else [])
                t_x1 = sch.op(qr, ld, n_dma=1, waits=wx)
                if si == 0 or si == NSEQT - 1:
                    zc = 0 if si == 0 else TH - 1
                    def zhalo(c, zc=zc, bufl=bufl):
                        r = None
                        for k in range(KC):
                            r = c.tensor_copy(x1ld[bufl][:, k * TH + zc:k * TH + zc + 1], zsb[:, 0:1])
                        return r
                    t_x1 = sch.op("v", zhalo, waits=[t_x1])
                stageB[("ld", ti)] = t_x1

            def add_fold_emitters(wv, ti):
                xl = x1ld[ti % 3]
                m1v = q1m[:, 0:TH]
                tmpX = [q1m[:, TH:2 * TH], q1m[:, 2 * TH:3 * TH]]
                st = {}
                def e_a0():
                    st["x1"] = stageB.pop(("ld", ti))
                    base = [st["x1"], m1q_free[0], mf1_tok.get(ti - 1)]
                    st[("f", 0)] = sch.op("v", lambda c: c.scalar_tensor_tensor(
                        m1v, xl[:, 0:TH], -1.0, xl[:, 0:TH], ALU.mult, ALU.max), waits=base)
                wv.add(e_a0)
                def mk_abs(k, wkey):
                    def e_abs(k=k, wkey=wkey):
                        tm = tmpX[k % 2]
                        st[("a", k)] = sch.op("v", lambda c, tm=tm, k=k: c.scalar_tensor_tensor(
                            tm, xl[:, k * TH:(k + 1) * TH], -1.0, xl[:, k * TH:(k + 1) * TH],
                            ALU.mult, ALU.max),
                            waits=[st["x1"], st.get(wkey),
                                   mf1_tok.get(ti - 1) if k <= 2 else None])
                    return e_abs
                def mk_fold(k):
                    def e_fold(k=k):
                        tm = tmpX[k % 2]
                        st[("f", k)] = sch.op("v", lambda c, tm=tm: c.tensor_tensor(
                            m1v, m1v, tm, ALU.max), waits=[st[("a", k)], st[("f", k - 1)]])
                        if k == KC - 1:
                            mf1_tok[ti] = st[("f", k)]
                    return e_fold
                wv.add(mk_abs(1, None))
                wv.add(mk_abs(2, None))
                for k in range(1, KC):
                    wv.add(mk_fold(k))
                    if k + 2 < KC:
                        wv.add(mk_abs(k + 2, ("f", k)))

            def emit_trq1(ti):
                def trq(c):
                    c.transpose(ps[0][:, 0:128], q1m[:, 0:128], ident[:, :])
                    c.transpose(ps[0][:, 128:256], q1m[:, 128:256], ident[:, :])
                    return c.transpose(ps[0][:, 256:384], q1m[:, 130:258], ident[:, :])
                trq1_tok[ti] = sch.op("t", trq, waits=[mf1_tok[ti], fin_prev[0]])
                m1q_free[0] = trq1_tok[ti]

            def r1_emitters(ti):
                st = {}
                def r1a():
                    def f(c):
                        r = None
                        for j in range(3):
                            r = c.reduce_max(stok[:, j:j + 1], ps[0][:, j * 128:(j + 1) * 128],
                                             axis=AX.X)
                        return r
                    st["a"] = sch.op("v", f, waits=[trq1_tok[ti], stok1_free[0]])
                def r1b():
                    st["b"] = sch.op("v", lambda c: c.tensor_scalar(
                        stok[:, 0:3], stok[:, 0:3], 1e-5, None, ALU.max), waits=[st["a"]])
                def r1c():
                    st["c"] = sch.op("v", lambda c: c.reciprocal(stok[:, 4:7], stok[:, 0:3]),
                                     waits=[st["b"]])
                def r1d():
                    st["d"] = sch.op("v", lambda c: c.tensor_scalar(
                        stok[:, 4:7], stok[:, 4:7], 127.0, None, ALU.mult), waits=[st["c"]])
                def r1e():
                    st["e"] = sch.op("v", lambda c: c.tensor_scalar(
                        stok[:, 8:11], stok[:, 0:3], 1.0 / 127.0, None, ALU.mult),
                        waits=[st["b"]])
                    r1_tok[ti] = (st["d"], st["e"])
                return [r1a, r1b, r1c, r1d, r1e]

            def emit_btq1(ti):
                def btq(c):
                    for j in range(3):
                        c.transpose(ps[1][0:1, j * 128:(j + 1) * 128], stok[:, 4 + j:5 + j],
                                    ident[:, :])
                    c.transpose(ps[1][0:1, 384:512], stok[:, 8:9], ident[:, :])
                    c.transpose(ps[0][0:1, 384:512], stok[:, 9:10], ident[:, :])
                    return c.transpose(ps[0][0:1, 0:128], stok[:, 10:11], ident[:, :])
                d, e = r1_tok[ti]
                t = sch.op("t", btq, waits=[d, e, fin_prev[0]])
                stok1_free[0] = t
                bsq_tok[ti] = t

            def emit_rowsq1(ti):
                def rowsq(c):
                    c.tensor_copy(srowq1[0:1, 0:256], ps[1][0:1, 0:256])
                    c.tensor_copy(srowq1[0:1, 256:258], ps[1][0:1, 382:384])
                    c.tensor_copy(srowq1[0:1, TH:TH + 128], ps[1][0:1, 384:512])
                    c.tensor_copy(srowq1[0:1, TH + 128:TH + 256], ps[0][0:1, 384:512])
                    return c.tensor_copy(srowq1[0:1, TH + 256:TH + 258], ps[0][0:1, 126:128])
                bsq_tok[ti] = sch.op("v", rowsq, waits=[bsq_tok[ti], srowq_free[0]])

            def emit_bcq1(ti):
                def bcq(c):
                    c.matmul(ps[0][:, 0:TH], ones1[0:1, :], srowq1[0:1, 0:TH],
                             start=True, stop=True)
                    return c.matmul(ps[1][:, 0:TH], ones1[0:1, :], srowq1[0:1, TH:2 * TH],
                                    start=True, stop=True)
                t = sch.op("t", bcq, waits=[bsq_tok[ti]])
                srowq_free[0] = t
                bcq1_tok[ti] = t

            def qx_emitters(ti):
                bufl = ti % 3
                s1bb = ps[0][:, 0:TH].unsqueeze(1).broadcast_to([128, 2, TH])
                s1ib = ps[1][:, 0:TH].unsqueeze(1).broadcast_to([128, 2, TH])
                zqQ = lambda j: zq1[:, j * 2 * TH:(j + 1) * 2 * TH]
                ziQ = lambda j: zqi1[:, j * 2 * TH:(j + 1) * 2 * TH]
                st = {}
                out = []
                for kp in range(KC // 2):
                    jj = kp % 2
                    def e_qm(kp=kp, jj=jj):
                        def f(c, kp=kp, jj=jj, bufl=bufl):
                            x3 = x1ld[bufl][:, 2 * kp * TH:(2 * kp + 2) * TH].rearrange(
                                "p (k t) -> p k t", k=2)
                            z3 = zqQ(jj)[:, :].rearrange("p (k t) -> p k t", k=2)
                            return c.tensor_tensor(z3, x3, s1bb, ALU.mult)
                        st[("m", kp)] = sch.op("v", f, waits=[bcq1_tok[ti], st.get(("i", kp - 2))] +
                                               ([xqfree[0], xqfree[1]] if kp <= 1 else []))
                    def e_qi(kp=kp, jj=jj):
                        def f(c, jj=jj):
                            return c.tensor_scalar(ziQ(jj), zqQ(jj), MAGIC, -MAGIC,
                                                   ALU.add, ALU.add)
                        st[("i", kp)] = sch.op("v", f, waits=[st[("m", kp)], st.get(("q", kp - 2))])
                    def e_qq(kp=kp, jj=jj):
                        def f(c, kp=kp, jj=jj):
                            zi3 = ziQ(jj).rearrange("p (k t) -> p k t", k=2)
                            xq3 = xq[:, 2 * kp * TH:(2 * kp + 2) * TH].rearrange(
                                "p (k t) -> p k t", k=2)
                            return c.tensor_tensor(xq3, zi3, s1ib, ALU.mult)
                        st[("q", kp)] = sch.op("v", f, waits=[st[("i", kp)]])
                        if kp == KC // 2 - 1:
                            qx_tok[ti] = st[("q", kp)]
                    out += [e_qm, e_qi, e_qq]
                return out

            def b_s2(ti, weave=()):
                buf = ti % 2
                t_qx = qx_tok[ti]
                silu_toks = []
                sil_prev = {}
                cB_prev = {}
                conv_done = {}
                wq = list(weave)
                def wpull(n=1):
                    for _ in range(n):
                        if wq:
                            wq.pop(0)()
                cAap = lambda pb: convA[pb][:, 0:T]
                cBap = lambda pb: convB[:, CB0:CB0 + T] if pb == 0 else convB1[:, 0:T]
                habs_free = {}
                def habs_ap(gb):
                    return mbuf[:, HB0:HB0 + 2 * T] if gb == 0 else habs1[:, :]
                mAB = [lambda: mbuf[:, MB0:MB0 + 2 * T], lambda: res2[:, 0:2 * T]]
                acc_t = {0: None, 1: None}
                def habsfold(pr):
                    gb = pr % 2
                    t_ab = sch.op("s", lambda c, pr=pr, gb=gb: c.activation(
                        habs_ap(gb), hsil[:, 2 * pr * T:(2 * pr + 2) * T], AF.Abs),
                        waits=[silu_toks[2 * pr], silu_toks[2 * pr + 1], habs_free.get(pr - 2)])
                    lane = pr % 2
                    if acc_t[lane] is None:
                        t_fo = sch.op("v", lambda c, gb=gb, lane=lane: c.tensor_copy(
                            mAB[lane](), habs_ap(gb)),
                            waits=[t_ab, res2q_free[1] if lane == 0 else res2q_free[0]])
                    else:
                        t_fo = sch.op("v", lambda c, gb=gb, lane=lane: c.tensor_tensor(
                            mAB[lane](), mAB[lane](), habs_ap(gb), ALU.max),
                            waits=[t_ab, acc_t[lane]])
                    acc_t[lane] = t_fo
                    habs_free[pr] = t_fo
                for pr in range(CC // 2):
                    mmt = {}
                    for half in range(2):
                        ch = 2 * pr + half
                        bank = ps[ch % 4]
                        def mmw1(c, ch=ch, bank=bank):
                            r = None
                            for k in range(KC):
                                r = c.matmul(bank[:, 0:TH],
                                             kw1[:, k * DH + ch * 128:k * DH + (ch + 1) * 128],
                                             xq[:, k * TH:(k + 1) * TH],
                                             start=(k == 0), stop=(k == KC - 1))
                            return r
                        mmt[ch] = sch.op("t", mmw1, waits=[t_qx, conv_done.get(ch - 4)] +
                                         ([evm_prev[ch - 2]] if ch in (2, 3) else []))
                    cAt = {}
                    for half in range(2):
                        ch = 2 * pr + half
                        bank = ps[ch % 4]
                        pb = ch % 2
                        cAt[ch] = sch.op("s", lambda c, ch=ch, bank=bank, pb=pb: c.activation(
                            cAap(pb), bank[:, 2:2 + T], AF.Identity,
                            bias=cbc[:, ch:ch + 1], scale=cwc[:, ch * 3 + 2:ch * 3 + 3]),
                            waits=[mmt[ch], cB_prev.get(ch - 2)])
                    s1t = {}
                    for half in range(2):
                        ch = 2 * pr + half
                        bank = ps[ch % 4]
                        pb = ch % 2
                        s1t[ch] = sch.op("v", lambda c, ch=ch, bank=bank, pb=pb:
                            c.scalar_tensor_tensor(cBap(pb), bank[:, 0:T],
                                                   cwc[:, ch * 3 + 0:ch * 3 + 1], cAap(pb),
                                                   ALU.mult, ALU.add),
                            waits=[cAt[ch], sil_prev.get(ch - 2)])
                    s2t = {}
                    for half in range(2):
                        ch = 2 * pr + half
                        bank = ps[ch % 4]
                        pb = ch % 2
                        s2t[ch] = sch.op("v", lambda c, ch=ch, bank=bank, pb=pb:
                            c.scalar_tensor_tensor(cBap(pb), bank[:, 1:1 + T],
                                                   cwc[:, ch * 3 + 1:ch * 3 + 2], cBap(pb),
                                                   ALU.mult, ALU.add),
                            waits=[s1t[ch]])
                        conv_done[ch] = s2t[ch]
                        cB_prev[ch] = s2t[ch]
                    for half in range(2):
                        ch = 2 * pr + half
                        pb = ch % 2
                        t_sil = sch.op("s", lambda c, ch=ch, pb=pb: c.activation(
                            hsil[:, ch * T:(ch + 1) * T], cBap(pb), AF.Silu),
                            waits=[s2t[ch]] +
                                  ([hsilfree[0], hsilfree[1]] if ch <= 1 else []))
                        sil_prev[ch] = t_sil
                        silu_toks.append(t_sil)
                    wpull(1)
                    if pr >= 1:
                        habsfold(pr - 1)
                wpull(99)
                habsfold(CC // 2 - 1)
                xqfree[0] = conv_done[CC - 2]
                xqfree[1] = conv_done[CC - 1]
                xq_free_warm[0] = mmt[CC - 1]
                t_j = sch.op("v", lambda c: c.tensor_tensor(
                    mAB[0](), mAB[0](), mAB[1](), ALU.max), waits=[acc_t[0], acc_t[1]])
                t_fin = sch.op("v", lambda c: c.tensor_tensor(
                    mbuf[:, MB0:MB0 + T], mbuf[:, MB0:MB0 + T], mbuf[:, MB0 + T:MB0 + 2 * T],
                    ALU.max), waits=[t_j])
                stageB[("mf2", ti)] = t_fin
                fin_prev[0] = t_fin

            def b_s3(ti, wv):
                t_fin = stageB.pop(("mf2", ti))
                def warm(c):
                    r = None
                    for _ in range(26):
                        r = c.matmul(ps[2][:, 0:256], kw1[:, 0:128], xq[:, 0:256],
                                     start=True, stop=True)
                    return r
                sch.op("t", warm, waits=[a2_prev[0], xq_free_warm[0], xqfree[0], xqfree[1]])
                def trq2(c):
                    c.transpose(ps[6][:, 0:128], mbuf[:, MB0:MB0 + 128], ident[:, :])
                    return c.transpose(ps[6][:, 128:256], mbuf[:, MB0 + 128:MB0 + 256], ident[:, :])
                t_tq2 = sch.op("t", trq2, waits=[t_fin, hsilfree[0], hsilfree[1]])
                wv.pull()
                def r2a(c):
                    r = None
                    for j in range(NB):
                        r = c.reduce_max(stok[:, 12 + j:13 + j], ps[6][:, j * 128:(j + 1) * 128],
                                         axis=AX.X)
                    return r
                t_a = sch.op("v", r2a, waits=[t_tq2, stok2_free[0]])
                wv.pull()
                t_b = sch.op("v", lambda c: c.tensor_scalar(
                    stok[:, 12:12 + NB], stok[:, 12:12 + NB], 1e-5, None, ALU.max), waits=[t_a])
                wv.pull()
                t_c = sch.op("v", lambda c: c.reciprocal(stok[:, 16:16 + NB], stok[:, 12:12 + NB]),
                             waits=[t_b])
                wv.pull()
                t_d = sch.op("v", lambda c: c.tensor_scalar(
                    stok[:, 16:16 + NB], stok[:, 16:16 + NB], 127.0, None, ALU.mult), waits=[t_c])
                wv.pull()
                t_e = sch.op("v", lambda c: c.tensor_scalar(
                    stok[:, 20:20 + NB], stok[:, 12:12 + NB], 1.0 / 127.0, swcol[:, 2:3],
                    ALU.mult, ALU.mult), waits=[t_b])
                wv.pull()
                def btq2(c):
                    for j in range(NB):
                        c.transpose(ps[6][0:1, 256 + j * 128:256 + (j + 1) * 128],
                                    stok[:, 16 + j:17 + j], ident[:, :])
                    r = None
                    for j in range(NB):
                        r = c.transpose(ps[7][0:1, j * 128:(j + 1) * 128],
                                        stok[:, 20 + j:21 + j], ident[:, :])
                    return r
                t_bq2 = sch.op("t", btq2, waits=[t_d, t_e, a2_prev[0]])
                stok2_free[0] = t_bq2
                wv.pull()
                def rowsq2(c):
                    c.tensor_copy(srowq1[0:1, 0:T], ps[6][0:1, 256:256 + T])
                    return c.tensor_copy(srowq1[0:1, TH:TH + T], ps[7][0:1, 0:T])
                t_rowq2 = sch.op("v", rowsq2, waits=[t_bq2, srowq_free[0]])
                wv.pull()
                def bcq2(c):
                    c.matmul(ps[6][:, 0:T], ones1[0:1, :], srowq1[0:1, 0:T], start=True, stop=True)
                    return c.matmul(ps[6][:, 256:256 + T], ones1[0:1, :], srowq1[0:1, TH:TH + T],
                                    start=True, stop=True)
                t_bcq2 = sch.op("t", bcq2, waits=[t_rowq2])
                srowq_free[0] = t_bcq2
                wv.pull()
                t_e2c = sch.op("v", lambda c: c.tensor_copy(e2bc2[:, :], ps[6][:, 256:256 + T]),
                               waits=[t_bcq2, ln2row_tok[0]])
                stageB[("cpbc", ti)] = t_e2c
                stageB[("s2bc", ti)] = t_bcq2

            def b_s4(ti, wv, m1l, qxl, nxt):
                buf = ti % 2
                bufl = ti % 3
                b, si = divmod(ti, NSEQT)
                s0 = si * T
                t_cpbc = stageB.pop(("cpbc", ti))
                t_s2bc = stageB.pop(("s2bc", ti))
                k2_used = {}
                kc_tok = {}
                w2_prev = [None]
                zkm_tok = {}
                zki_tok = {}
                s2b = ps[6][:, 0:T].unsqueeze(1).broadcast_to([128, 2, T])
                zqA = lambda j: res2[:, 0:2 * T] if j == 0 else res2[:, MB0:MB0 + 2 * T]
                def filler(pr):
                    if wv.q:
                        wv.pull()
                    elif pr >= 5 and m1l:
                        m1l.pop(0)()
                for pr in range(CC // 2):
                    kb = pr % 2
                    def zkm(c, pr=pr, kb=kb):
                        h3 = hsil[:, 2 * pr * T:(2 * pr + 2) * T].rearrange("p (k t) -> p k t", k=2)
                        z3 = zqA(kb)[:, :].rearrange("p (k t) -> p k t", k=2)
                        return c.tensor_tensor(z3, h3, s2b, ALU.mult)
                    t_m = sch.op("v", zkm, waits=[t_s2bc, zki_tok.get(pr - 2)] +
                                 ([a2_prev[0], ln2row_tok[0]] if pr <= 1 else []))
                    zkm_tok[pr] = t_m
                    filler(pr)
                    def zki(c, pr=pr, kb=kb):
                        return c.tensor_scalar(k2i8[kb][:, 0:2 * T], zqA(kb)[:, 0:2 * T],
                                               MAGIC, -MAGIC, ALU.add, ALU.add)
                    t_i = sch.op("v", zki, waits=[t_m, kc_tok.get(pr - 2)])
                    zki_tok[pr] = t_i
                    filler(pr)
                    t_kc = sch.op("s", lambda c, kb=kb: c.activation(
                        k2bf[kb][:, :], k2i8[kb][:, 0:2 * T], AF.Identity),
                        waits=[t_i, k2_used.get(pr - 2)])
                    kc_tok[pr] = t_kc
                    def mmw2(c, pr=pr, kb=kb):
                        for half in range(2):
                            ch = 2 * pr + half
                            for m in range(KC):
                                c.matmul(ps[2 + m // 2][:, (m % 2) * 256:(m % 2) * 256 + T],
                                         kw2[:, ch * D + m * 128:ch * D + (m + 1) * 128],
                                         k2bf[kb][:, half * T:(half + 1) * T],
                                         start=(ch == 0 and m % 2 == 0),
                                         stop=(ch == CC - 1 and m % 2 == 1))
                        r = None
                        for half in range(2):
                            ch = 2 * pr + half
                            r = c.matmul(ps[7][0:1, 256:256 + T], mrow2[:, ch:ch + 1],
                                         k2bf[kb][:, half * T:(half + 1) * T],
                                         start=(ch == 0), stop=(ch == CC - 1))
                        return r
                    t_m2 = sch.op("t", mmw2, waits=[t_kc, w2_prev[0]])
                    w2_prev[0] = t_m2
                    k2_used[pr] = t_m2
                hsilfree[0] = zkm_tok[CC // 2 - 2]
                hsilfree[1] = zkm_tok[CC // 2 - 1]
                res2q_free[0] = zki_tok[CC // 2 - 2]
                res2q_free[1] = zki_tok[CC // 2 - 1]
                wv.drain()
                while m1l:
                    m1l.pop(0)()
                while qxl:
                    qxl.pop(0)()
                if nxt is not None:
                    def warm2(c):
                        r = None
                        for _ in range(26):
                            r = c.matmul(ps[0][:, 384:512], ident[:, :], ident[:, :],
                                         start=True, stop=True)
                        return r
                    sch.op("t", warm2, waits=[w2_prev[0], bsq_tok.get(nxt)])

                ev_m = {}
                ev_a = {}
                def mk_evm(mp):
                    def f(c, mp=mp):
                        pr3 = ps[2 + mp][:, 0:2 * T].rearrange("p (k t) -> p k t", k=2)
                        r3 = outb[:, 2 * mp * T:(2 * mp + 2) * T].rearrange("p (k t) -> p k t", k=2)
                        e3 = e2bc2[:, :].unsqueeze(1).broadcast_to([128, 2, T])
                        return c.tensor_tensor(r3, pr3, e3, ALU.mult)
                    return f
                def mk_eva(mp):
                    def f(c, mp=mp, bufl=bufl):
                        r3 = outb[:, 2 * mp * T:(2 * mp + 2) * T].rearrange("p (k t) -> p k t", k=2)
                        x3 = x1ld[bufl][:, :].rearrange("p (k t) -> p k t", k=KC)[:, 2 * mp:2 * mp + 2, 1:1 + T]
                        return c.tensor_tensor(r3, r3, x3, ALU.add)
                    return f
                ev_m[0] = sch.op("v", mk_evm(0), waits=[w2_prev[0], t_cpbc, out_prev[0]])
                ev_m[1] = sch.op("v", mk_evm(1), waits=[w2_prev[0], t_cpbc, out_prev[0]])
                ev_a[0] = sch.op("v", mk_eva(0), waits=[ev_m[0]])
                ev_m[2] = sch.op("v", mk_evm(2), waits=[w2_prev[0], t_cpbc, out_prev[0]])
                ev_a[1] = sch.op("v", mk_eva(1), waits=[ev_m[1], ev_a[0]])
                ev_m[3] = sch.op("v", mk_evm(3), waits=[w2_prev[0], t_cpbc, out_prev[0]])
                ev_a[2] = sch.op("v", mk_eva(2), waits=[ev_m[2], ev_a[1]])
                ev_a[3] = sch.op("v", mk_eva(3), waits=[ev_m[3], ev_a[2]])
                evm_prev[0] = ev_m[0]
                evm_prev[1] = ev_m[1]
                x1_free[bufl] = ev_a[3]
                t_rsq2 = sch.op("s", lambda c: c.activation(sqb[:, :], outb[:, :], AF.Square),
                                waits=[ev_a[3]])

                tl = {}
                def w_st2():
                    def st2(c):
                        r = None
                        for k in range(KC):
                            r = c.matmul(ps[7][0:1, 0:T], onescol[:, 0:1],
                                         sqb[:, k * T:(k + 1) * T],
                                         start=(k == 0), stop=(k == KC - 1))
                        return r
                    tl["st2"] = sch.op("t", st2, waits=[t_rsq2])
                def w_rw1():
                    def rw1(c):
                        c.tensor_copy(srowm[0:1, 0:T], ps[7][0:1, 256:256 + T])
                        return c.tensor_copy(srowm[0:1, TH:TH + T], ps[7][0:1, 0:T])
                    tl["rw1"] = sch.op("v", rw1, waits=[tl["st2"]])
                def w_rw2():
                    tl["rw2"] = sch.op("v", lambda c: c.tensor_tensor(
                        srowm[0:1, 0:T], srowm[0:1, 0:T], e2bc2[0:1, 0:T], ALU.mult),
                        waits=[tl["rw1"]])
                    ln2row_tok[0] = tl["rw2"]
                def w_rm1():
                    def rm1(c):
                        c.tensor_scalar(srowm[0:1, 0:T], srowm[0:1, 0:T], INV_D, None, ALU.mult)
                        return c.tensor_scalar(srowm[0:1, TH:TH + T], srowm[0:1, TH:TH + T],
                                               INV_D, None, ALU.mult)
                    tl["rm1"] = sch.op("v", rm1, waits=[tl["rw2"]])
                def w_rm2():
                    tl["rm2"] = sch.op("v", lambda c: c.tensor_tensor(
                        rtmp[0:1, 0:T], srowm[0:1, 0:T], srowm[0:1, 0:T], ALU.mult),
                        waits=[tl["rm1"]])
                def w_rm3():
                    tl["rm3"] = sch.op("v", lambda c: c.scalar_tensor_tensor(
                        srowm[0:1, TH:TH + T], srowm[0:1, TH:TH + T], EPS, rtmp[0:1, 0:T],
                        ALU.add, ALU.subtract), waits=[tl["rm2"]])
                def w_rm4():
                    tl["rm4"] = sch.op("v", lambda c: c.reciprocal(
                        rtmp[0:1, 0:T], srowm[0:1, TH:TH + T]), waits=[tl["rm3"]])
                def w_sq():
                    tl["sq"] = sch.op("s", lambda c: c.activation(
                        srowm[0:1, TH:TH + T], rtmp[0:1, 0:T], AF.Sqrt), waits=[tl["rm4"]])
                def w_nm():
                    tl["nm"] = sch.op("v", lambda c: c.scalar_tensor_tensor(
                        srowm[0:1, 0:T], srowm[0:1, 0:T], -1.0, srowm[0:1, TH:TH + T],
                        ALU.mult, ALU.mult), waits=[tl["sq"]])
                def w_bc():
                    def bcast(c):
                        c.matmul(ps[7][:, 0:T], ones1[0:1, :], srowm[0:1, TH:TH + T],
                                 start=True, stop=True)
                        return c.matmul(ps[7][:, 256:256 + T], ones1[0:1, :], srowm[0:1, 0:T],
                                        start=True, stop=True)
                    tl["bc"] = sch.op("t", bcast, waits=[tl["nm"]])
                def w_a2m():
                    def a2m(c):
                        abc = ps[7][:, 0:T].unsqueeze(1).broadcast_to([128, KC, T])
                        r3 = outb[:, :].rearrange("p (k t) -> p k t", k=KC)
                        return c.tensor_tensor(r3, r3, abc, ALU.mult)
                    tl["a2m"] = sch.op("v", a2m, waits=[tl["bc"]])
                def w_a2a():
                    def a2a(c):
                        bbc = ps[7][:, 256:256 + T].unsqueeze(1).broadcast_to([128, KC, T])
                        r3 = outb[:, :].rearrange("p (k t) -> p k t", k=KC)
                        return c.tensor_tensor(r3, r3, bbc, ALU.add)
                    tl["a2a"] = sch.op("v", a2a, waits=[tl["a2m"]])
                    a2_prev[0] = tl["a2a"]
                def w_sto():
                    def sto(e, b=b, s0=s0):
                        srcap = outb[:, :].rearrange("p (k t) -> p k t", k=KC)
                        dst = out_h[b, :, s0:s0 + T].rearrange("(k p) t -> p k t", p=128)
                        return e.dma_start(out=dst, in_=srcap)
                    out_prev[0] = sch.op("go0", sto, n_dma=1, waits=[tl["a2a"]])
                return [w_st2, w_rw1, w_rw2, w_rm1, w_rm2, w_rm3, w_rm4, w_sq, w_nm,
                        w_bc, w_a2m, w_a2a, w_sto]

            b_load(0)
            b_load(1)
            b_load(2)
            wv0 = Weaver()
            add_fold_emitters(wv0, 0)
            wv0.drain()
            emit_trq1(0)
            for f in r1_emitters(0):
                f()
            emit_btq1(0)
            emit_rowsq1(0)
            emit_bcq1(0)
            for f in qx_emitters(0):
                f()
            b_s2(0)
            for ti in range(ntiles_b):
                nxt = ti + 1 if ti + 1 < ntiles_b else None
                wv = Weaver()
                if nxt is not None:
                    add_fold_emitters(wv, nxt)
                m1l = ([(lambda n=nxt: emit_trq1(n))] + r1_emitters(nxt) +
                       [(lambda n=nxt: emit_btq1(n)), (lambda n=nxt: emit_rowsq1(n)),
                        (lambda n=nxt: emit_bcq1(n))]) if nxt is not None else []
                qxl = qx_emitters(nxt) if nxt is not None else []
                b_s3(ti, wv)
                tail = b_s4(ti, wv, m1l, qxl, nxt)
                if nxt is not None:
                    b_s2(nxt, weave=tail)
                else:
                    for f in tail:
                        f()
                b_load(ti + 3)
            final_toks = [out_prev[0]]

        else:
            def dbg(e):
                rs = []
                for b in range(BPC):
                    for c0 in range(KC):
                        rs.append(e.dma_start(out=out_h[b, c0 * 128:(c0 + 1) * 128, :],
                                              in_=x1s_h[b, c0 * 128:(c0 + 1) * 128, :]))
                return rs
            final_toks = [sch.op("qz", dbg, n_dma=BPC * KC, waits=pa_done)]

        sch.op("g", lambda c: c.engine_nop(), waits=final_toks)

        secfns = {"tensor": blk.tensor, "vector": blk.vector, "scalar": blk.scalar,
                  "gpsimd": blk.gpsimd, "sync": blk.sync}
        sem2sec = {"t": "tensor", "v": "vector", "s": "scalar", "g": "gpsimd",
                   "qp": "sync", "qx0": "sync", "qx1": "sync", "qx2": "sync", "qr0": "sync", "qr2": "sync",
                   "qr1": "sync", "pw0": "sync", "pw1": "sync", "pw2": "sync", "pw3": "sync", "qz": "sync",
                   "qw": "gpsimd", "gx0": "gpsimd", "gx1": "gpsimd",
                   "go0": "gpsimd", "go1": "gpsimd"}
        sec_ops = {}
        order = []
        for entry in sch.ops:
            secname = sem2sec[entry[1]]
            if secname not in sec_ops:
                sec_ops[secname] = (secfns[secname], [])
                order.append(secname)
            sec_ops[secname][1].append(entry)
        import os
        if os.environ.get("MK_DEBUG_CNT"):
            print("SCHED CNT:", {k: v for k, v in sch.cnt.items() if v})
            from collections import Counter
            print("OPS PER SEM:", Counter(e[1] for e in sch.ops))
        for sid in order:
            sec, entries = sec_ops[sid]
            def run(eng, entries=entries):
                for (kind, e, emitfn, narg, waits, base) in entries:
                    for (ws, wn) in waits:
                        eng.wait_ge(sch.sems[ws], wn)
                    if kind == "dma":
                        ins = emitfn(eng)
                        if not isinstance(ins, (list, tuple)):
                            ins = [ins]
                        assert len(ins) == narg, f"dma {e}: {len(ins)} != {narg}"
                        for i_ in ins:
                            i_.then_inc(sch.sems[e], 16)
                    else:
                        ctx = Ctx(eng, e, base, sch.sems)
                        last = emitfn(ctx)
                        assert ctx.nbar == narg, f"{e}: {ctx.nbar} barriers != {narg}"
                        assert last is not None, f"op on {e} returned None"
                        last.then_inc(sch.sems[e], 1)
            sec(run)
    return nc


_NC_CACHE = {}


def kernel(**inputs):
    from concourse.bass_utils import run_bass_kernel_spmd
    if "nc" not in _NC_CACHE:
        _NC_CACHE["nc"] = build_nc()
    nc = _NC_CACHE["nc"]
    in_maps = [host_prepare(inputs, core) for core in range(8)]
    res = run_bass_kernel_spmd(nc, in_maps, core_ids=list(range(8)))
    out = np.concatenate([np.asarray(r["out"]) for r in res.results], axis=0)
    return np.ascontiguousarray(out.transpose(0, 2, 1)).astype(np.float32)


# version 32
# speedup vs baseline: 1.0063x; 1.0063x over previous
import contextlib
import numpy as np
import concourse.bass as bass
import concourse.mybir as mybir

F32 = mybir.dt.float32
BF16 = mybir.dt.bfloat16
I8 = mybir.dt.int8
F8 = mybir.dt.float8e4
AF = mybir.ActivationFunctionType
ALU = mybir.AluOpType
AX = mybir.AxisListType

D = 1024
DH = 4096
S = 2048
BPC = 4
T = 256
NSEQT = S // T
NTILES = BPC * NSEQT
KC = D // 128
CC = DH // 128
NB = T // 128
TH = T + 2
EPS = 1e-5
INV_D = 1.0 / D
MAGIC = 12582912.0


class Ctx:

    def __init__(self, eng, sem, base, sems_h):
        self._eng = eng
        self._semname = sem
        self._val = base
        self._sems_h = sems_h
        self._last = None
        self.nbar = 0

    def __getattr__(self, name):
        fn = getattr(self._eng, name)
        if callable(fn):
            def wrap(*a, **k):
                ins = fn(*a, **k)
                self._last = ins
                return ins
            return wrap
        return fn

    def barrier(self):
        assert self._last is not None, "barrier with no prior instruction"
        self._val += 1
        self.nbar += 1
        self._last.then_inc(self._sems_h[self._semname], 1)
        self._eng.wait_ge(self._sems_h[self._semname], self._val)
        self._last = None


class Sched:
    def __init__(self, sems):
        self.ops = []
        self.cnt = {s: 0 for s in sems}
        self.sems = sems

    def op(self, eng, emitfn, waits=(), barriers=0, n_dma=0):
        waits = tuple(w for w in waits if w is not None)
        if n_dma:
            self.cnt[eng] += 16 * n_dma
            tok = (eng, self.cnt[eng])
            self.ops.append(("dma", eng, emitfn, n_dma, waits, None))
            return tok
        base = self.cnt[eng]
        self.cnt[eng] += barriers + 1
        tok = (eng, self.cnt[eng])
        self.ops.append(("cmp", eng, emitfn, barriers, waits, base))
        return tok


def host_prepare(inputs, core):
    x = np.asarray(inputs["L0_attn_x"])
    xT = np.ascontiguousarray(np.transpose(x[core * BPC:(core + 1) * BPC], (0, 2, 1)))
    cT = lambda a: np.ascontiguousarray(np.asarray(a).T).astype(np.float32)
    col = lambda a, n: np.ascontiguousarray(np.asarray(a).astype(np.float32).reshape(n, 128).T)
    return {
        "x": xT.astype(np.float32),
        "wq": cT(inputs["L0_attn_Wq"]), "wv": cT(inputs["L0_attn_Wv"]),
        "wg": cT(inputs["L0_attn_Wg"]), "wo": cT(inputs["L0_attn_Wo"]),
        "w1t": cT(inputs["L0_ffn_w1"]), "w2t": cT(inputs["L0_ffn_w2"]),
        "beta_col": col(np.repeat(np.asarray(inputs["L0_attn_beta"]).astype(np.float32), 64), KC),
        "lng_col": col(inputs["L0_attn_ln_gamma"], KC),
        "lnb_col": col(inputs["L0_attn_ln_beta"], KC),
        "cw_col": np.ascontiguousarray(
            np.asarray(inputs["L0_ffn_conv_w"]).astype(np.float32).reshape(CC, 128, 3)
            .transpose(1, 0, 2).reshape(128, CC * 3)),
        "cb_col": col(inputs["L0_ffn_conv_b"], CC),
        "ident": np.eye(128, dtype=np.float32),
        "ones1row": np.ones((1, 128), dtype=np.float32),
        "onescol_bf": np.ones((128, 1), dtype=np.float32),
        "onescol_f": np.ones((128, 1), dtype=np.float32),
        "onesT_h": np.ones((128, T), dtype=np.float32),
        "zpad_h": np.zeros((BPC, D, 2), dtype=np.float32),
        "zcol_h": np.zeros((128, KC), dtype=np.float32),
    }


def build_nc(phases="AB", ntiles_a=NTILES, ntiles_b=NTILES):
    nc = bass.Bass("TRN2", target_bir_lowering=False)
    x_h = nc.dram_tensor("x", [BPC, D, S], F32, kind="ExternalInput").ap()
    w_h = {nm: nc.dram_tensor(nm, [D, D], F32, kind="ExternalInput").ap()
           for nm in ("wq", "wv", "wg", "wo")}
    w1t_h = nc.dram_tensor("w1t", [D, DH], F32, kind="ExternalInput").ap()
    w2t_h = nc.dram_tensor("w2t", [DH, D], F32, kind="ExternalInput").ap()
    beta_h = nc.dram_tensor("beta_col", [128, KC], F32, kind="ExternalInput").ap()
    lng_h = nc.dram_tensor("lng_col", [128, KC], F32, kind="ExternalInput").ap()
    lnb_h = nc.dram_tensor("lnb_col", [128, KC], F32, kind="ExternalInput").ap()
    cw_h = nc.dram_tensor("cw_col", [128, CC * 3], F32, kind="ExternalInput").ap()
    cb_h = nc.dram_tensor("cb_col", [128, CC], F32, kind="ExternalInput").ap()
    id_h = nc.dram_tensor("ident", [128, 128], F32, kind="ExternalInput").ap()
    ones1_h = nc.dram_tensor("ones1row", [1, 128], F32, kind="ExternalInput").ap()
    onescol_h = nc.dram_tensor("onescol_bf", [128, 1], F32, kind="ExternalInput").ap()
    onescolf_h = nc.dram_tensor("onescol_f", [128, 1], F32, kind="ExternalInput").ap()
    onesT_hh = nc.dram_tensor("onesT_h", [128, T], F32, kind="ExternalInput").ap()
    zpad_hh = nc.dram_tensor("zpad_h", [BPC, D, 2], F32, kind="ExternalInput").ap()
    zcol_hh = nc.dram_tensor("zcol_h", [128, KC], F32, kind="ExternalInput").ap()
    out_h = nc.dram_tensor("out", [BPC, D, S], F32, kind="ExternalOutput").ap()
    x1s_h = nc.dram_tensor("x1s", [BPC, D, S], F32, kind="Internal").ap()

    es = contextlib.ExitStack()
    with es:
        def sb(name, shape, dt, stack=None):
            return (stack or es).enter_context(nc.sbuf_tensor(name, shape, dt))
        ident = sb("ident_sb", [128, 128], F32)
        ones1 = sb("ones1_sb", [1, 128], F32)
        onescol = sb("onescol_sb", [128, 1], BF16)
        lamcol = sb("lamcol", [128, KC], F32)
        betac = sb("betac", [128, KC], F32)
        lngc = sb("lngc", [128, KC], F32)
        lnbc = sb("lnbc", [128, KC], F32)
        cwc = sb("cwc", [128, CC * 3], F32)
        cbc = sb("cbc", [128, CC], F32)
        carry = sb("carry", [128, KC], F32)
        mrow_o = sb("mrow_o", [128, KC], BF16)
        mrow_f = sb("mrow_f", [128, CC], F32)
        mrow2 = sb("mrow2", [128, CC], BF16)
        srow = sb("srow", [1, 2 * TH], F32)
        stok = sb("stok", [128, 24], F32)
        stokm = sb("stokm", [128, 12], F32)
        srowm = sb("srowm", [1, 2 * TH], F32)
        srowmB = sb("srowmB", [1, 2 * TH], F32)
        rtmp = sb("rtmp", [1, T], F32)
        rtmpB = sb("rtmpB", [1, T], F32)
        sqb = sb("sqb", [128, KC * T], BF16)
        swcol = sb("swcol", [128, 4], F32)
        zsb = sb("zsb", [128, KC], F32)
        onescolf = sb("onescolf", [128, 1], F32)
        tinyrow = sb("tinyrow", [1, 8], F32)
        wtmp1 = [sb(f"wtmp1_{i}", [128, 1024], F32) for i in range(2)]
        wacc1 = sb("wacc1", [128, 2], F32)
        wsum1 = sb("wsum1", [128, 2], F32)

        semnames = ["t", "v", "s", "g", "qp", "qw", "qx0", "qx1", "qx2", "qr2", "gx0", "gx1",
                    "qr0", "qr1", "go0", "go1", "qz", "pw0", "pw1", "pw2", "pw3"]
        sems = {n: es.enter_context(nc.semaphore("sem_" + n)) for n in semnames}
        ps = [es.enter_context(nc.psum_tensor(f"ps{i}", [128, 512], F32)) for i in range(8)]
        blk = es.enter_context(nc.Block())
        sch = Sched(sems)

        def p0(e):
            return [e.dma_start(out=ident[:, :], in_=id_h[:, :]),
                    e.dma_start(out=ones1[:, :], in_=ones1_h[:, :]),
                    e.dma_start(out=betac[:, :], in_=beta_h[:, :]),
                    e.dma_start(out=lngc[:, :], in_=lng_h[:, :]),
                    e.dma_start(out=lnbc[:, :], in_=lnb_h[:, :]),
                    e.dma_start(out=cwc[:, :], in_=cw_h[:, :]),
                    e.dma_start(out=cbc[:, :], in_=cb_h[:, :]),
                    e.dma_start(out=carry[:, :], in_=zcol_hh[:, :]),
                    e.dma_start(out=zsb[:, :], in_=zcol_hh[:, :]),
                    e.dma_start(out=onescolf[:, :], in_=onescolf_h[:, :])]
        t_p0 = sch.op("qp", p0, n_dma=10)
        t_onescol = sch.op("qw", lambda e: e.dma_start(out=onescol[:, :], in_=onescol_h[:, :]),
                           n_dma=1)
        t_lam = sch.op("s", lambda c: c.activation(lamcol[:, :], betac[:, :], AF.Sigmoid),
                       waits=[t_p0])

        scopeA = contextlib.ExitStack()
        wsb = {nm: sb(nm + "_sb", [128, KC * D], BF16, scopeA) for nm in ("wq", "wv", "wg", "wo")}
        lamtile = sb("lamtile", [128, KC * T], F32, scopeA)
        onesT = sb("onesT", [128, T], F32, scopeA)
        t_onesT = sch.op("qp", lambda e: e.dma_start(out=onesT[:, :], in_=onesT_hh[:, :]),
                         n_dma=1, waits=[t_p0])

        def mk_lamtile(c):
            r = None
            for k in range(KC):
                r = c.tensor_scalar(lamtile[:, k * T:(k + 1) * T], onesT[:, :],
                                    lamcol[:, k:k + 1], None, ALU.mult)
            return r
        t_lamtile = sch.op("v", mk_lamtile, waits=[t_lam, t_onesT])

        def wload(e):
            out = []
            for nm in ("wq", "wv", "wg", "wo"):
                src = w_h[nm][:, :].rearrange("(k p) m -> p k m", p=128)
                dst = wsb[nm][:, :].rearrange("p (k m) -> p k m", k=KC)
                out.append(e.dma_start(out=dst, in_=src))
            return out
        t_w = sch.op("qw", wload, n_dma=4, waits=[t_onescol])

        def mk_mrow(c):
            for k in range(KC):
                c.reduce_sum(mrow_f[:, k:k + 1], wsb["wo"][:, k * D:(k + 1) * D], axis=AX.X)
            c.barrier()
            return c.tensor_copy(mrow_o[:, :], mrow_f[:, 0:KC])
        t_mrow = sch.op("v", mk_mrow, barriers=1, waits=[t_w])

        def ln_machinery(t_statrow, btr, bbc):
            def rowmath(c):
                c.tensor_scalar(srowm[0:1, 0:T], srowm[0:1, 0:T], INV_D, None, ALU.mult)
                c.tensor_scalar(srowm[0:1, TH:TH + T], srowm[0:1, TH:TH + T], INV_D, None, ALU.mult)
                c.barrier()
                c.tensor_tensor(srowm[0:1, TH + T:TH + T + T] if False else rtmp[0:1, 0:T],
                                srowm[0:1, 0:T], srowm[0:1, 0:T], ALU.mult)
                c.barrier()
                c.tensor_tensor(srowm[0:1, TH:TH + T], srowm[0:1, TH:TH + T], rtmp[0:1, 0:T],
                                ALU.subtract)
                c.barrier()
                c.tensor_scalar(srowm[0:1, TH:TH + T], srowm[0:1, TH:TH + T], EPS, None, ALU.add)
                c.barrier()
                return c.reciprocal(rtmp[0:1, 0:T], srowm[0:1, TH:TH + T])
            t_rm = sch.op("v", rowmath, barriers=4, waits=[t_statrow])
            t_sq = sch.op("s", lambda c: c.activation(srowm[0:1, TH:TH + T], rtmp[0:1, 0:T],
                                                      AF.Sqrt), waits=[t_rm])
            def negmu(c):
                c.tensor_tensor(srowm[0:1, 0:T], srowm[0:1, 0:T], srowm[0:1, TH:TH + T], ALU.mult)
                c.barrier()
                return c.tensor_scalar(srowm[0:1, 0:T], srowm[0:1, 0:T], -1.0, None, ALU.mult)
            t_nm = sch.op("v", negmu, barriers=1, waits=[t_sq])

            def bcast(c):
                c.matmul(ps[bbc][:, 0:T], ones1[0:1, :], srowm[0:1, TH:TH + T],
                         start=True, stop=True)
                return c.matmul(ps[bbc][:, 256:256 + T], ones1[0:1, :], srowm[0:1, 0:T],
                                start=True, stop=True)
            return sch.op("t", bcast, waits=[t_nm])

        xt = [sb(f"xt{i}", [128, KC * T], F32, scopeA) for i in range(3)]
        xb = [sb(f"xb{i}", [128, KC * T], BF16, scopeA) for i in range(2)]
        qbf = [sb(f"qbf{i}", [128, KC * T], BF16, scopeA) for i in range(2)]
        vfull = [sb(f"vfull{i}", [128, KC * T], F32, scopeA) for i in range(2)]
        state = [sb(f"state{i}", [128, KC * T], F32, scopeA) for i in range(2)]
        gate = [sb(f"gate{i}", [128, KC * T], BF16, scopeA) for i in range(2)]
        ybf = sb("ybf", [128, KC * T], BF16, scopeA)
        z1 = sb("z1", [128, KC * T], F32, scopeA)
        y1b = sb("y1b", [128, KC * T], BF16, scopeA)
        x1t = [sb(f"x1t{i}", [128, KC * T], F32, scopeA) for i in range(2)]

        xt_free = [None, None, None]
        xb_free = [None, None]
        x1_dma = [None, None]
        prev_scan = [None]
        bankA = {}
        bankWo = {}
        sty_tok = [None]
        st1_tok = [None]
        y1_par = [None, None]
        a1_prev = [None]
        row1_prev = [None]
        pend_tail = [[]]
        aft = {}

        def a_ld(ti):
            if ti >= ntiles_a:
                return
            buf3 = ti % 3
            qx = f"qx{ti % 3}"
            b, si = divmod(ti, NSEQT)
            s0 = si * T
            def ld(e, b=b, s0=s0, buf3=buf3):
                srcap = x_h[b, :, s0:s0 + T].rearrange("(k p) t -> p k t", p=128)
                dst = xt[buf3][:, :].rearrange("p (k t) -> p k t", k=KC)
                return e.dma_start(out=dst, in_=srcap)
            aft[("x", ti)] = sch.op(qx, ld, n_dma=1, waits=[xt_free[buf3]])

        def front_emitters(ti):
            if ti >= ntiles_a:
                return []
            buf = ti % 2
            fs = {}
            aft[ti] = fs
            out = []
            buf3 = ti % 3
            def e_xb():
                t_x = aft.pop(("x", ti))
                fs["xb"] = sch.op("s", lambda c, buf=buf, buf3=buf3: c.activation(
                    xb[buf][:, :], xt[buf3][:, :], AF.Identity),
                    waits=[t_x, xb_free[buf]])
            out.append(e_xb)
            evq, evv, evg = [], [], []
            for wname, dstl, evl in (("wq", qbf, evq), ("wv", vfull, evv), ("wg", gate, evg)):
                for mp in range(4):
                    def e_mm(wname=wname, mp=mp):
                        bank = ps[mp]
                        def mm(c, wname=wname, mp=mp, bank=bank, buf=buf):
                            r = None
                            for k in range(KC):
                                for half in range(2):
                                    m = 2 * mp + half
                                    r = c.matmul(bank[:, half * 256:half * 256 + T],
                                                 wsb[wname][:, k * D + m * 128:k * D + (m + 1) * 128],
                                                 xb[buf][:, k * T:(k + 1) * T],
                                                 start=(k == 0 and half == 0),
                                                 stop=(k == KC - 1 and half == 1))
                            return r
                        fs[("mm", wname, mp)] = sch.op("t", mm,
                                                       waits=[fs["xb"], t_w, bankA.get(mp)])
                    def e_ev(wname=wname, mp=mp, dstl=dstl, evl=evl):
                        bank = ps[mp]
                        if wname == "wg":
                            emitf = lambda c, dstl=dstl, mp=mp, bank=bank, buf=buf: c.activation(
                                dstl[buf][:, 2 * mp * T:(2 * mp + 2) * T], bank[:, 0:2 * T],
                                AF.Silu)
                            w2_ = [y1_par[buf]]
                        else:
                            emitf = lambda c, dstl=dstl, mp=mp, bank=bank, buf=buf: c.activation(
                                dstl[buf][:, 2 * mp * T:(2 * mp + 2) * T], bank[:, 0:2 * T],
                                AF.Identity)
                            w2_ = []
                        t = sch.op("s", emitf, waits=[fs[("mm", wname, mp)]] + w2_)
                        evl.append(t)
                        bankA[mp] = t
                    out += [e_mm, e_ev]
            def e_scan():
                b, si = divmod(ti, NSEQT)
                def scans(c, si=si, buf=buf):
                    if si == 0:
                        c.tensor_scalar(carry[:, :], carry[:, :], 0.0, None, ALU.mult)
                        c.barrier()
                    for k in range(KC):
                        c.tensor_tensor_scan(state[buf][:, k * T:(k + 1) * T],
                                             lamtile[:, k * T:(k + 1) * T],
                                             vfull[buf][:, k * T:(k + 1) * T],
                                             carry[:, k:k + 1], ALU.mult, ALU.add)
                    c.barrier()
                    r = None
                    for k in range(KC):
                        r = c.tensor_copy(carry[:, k:k + 1],
                                          state[buf][:, (k + 1) * T - 1:(k + 1) * T])
                    return r
                fs["scan"] = sch.op("v", scans, barriers=(2 if si == 0 else 1),
                                    waits=evv + [t_lamtile, prev_scan[0], t_p0])
                prev_scan[0] = fs["scan"]
            out.append(e_scan)
            fs["evq"] = evq
            fs["evg"] = evg
            return out

        def back_emitters(ti):
            buf = ti % 2
            buf3 = ti % 3
            b, si = divmod(ti, NSEQT)
            s0 = si * T
            ft = aft.pop(ti)
            bt = {}
            def w_ty():
                bt["ty"] = sch.op("v", lambda c, buf=buf: c.tensor_tensor(
                    ybf[:, :], qbf[buf][:, :], state[buf][:, :], ALU.mult),
                    waits=[ft["scan"], sty_tok[0]] + ft["evq"])
            def w_ysq():
                bt["ysq"] = sch.op("s", lambda c: c.activation(sqb[:, :], ybf[:, :], AF.Square),
                                   waits=[bt["ty"], st1_tok[0]])
            def w_statsy():
                def stats_y(c):
                    for k in range(KC):
                        c.matmul(ps[7][0:1, 0:T], onescol[:, 0:1], ybf[:, k * T:(k + 1) * T],
                                 start=(k == 0), stop=(k == KC - 1))
                    r = None
                    for k in range(KC):
                        r = c.matmul(ps[7][0:1, 256:256 + T], onescol[:, 0:1],
                                     sqb[:, k * T:(k + 1) * T],
                                     start=(k == 0), stop=(k == KC - 1))
                    return r
                bt["sty"] = sch.op("t", stats_y,
                                   waits=[bt["ysq"], t_onescol, bt["ty"], row1_prev[0]])
                sty_tok[0] = bt["sty"]
            def w_rowy():
                def rowy(c):
                    c.tensor_copy(srowm[0:1, 0:T], ps[7][0:1, 0:T])
                    return c.tensor_copy(srowm[0:1, TH:TH + T], ps[7][0:1, 256:256 + T])
                bt["rowy"] = sch.op("v", rowy, waits=[bt["sty"]])
            def mk_rm(px, prev_key, srow, rt):
                def w_rm1():
                    def rm1(c):
                        c.tensor_scalar(srow[0:1, 0:T], srow[0:1, 0:T], INV_D, None, ALU.mult)
                        return c.tensor_scalar(srow[0:1, TH:TH + T], srow[0:1, TH:TH + T],
                                               INV_D, None, ALU.mult)
                    bt[px + "rm1"] = sch.op("v", rm1, waits=[bt[prev_key]])
                def w_rm2():
                    bt[px + "rm2"] = sch.op("v", lambda c: c.tensor_tensor(
                        rt[0:1, 0:T], srow[0:1, 0:T], srow[0:1, 0:T], ALU.mult),
                        waits=[bt[px + "rm1"]])
                def w_rm3():
                    bt[px + "rm3"] = sch.op("v", lambda c: c.scalar_tensor_tensor(
                        srow[0:1, TH:TH + T], srow[0:1, TH:TH + T], EPS, rt[0:1, 0:T],
                        ALU.add, ALU.subtract), waits=[bt[px + "rm2"]])
                def w_rm4():
                    bt[px + "rm4"] = sch.op("v", lambda c: c.reciprocal(
                        rt[0:1, 0:T], srow[0:1, TH:TH + T]), waits=[bt[px + "rm3"]])
                def w_sq():
                    bt[px + "sq"] = sch.op("s", lambda c: c.activation(
                        srow[0:1, TH:TH + T], rt[0:1, 0:T], AF.Sqrt),
                        waits=[bt[px + "rm4"]])
                def w_nm():
                    bt[px + "nm"] = sch.op("v", lambda c: c.scalar_tensor_tensor(
                        srow[0:1, 0:T], srow[0:1, 0:T], -1.0, srow[0:1, TH:TH + T],
                        ALU.mult, ALU.mult), waits=[bt[px + "sq"]])
                return [w_rm1, w_rm2, w_rm3, w_rm4, w_sq, w_nm]
            def mk_bc(px, srow, extra=()):
                def w_bc():
                    def bcast(c):
                        c.matmul(ps[6][:, 0:T], ones1[0:1, :], srow[0:1, TH:TH + T],
                                 start=True, stop=True)
                        return c.matmul(ps[6][:, 256:256 + T], ones1[0:1, :], srow[0:1, 0:T],
                                        start=True, stop=True)
                    bt[px + "bc"] = sch.op("t", bcast,
                                           waits=[bt[px + "nm"]] + [x[0] for x in extra])
                return w_bc
            def w_apym():
                def apym(c):
                    abc = ps[6][:, 0:T].unsqueeze(1).broadcast_to([128, KC, T])
                    y3 = ybf[:, :].rearrange("p (k t) -> p k t", k=KC)
                    z3 = z1[:, :].rearrange("p (k t) -> p k t", k=KC)
                    return c.tensor_tensor(z3, y3, abc, ALU.mult)
                bt["apym"] = sch.op("v", apym, waits=[bt["ybc"]])
            def w_apya():
                def apya(c):
                    bbc = ps[6][:, 256:256 + T].unsqueeze(1).broadcast_to([128, KC, T])
                    z3 = z1[:, :].rearrange("p (k t) -> p k t", k=KC)
                    return c.tensor_tensor(z3, z3, bbc, ALU.add)
                bt["apya"] = sch.op("v", apya, waits=[bt["apym"]])
            def w_gb():
                def gb(c):
                    r = None
                    for k in range(KC):
                        r = c.tensor_scalar(z1[:, k * T:(k + 1) * T], z1[:, k * T:(k + 1) * T],
                                            lngc[:, k:k + 1], lnbc[:, k:k + 1], ALU.mult, ALU.add)
                    return r
                bt["gb"] = sch.op("v", gb, waits=[bt["apya"]])
            def w_y1():
                bt["y1"] = sch.op("v", lambda c, buf=buf: c.tensor_tensor(
                    y1b[:, :], z1[:, :], gate[buf][:, :], ALU.mult),
                    waits=[bt["gb"]] + ft["evg"])
                y1_par[buf] = bt["y1"]
            def mk_wo(mp):
                def w_wo(mp=mp):
                    bank = ps[4 + (mp % 2)]
                    def mm(c, mp=mp, bank=bank):
                        r = None
                        for k in range(KC):
                            for half in range(2):
                                m = 2 * mp + half
                                r = c.matmul(bank[:, half * 256:half * 256 + T],
                                             wsb["wo"][:, k * D + m * 128:k * D + (m + 1) * 128],
                                             y1b[:, k * T:(k + 1) * T],
                                             start=(k == 0 and half == 0),
                                             stop=(k == KC - 1 and half == 1))
                        return r
                    bt[("wo", mp)] = sch.op("t", mm, waits=[bt["y1"], bankWo.get(mp % 2)])
                return w_wo
            def mk_evr(mp):
                def w_evr(mp=mp):
                    bank = ps[4 + (mp % 2)]
                    t = sch.op("v", lambda c, mp=mp, bank=bank, buf3=buf3: c.tensor_tensor(
                        xt[buf3][:, 2 * mp * T:(2 * mp + 2) * T], bank[:, 0:2 * T],
                        xt[buf3][:, 2 * mp * T:(2 * mp + 2) * T], ALU.add),
                        waits=[bt[("wo", mp)], bt.get(("evr", mp - 1))])
                    bt[("evr", mp)] = t
                    bankWo[mp % 2] = t
                return w_evr
            def w_rsq():
                bt["rsq"] = sch.op("s", lambda c, buf3=buf3: c.activation(
                    sqb[:, :], xt[buf3][:, :], AF.Square), waits=[bt[("evr", 3)]])
            def w_stats1():
                def stats_1(c, buf=buf):
                    for k in range(KC):
                        c.matmul(ps[7][0:1, 0:T], onescol[:, 0:1], xb[buf][:, k * T:(k + 1) * T],
                                 start=(k == 0), stop=False)
                    for k in range(KC):
                        c.matmul(ps[7][0:1, 0:T], mrow_o[:, k:k + 1], y1b[:, k * T:(k + 1) * T],
                                 start=False, stop=(k == KC - 1))
                    r = None
                    for k in range(KC):
                        r = c.matmul(ps[7][0:1, 256:256 + T], onescol[:, 0:1],
                                     sqb[:, k * T:(k + 1) * T],
                                     start=(k == 0), stop=(k == KC - 1))
                    return r
                bt["st1"] = sch.op("t", stats_1, waits=[bt["rsq"], t_mrow])
                st1_tok[0] = bt["st1"]
                xb_free[buf] = bt["st1"]
            def w_row1():
                def row1(c):
                    c.tensor_copy(srowmB[0:1, 0:T], ps[7][0:1, 0:T])
                    return c.tensor_copy(srowmB[0:1, TH:TH + T], ps[7][0:1, 256:256 + T])
                bt["row1"] = sch.op("v", row1, waits=[bt["st1"]])
                row1_prev[0] = bt["row1"]
            def w_a1m():
                def a1m(c, buf=buf, buf3=buf3):
                    abc = ps[6][:, 0:T].unsqueeze(1).broadcast_to([128, KC, T])
                    xt3 = xt[buf3][:, :].rearrange("p (k t) -> p k t", k=KC)
                    x13 = x1t[buf][:, :].rearrange("p (k t) -> p k t", k=KC)
                    return c.tensor_tensor(x13, xt3, abc, ALU.mult)
                bt["a1m"] = sch.op("v", a1m, waits=[bt["1bc"], x1_dma[buf]])
            def w_a1a():
                def a1a(c, buf=buf):
                    bbc = ps[6][:, 256:256 + T].unsqueeze(1).broadcast_to([128, KC, T])
                    x13 = x1t[buf][:, :].rearrange("p (k t) -> p k t", k=KC)
                    return c.tensor_tensor(x13, x13, bbc, ALU.add)
                bt["a1a"] = sch.op("v", a1a, waits=[bt["a1m"]])
                xt_free[buf3] = bt["a1a"]
                a1_prev[0] = bt["a1a"]
            def w_stx1():
                gx = "gx0" if buf == 0 else "gx1"
                def stx1(e, b=b, s0=s0, buf=buf):
                    srcap = x1t[buf][:, :].rearrange("p (k t) -> p k t", k=KC)
                    dst = x1s_h[b, :, s0:s0 + T].rearrange("(k p) t -> p k t", p=128)
                    return e.dma_start(out=dst, in_=srcap)
                x1_dma[buf] = sch.op(gx, stx1, n_dma=1, waits=[bt["a1a"]])
            yrm = mk_rm("y", "rowy", srowm, rtmp)
            orm = mk_rm("1", "row1", srowmB, rtmpB)
            bp = dict(ty=w_ty, ysq=w_ysq, statsy=w_statsy, rowy=w_rowy,
                      yrm1=yrm[0], yrm2=yrm[1], yrm3=yrm[2], yrm4=yrm[3],
                      ysqrt=yrm[4], ynm=yrm[5],
                      ybc=mk_bc("y", srowm, extra=[a1_prev]),
                      apym=w_apym, apya=w_apya, gb=w_gb, y1=w_y1,
                      wo0=mk_wo(0), evr0=mk_evr(0), wo1=mk_wo(1), evr1=mk_evr(1),
                      wo2=mk_wo(2), evr2=mk_evr(2), wo3=mk_wo(3), evr3=mk_evr(3))
            tail = ([w_rsq, w_stats1, w_row1] + orm +
                    [mk_bc("1", srowmB), w_a1m, w_a1a, w_stx1])
            return bp, tail

        pp1l = []
        if "B" in phases:
            t_z1 = sch.op("v", lambda c: c.tensor_scalar(
                wsum1[:, 0:2], lamcol[:, 0:2], 0.0, None, ALU.mult), waits=[t_p0])
            halves = []
            for ch in range(KC):
                for h in range(4):
                    halves.append((w1t_h, ch, h, 0))
            for ch in range(CC):
                halves.append((w2t_h, ch, 0, 1))
            pp_ld = [None, None]
            pp_abs = [None]
            pp_add = [t_z1]
            def mk_pp(i):
                hh, ch, h, col_i = halves[i]
                pb = i % 2
                wid = 1024
                def e_ld():
                    def ldw(e, hh=hh, ch=ch, h=h, pb=pb, wid=wid):
                        return e.dma_start(out=wtmp1[pb][:, 0:wid],
                                           in_=hh[ch * 128:(ch + 1) * 128, h * wid:(h + 1) * wid])
                    tok = sch.op(f"pw{pb}", ldw, n_dma=1, waits=[pp_ld[pb]])
                    pp_ld[pb] = tok
                    mk_pp.lt[i] = tok
                def e_abs():
                    t = sch.op("s", lambda c, pb=pb, wid=wid: c.activation(
                        wtmp1[pb][:, 0:wid], wtmp1[pb][:, 0:wid], AF.Abs,
                        accum_out=wacc1[:, pb:pb + 1]),
                        waits=[mk_pp.lt[i], pp_abs[0], pp_add[0]])
                    pp_abs[0] = t
                    pp_ld[pb] = t
                    mk_pp.at[i] = t
                def e_add():
                    t = sch.op("v", lambda c, col_i=col_i, pb=pb: c.tensor_tensor(
                        wsum1[:, col_i:col_i + 1], wsum1[:, col_i:col_i + 1],
                        wacc1[:, pb:pb + 1], ALU.add), waits=[mk_pp.at[i], pp_add[0]])
                    pp_add[0] = t
                return [e_ld, e_abs, e_add]
            mk_pp.lt = {}
            mk_pp.at = {}
            for i in range(len(halves)):
                pp1l.extend(mk_pp(i))
        if "A" in phases:
            a_ld(0)
            for f in front_emitters(0):
                f()
            for ti in range(ntiles_a):
                a_ld(ti + 1)
                bp, tail = back_emitters(ti)
                fq = front_emitters(ti + 1)
                tl = pend_tail[0]
                def tpull(n=1, tl=tl):
                    for _ in range(n):
                        if tl:
                            tl.pop(0)()
                def fpull(n=1, fq=fq):
                    for _ in range(n):
                        if fq:
                            fq.pop(0)()
                tpull(2)
                bp["ty"](); fpull()
                tpull()
                bp["ysq"](); fpull()
                tpull()
                bp["statsy"](); fpull()
                tpull()
                bp["rowy"](); fpull()
                tpull(); bp["yrm1"](); fpull()
                tpull(); bp["yrm2"](); fpull()
                tpull(); bp["yrm3"](); fpull()
                tpull(); bp["yrm4"](); fpull()
                tpull(); bp["ysqrt"](); fpull()
                tpull(); bp["ynm"](); fpull()
                tpull()
                bp["ybc"](); fpull()
                tpull(99)
                bp["apym"](); fpull(2)
                bp["apya"](); fpull(2)
                bp["gb"](); fpull(2)
                bp["y1"](); fpull(2)
                bp["wo0"](); fpull(2); bp["evr0"](); fpull(2)
                bp["wo1"](); fpull(2); bp["evr1"](); fpull(2)
                bp["wo2"](); fpull(2); bp["evr2"](); fpull(2)
                bp["wo3"](); fpull(99); bp["evr3"]()
                for _ in range(10):
                    if pp1l:
                        pp1l.pop(0)()
                pend_tail[0] = tail
            for f in pend_tail[0]:
                f()
        while pp1l:
            pp1l.pop(0)()
        pa_done = [x1_dma[0], x1_dma[1], prev_scan[0], xt_free[0], xt_free[1]]

        scopeA.close()

        kw1 = sb("kw1", [128, KC * DH], F8)
        kw2 = sb("kw2", [128, CC * D], F8)
        t_kw = t_mrow2 = None
        if "B" in phases:
            scopeP = contextlib.ExitStack()
            wtmp = [sb(f"wtmp{i}", [128, DH], F32, scopeP) for i in range(4)]
            wi8 = [sb(f"wi8{i}", [128, DH], I8, scopeP) for i in range(2)]

            def total_mm(c):
                c.matmul(ps[6][0:1, 0:1], wsum1[:, 0:1], onescolf[:, 0:1], start=True, stop=True)
                return c.matmul(ps[6][0:1, 1:2], wsum1[:, 1:2], onescolf[:, 0:1], start=True, stop=True)
            t_tot = sch.op("t", total_mm, waits=[pp_add[0], t_p0] + pa_done)

            def scalemath(c):
                c.tensor_copy(tinyrow[0:1, 0:2], ps[6][0:1, 0:2])
                c.barrier()
                c.tensor_scalar(tinyrow[0:1, 0:2], tinyrow[0:1, 0:2], 1.0 / (DH * D), None, ALU.mult)
                c.barrier()
                c.tensor_scalar(tinyrow[0:1, 0:2], tinyrow[0:1, 0:2], 1e-5, None, ALU.max)
                c.barrier()
                return c.reciprocal(tinyrow[0:1, 2:4], tinyrow[0:1, 0:2])
            t_sm = sch.op("v", scalemath, barriers=3, waits=[t_tot])

            def bc_scales(c):
                c.matmul(ps[6][:, 4:5], ones1[0:1, :], tinyrow[0:1, 2:3], start=True, stop=True)
                c.matmul(ps[6][:, 5:6], ones1[0:1, :], tinyrow[0:1, 3:4], start=True, stop=True)
                c.matmul(ps[6][:, 6:7], ones1[0:1, :], tinyrow[0:1, 1:2], start=True, stop=True)
                return c.matmul(ps[6][:, 7:8], ones1[0:1, :], tinyrow[0:1, 0:1], start=True, stop=True)
            t_bs = sch.op("t", bc_scales, waits=[t_sm])
            t_swc = sch.op("v", lambda c: c.tensor_copy(swcol[:, 0:4], ps[6][:, 4:8]), waits=[t_bs])
            t_cw = sch.op("v", lambda c: c.tensor_scalar(cwc[:, :], cwc[:, :],
                                                         swcol[:, 3:4], None, ALU.mult),
                          waits=[t_swc, t_p0])

            chunks2 = [("w1", w1t_h, ch, DH, 0, kw1) for ch in range(KC)] + \
                      [("w2", w2t_h, ch, D, 1, kw2) for ch in range(CC)]
            ld2 = [None, None, None, None]
            qa = {}
            qb = {}
            qc = {}
            qd = {}
            def eld(i):
                if i >= len(chunks2):
                    return
                nm, hh, ch, chw, scol, dst = chunks2[i]
                pb = i % 4
                pw = f"pw{pb}"
                def ldw(e, hh=hh, ch=ch, chw=chw, pb=pb):
                    return e.dma_start(out=wtmp[pb][:, 0:chw], in_=hh[ch * 128:(ch + 1) * 128, :])
                qa[("l", i)] = sch.op(pw, ldw, n_dma=1,
                                      waits=[ld2[pb], t_swc, t_cw] +
                                            (pa_done if i < 4 else []))
            def eq1(i):
                nm, hh, ch, chw, scol, dst = chunks2[i]
                pb = i % 4
                qa[i] = sch.op("v", lambda c, pb=pb, chw=chw, scol=scol: c.tensor_scalar(
                    wtmp[pb][:, 0:chw], wtmp[pb][:, 0:chw],
                    swcol[:, scol:scol + 1], 1.0, ALU.mult, ALU.min),
                    waits=[qa[("l", i)], qd.get(i - 4)])
            def eq2(i):
                nm, hh, ch, chw, scol, dst = chunks2[i]
                pb = i % 4
                qb[i] = sch.op("v", lambda c, pb=pb, chw=chw: c.tensor_scalar(
                    wtmp[pb][:, 0:chw], wtmp[pb][:, 0:chw], -1.0, None, ALU.max),
                    waits=[qa[i]])
            def eq3(i):
                nm, hh, ch, chw, scol, dst = chunks2[i]
                pb = i % 4
                wb = i % 2
                qc[i] = sch.op("v", lambda c, pb=pb, chw=chw, wb=wb: c.tensor_scalar(
                    wi8[wb][:, 0:chw], wtmp[pb][:, 0:chw], MAGIC, -MAGIC, ALU.add, ALU.add),
                    waits=[qb[i], qd.get(i - 2)])
            def eq4(i):
                nm, hh, ch, chw, scol, dst = chunks2[i]
                wb = i % 2
                qd[i] = sch.op("v", lambda c, chw=chw, wb=wb, dst=dst, ch=ch: c.tensor_copy(
                    dst[:, ch * chw:(ch + 1) * chw], wi8[wb][:, 0:chw]), waits=[qc[i]])
                ld2[i % 4] = qd[i]
            for i in range(4):
                eld(i)
            eq1(0)
            eq2(0)
            eq1(1)
            for i in range(len(chunks2)):
                eq3(i)
                if i + 2 < len(chunks2):
                    eq1(i + 2)
                eq4(i)
                eld(i + 4)
                if i + 1 < len(chunks2):
                    eq2(i + 1)
            t_kw = qd[len(chunks2) - 1]

            def mr2(c):
                for ch in range(CC):
                    c.reduce_sum(mrow_f[:, ch:ch + 1], kw2[:, ch * D:(ch + 1) * D], axis=AX.X)
                c.barrier()
                return c.tensor_copy(mrow2[:, :], mrow_f[:, 0:CC])
            t_mrow2 = sch.op("v", mr2, barriers=1, waits=[t_kw])
            scopeP.close()

        if "B" in phases:
            x1ld = [sb(f"x1ld{i}", [128, KC * TH], F32) for i in range(3)]
            xq = sb("xq", [128, KC * TH], BF16)
            convA = [sb(f"convA{i}", [128, T], F32) for i in range(2)]
            convB1 = sb("convB1", [128, T], F32)
            hsil = sb("hsil", [128, CC * T], F32)
            k2bf = [sb(f"k2bf{i}", [128, 2 * T], BF16) for i in range(2)]
            k2i8 = [sb(f"k2i8{i}", [128, 2 * T], I8) for i in range(2)]
            res2 = sb("res2", [128, KC * T], F32)
            habs1 = sb("habs1", [128, 2 * T], F32)
            outb = sb("outb", [128, KC * T], F32)
            q1m = sb("q1m", [128, 3 * TH], F32)
            zq1 = sb("zq1", [128, 4 * TH], F32)
            zqi1 = sb("zqi1", [128, 4 * TH], I8)
            srowq1 = sb("srowq1", [1, 2 * TH], F32)
            e2bc2 = sb("e2bc2", [128, T], F32)
            MB0 = 2 * TH
            HB0 = 2 * TH + 2 * TH
            CB0 = HB0 + 2 * T
            mbuf = res2
            convB = res2

            x1_free = [None, None, None]
            out_prev = [None]
            a2_prev = [None]
            fin_prev = [None]
            m1q_free = [None]
            stok1_free = [None]
            stok2_free = [None]
            srowq_free = [None]
            xqfree = [None, None]
            ln2row_tok = [None]
            res2q_free = [None, None]
            evm_prev = [None, None]
            xq_free_warm = [None]
            hsilfree = [None, None]
            mf1_tok = {}
            qx_tok = {}
            trq1_tok = {}
            r1_tok = {}
            bsq_tok = {}
            bcq1_tok = {}
            stageB = {}

            class Weaver:
                def __init__(self):
                    self.q = []
                def add(self, fn):
                    self.q.append(fn)
                def pull(self, n=1):
                    for _ in range(n):
                        if self.q:
                            self.q.pop(0)()
                def drain(self):
                    while self.q:
                        self.q.pop(0)()

            def b_load(ti):
                if ti >= ntiles_b:
                    return
                b, si = divmod(ti, NSEQT)
                s0 = si * T
                bufl = ti % 3
                qr = f"qr{bufl}"
                lo = max(s0 - 1, 0)
                hi = min(s0 - 1 + TH, S)
                doff = lo - (s0 - 1)
                ncols = hi - lo
                def ld(e, b=b, lo=lo, hi=hi, doff=doff, ncols=ncols, bufl=bufl):
                    srcap = x1s_h[b, :, lo:hi].rearrange("(k p) t -> p k t", p=128)
                    dst = x1ld[bufl][:, :].rearrange("p (k t) -> p k t", k=KC)[:, :, doff:doff + ncols]
                    return e.dma_start(out=dst, in_=srcap)
                wx = [x1_free[bufl]] + ([x1_dma[0], x1_dma[1], t_p0, t_kw, t_mrow2] if ti <= 2 else [])
                t_x1 = sch.op(qr, ld, n_dma=1, waits=wx)
                if si == 0 or si == NSEQT - 1:
                    zc = 0 if si == 0 else TH - 1
                    def zhalo(c, zc=zc, bufl=bufl):
                        r = None
                        for k in range(KC):
                            r = c.tensor_copy(x1ld[bufl][:, k * TH + zc:k * TH + zc + 1], zsb[:, 0:1])
                        return r
                    t_x1 = sch.op("v", zhalo, waits=[t_x1])
                stageB[("ld", ti)] = t_x1

            def add_fold_emitters(wv, ti):
                xl = x1ld[ti % 3]
                m1v = q1m[:, 0:TH]
                tmpX = [q1m[:, TH:2 * TH], q1m[:, 2 * TH:3 * TH]]
                st = {}
                def e_a0():
                    st["x1"] = stageB.pop(("ld", ti))
                    base = [st["x1"], m1q_free[0], mf1_tok.get(ti - 1)]
                    st[("f", 0)] = sch.op("v", lambda c: c.scalar_tensor_tensor(
                        m1v, xl[:, 0:TH], -1.0, xl[:, 0:TH], ALU.mult, ALU.max), waits=base)
                wv.add(e_a0)
                def mk_abs(k, wkey):
                    def e_abs(k=k, wkey=wkey):
                        tm = tmpX[k % 2]
                        st[("a", k)] = sch.op("v", lambda c, tm=tm, k=k: c.scalar_tensor_tensor(
                            tm, xl[:, k * TH:(k + 1) * TH], -1.0, xl[:, k * TH:(k + 1) * TH],
                            ALU.mult, ALU.max),
                            waits=[st["x1"], st.get(wkey),
                                   mf1_tok.get(ti - 1) if k <= 2 else None])
                    return e_abs
                def mk_fold(k):
                    def e_fold(k=k):
                        tm = tmpX[k % 2]
                        st[("f", k)] = sch.op("v", lambda c, tm=tm: c.tensor_tensor(
                            m1v, m1v, tm, ALU.max), waits=[st[("a", k)], st[("f", k - 1)]])
                        if k == KC - 1:
                            mf1_tok[ti] = st[("f", k)]
                    return e_fold
                wv.add(mk_abs(1, None))
                wv.add(mk_abs(2, None))
                for k in range(1, KC):
                    wv.add(mk_fold(k))
                    if k + 2 < KC:
                        wv.add(mk_abs(k + 2, ("f", k)))

            def emit_trq1(ti):
                def trq(c):
                    c.transpose(ps[0][:, 0:128], q1m[:, 0:128], ident[:, :])
                    c.transpose(ps[0][:, 128:256], q1m[:, 128:256], ident[:, :])
                    return c.transpose(ps[0][:, 256:384], q1m[:, 130:258], ident[:, :])
                trq1_tok[ti] = sch.op("t", trq, waits=[mf1_tok[ti], fin_prev[0]])
                m1q_free[0] = trq1_tok[ti]

            def r1_emitters(ti):
                st = {}
                def r1a():
                    def f(c):
                        r = None
                        for j in range(3):
                            r = c.reduce_max(stok[:, j:j + 1], ps[0][:, j * 128:(j + 1) * 128],
                                             axis=AX.X)
                        return r
                    st["a"] = sch.op("v", f, waits=[trq1_tok[ti], stok1_free[0]])
                def r1b():
                    st["b"] = sch.op("v", lambda c: c.tensor_scalar(
                        stok[:, 0:3], stok[:, 0:3], 1e-5, None, ALU.max), waits=[st["a"]])
                def r1c():
                    st["c"] = sch.op("v", lambda c: c.reciprocal(stok[:, 4:7], stok[:, 0:3]),
                                     waits=[st["b"]])
                def r1d():
                    st["d"] = sch.op("v", lambda c: c.tensor_scalar(
                        stok[:, 4:7], stok[:, 4:7], 127.0, None, ALU.mult), waits=[st["c"]])
                def r1e():
                    st["e"] = sch.op("v", lambda c: c.tensor_scalar(
                        stok[:, 8:11], stok[:, 0:3], 1.0 / 127.0, None, ALU.mult),
                        waits=[st["b"]])
                    r1_tok[ti] = (st["d"], st["e"])
                return [r1a, r1b, r1c, r1d, r1e]

            def emit_btq1(ti):
                def btq(c):
                    for j in range(3):
                        c.transpose(ps[1][0:1, j * 128:(j + 1) * 128], stok[:, 4 + j:5 + j],
                                    ident[:, :])
                    c.transpose(ps[1][0:1, 384:512], stok[:, 8:9], ident[:, :])
                    c.transpose(ps[0][0:1, 384:512], stok[:, 9:10], ident[:, :])
                    return c.transpose(ps[0][0:1, 0:128], stok[:, 10:11], ident[:, :])
                d, e = r1_tok[ti]
                t = sch.op("t", btq, waits=[d, e, fin_prev[0]])
                stok1_free[0] = t
                bsq_tok[ti] = t

            def emit_rowsq1(ti):
                def rowsq(c):
                    c.tensor_copy(srowq1[0:1, 0:256], ps[1][0:1, 0:256])
                    c.tensor_copy(srowq1[0:1, 256:258], ps[1][0:1, 382:384])
                    c.tensor_copy(srowq1[0:1, TH:TH + 128], ps[1][0:1, 384:512])
                    c.tensor_copy(srowq1[0:1, TH + 128:TH + 256], ps[0][0:1, 384:512])
                    return c.tensor_copy(srowq1[0:1, TH + 256:TH + 258], ps[0][0:1, 126:128])
                bsq_tok[ti] = sch.op("v", rowsq, waits=[bsq_tok[ti], srowq_free[0]])

            def emit_bcq1(ti):
                def bcq(c):
                    c.matmul(ps[0][:, 0:TH], ones1[0:1, :], srowq1[0:1, 0:TH],
                             start=True, stop=True)
                    return c.matmul(ps[1][:, 0:TH], ones1[0:1, :], srowq1[0:1, TH:2 * TH],
                                    start=True, stop=True)
                t = sch.op("t", bcq, waits=[bsq_tok[ti]])
                srowq_free[0] = t
                bcq1_tok[ti] = t

            def qx_emitters(ti):
                bufl = ti % 3
                s1bb = ps[0][:, 0:TH].unsqueeze(1).broadcast_to([128, 2, TH])
                s1ib = ps[1][:, 0:TH].unsqueeze(1).broadcast_to([128, 2, TH])
                zqQ = lambda j: zq1[:, j * 2 * TH:(j + 1) * 2 * TH]
                ziQ = lambda j: zqi1[:, j * 2 * TH:(j + 1) * 2 * TH]
                st = {}
                out = []
                for kp in range(KC // 2):
                    jj = kp % 2
                    def e_qm(kp=kp, jj=jj):
                        def f(c, kp=kp, jj=jj, bufl=bufl):
                            x3 = x1ld[bufl][:, 2 * kp * TH:(2 * kp + 2) * TH].rearrange(
                                "p (k t) -> p k t", k=2)
                            z3 = zqQ(jj)[:, :].rearrange("p (k t) -> p k t", k=2)
                            return c.tensor_tensor(z3, x3, s1bb, ALU.mult)
                        st[("m", kp)] = sch.op("v", f, waits=[bcq1_tok[ti], st.get(("i", kp - 2))] +
                                               ([xqfree[0], xqfree[1]] if kp <= 1 else []))
                    def e_qi(kp=kp, jj=jj):
                        def f(c, jj=jj):
                            return c.tensor_scalar(ziQ(jj), zqQ(jj), MAGIC, -MAGIC,
                                                   ALU.add, ALU.add)
                        st[("i", kp)] = sch.op("v", f, waits=[st[("m", kp)], st.get(("q", kp - 2))])
                    def e_qq(kp=kp, jj=jj):
                        def f(c, kp=kp, jj=jj):
                            zi3 = ziQ(jj).rearrange("p (k t) -> p k t", k=2)
                            xq3 = xq[:, 2 * kp * TH:(2 * kp + 2) * TH].rearrange(
                                "p (k t) -> p k t", k=2)
                            return c.tensor_tensor(xq3, zi3, s1ib, ALU.mult)
                        st[("q", kp)] = sch.op("v", f, waits=[st[("i", kp)]])
                        if kp == KC // 2 - 1:
                            qx_tok[ti] = st[("q", kp)]
                    out += [e_qm, e_qi, e_qq]
                return out

            def b_s2(ti, weave=()):
                buf = ti % 2
                t_qx = qx_tok[ti]
                silu_toks = []
                sil_prev = {}
                cB_prev = {}
                conv_done = {}
                wq = list(weave)
                def wpull(n=1):
                    for _ in range(n):
                        if wq:
                            wq.pop(0)()
                cAap = lambda pb: convA[pb][:, 0:T]
                cBap = lambda pb: convB[:, CB0:CB0 + T] if pb == 0 else convB1[:, 0:T]
                habs_free = {}
                def habs_ap(gb):
                    return mbuf[:, HB0:HB0 + 2 * T] if gb == 0 else habs1[:, :]
                mAB = [lambda: mbuf[:, MB0:MB0 + 2 * T], lambda: res2[:, 0:2 * T]]
                acc_t = {0: None, 1: None}
                def habsfold(pr):
                    gb = pr % 2
                    t_ab = sch.op("s", lambda c, pr=pr, gb=gb: c.activation(
                        habs_ap(gb), hsil[:, 2 * pr * T:(2 * pr + 2) * T], AF.Abs),
                        waits=[silu_toks[2 * pr], silu_toks[2 * pr + 1], habs_free.get(pr - 2)])
                    lane = pr % 2
                    if acc_t[lane] is None:
                        t_fo = sch.op("v", lambda c, gb=gb, lane=lane: c.tensor_copy(
                            mAB[lane](), habs_ap(gb)),
                            waits=[t_ab, res2q_free[1] if lane == 0 else res2q_free[0]])
                    else:
                        t_fo = sch.op("v", lambda c, gb=gb, lane=lane: c.tensor_tensor(
                            mAB[lane](), mAB[lane](), habs_ap(gb), ALU.max),
                            waits=[t_ab, acc_t[lane]])
                    acc_t[lane] = t_fo
                    habs_free[pr] = t_fo
                for pr in range(CC // 2):
                    mmt = {}
                    for half in range(2):
                        ch = 2 * pr + half
                        bank = ps[ch % 4]
                        def mmw1(c, ch=ch, bank=bank):
                            r = None
                            for k in range(KC):
                                r = c.matmul(bank[:, 0:TH],
                                             kw1[:, k * DH + ch * 128:k * DH + (ch + 1) * 128],
                                             xq[:, k * TH:(k + 1) * TH],
                                             start=(k == 0), stop=(k == KC - 1))
                            return r
                        mmt[ch] = sch.op("t", mmw1, waits=[t_qx, conv_done.get(ch - 4)] +
                                         ([evm_prev[ch - 2]] if ch in (2, 3) else []))
                    cAt = {}
                    for half in range(2):
                        ch = 2 * pr + half
                        bank = ps[ch % 4]
                        pb = ch % 2
                        cAt[ch] = sch.op("s", lambda c, ch=ch, bank=bank, pb=pb: c.activation(
                            cAap(pb), bank[:, 2:2 + T], AF.Identity,
                            bias=cbc[:, ch:ch + 1], scale=cwc[:, ch * 3 + 2:ch * 3 + 3]),
                            waits=[mmt[ch], cB_prev.get(ch - 2)])
                    s1t = {}
                    for half in range(2):
                        ch = 2 * pr + half
                        bank = ps[ch % 4]
                        pb = ch % 2
                        s1t[ch] = sch.op("v", lambda c, ch=ch, bank=bank, pb=pb:
                            c.scalar_tensor_tensor(cBap(pb), bank[:, 0:T],
                                                   cwc[:, ch * 3 + 0:ch * 3 + 1], cAap(pb),
                                                   ALU.mult, ALU.add),
                            waits=[cAt[ch], sil_prev.get(ch - 2)])
                    s2t = {}
                    for half in range(2):
                        ch = 2 * pr + half
                        bank = ps[ch % 4]
                        pb = ch % 2
                        s2t[ch] = sch.op("v", lambda c, ch=ch, bank=bank, pb=pb:
                            c.scalar_tensor_tensor(cBap(pb), bank[:, 1:1 + T],
                                                   cwc[:, ch * 3 + 1:ch * 3 + 2], cBap(pb),
                                                   ALU.mult, ALU.add),
                            waits=[s1t[ch]])
                        conv_done[ch] = s2t[ch]
                        cB_prev[ch] = s2t[ch]
                    for half in range(2):
                        ch = 2 * pr + half
                        pb = ch % 2
                        t_sil = sch.op("s", lambda c, ch=ch, pb=pb: c.activation(
                            hsil[:, ch * T:(ch + 1) * T], cBap(pb), AF.Silu),
                            waits=[s2t[ch]] +
                                  ([hsilfree[0], hsilfree[1]] if ch <= 1 else []))
                        sil_prev[ch] = t_sil
                        silu_toks.append(t_sil)
                    wpull(1)
                    if pr >= 1:
                        habsfold(pr - 1)
                wpull(99)
                habsfold(CC // 2 - 1)
                xqfree[0] = conv_done[CC - 2]
                xqfree[1] = conv_done[CC - 1]
                xq_free_warm[0] = mmt[CC - 1]
                t_j = sch.op("v", lambda c: c.tensor_tensor(
                    mAB[0](), mAB[0](), mAB[1](), ALU.max), waits=[acc_t[0], acc_t[1]])
                t_fin = sch.op("v", lambda c: c.tensor_tensor(
                    mbuf[:, MB0:MB0 + T], mbuf[:, MB0:MB0 + T], mbuf[:, MB0 + T:MB0 + 2 * T],
                    ALU.max), waits=[t_j])
                stageB[("mf2", ti)] = t_fin
                fin_prev[0] = t_fin

            def b_s3(ti, wv):
                t_fin = stageB.pop(("mf2", ti))
                def warm(c):
                    r = None
                    for _ in range(26):
                        r = c.matmul(ps[2][:, 0:256], kw1[:, 0:128], xq[:, 0:256],
                                     start=True, stop=True)
                    return r
                sch.op("t", warm, waits=[a2_prev[0], xq_free_warm[0], xqfree[0], xqfree[1]])
                def trq2(c):
                    c.transpose(ps[6][:, 0:128], mbuf[:, MB0:MB0 + 128], ident[:, :])
                    return c.transpose(ps[6][:, 128:256], mbuf[:, MB0 + 128:MB0 + 256], ident[:, :])
                t_tq2 = sch.op("t", trq2, waits=[t_fin, hsilfree[0], hsilfree[1]])
                wv.pull()
                def r2a(c):
                    r = None
                    for j in range(NB):
                        r = c.reduce_max(stok[:, 12 + j:13 + j], ps[6][:, j * 128:(j + 1) * 128],
                                         axis=AX.X)
                    return r
                t_a = sch.op("v", r2a, waits=[t_tq2, stok2_free[0]])
                wv.pull()
                t_b = sch.op("v", lambda c: c.tensor_scalar(
                    stok[:, 12:12 + NB], stok[:, 12:12 + NB], 1e-5, None, ALU.max), waits=[t_a])
                wv.pull()
                t_c = sch.op("v", lambda c: c.reciprocal(stok[:, 16:16 + NB], stok[:, 12:12 + NB]),
                             waits=[t_b])
                wv.pull()
                t_d = sch.op("v", lambda c: c.tensor_scalar(
                    stok[:, 16:16 + NB], stok[:, 16:16 + NB], 127.0, None, ALU.mult), waits=[t_c])
                wv.pull()
                t_e = sch.op("v", lambda c: c.tensor_scalar(
                    stok[:, 20:20 + NB], stok[:, 12:12 + NB], 1.0 / 127.0, swcol[:, 2:3],
                    ALU.mult, ALU.mult), waits=[t_b])
                wv.pull()
                def btq2(c):
                    for j in range(NB):
                        c.transpose(ps[6][0:1, 256 + j * 128:256 + (j + 1) * 128],
                                    stok[:, 16 + j:17 + j], ident[:, :])
                    r = None
                    for j in range(NB):
                        r = c.transpose(ps[7][0:1, j * 128:(j + 1) * 128],
                                        stok[:, 20 + j:21 + j], ident[:, :])
                    return r
                t_bq2 = sch.op("t", btq2, waits=[t_d, t_e, a2_prev[0]])
                stok2_free[0] = t_bq2
                wv.pull()
                def rowsq2(c):
                    c.tensor_copy(srowq1[0:1, 0:T], ps[6][0:1, 256:256 + T])
                    return c.tensor_copy(srowq1[0:1, TH:TH + T], ps[7][0:1, 0:T])
                t_rowq2 = sch.op("v", rowsq2, waits=[t_bq2, srowq_free[0]])
                wv.pull()
                def bcq2(c):
                    c.matmul(ps[6][:, 0:T], ones1[0:1, :], srowq1[0:1, 0:T], start=True, stop=True)
                    return c.matmul(ps[6][:, 256:256 + T], ones1[0:1, :], srowq1[0:1, TH:TH + T],
                                    start=True, stop=True)
                t_bcq2 = sch.op("t", bcq2, waits=[t_rowq2])
                srowq_free[0] = t_bcq2
                wv.pull()
                t_e2c = sch.op("v", lambda c: c.tensor_copy(e2bc2[:, :], ps[6][:, 256:256 + T]),
                               waits=[t_bcq2, ln2row_tok[0]])
                stageB[("cpbc", ti)] = t_e2c
                stageB[("s2bc", ti)] = t_bcq2

            def b_s4(ti, wv, m1l, qxl, nxt):
                buf = ti % 2
                bufl = ti % 3
                b, si = divmod(ti, NSEQT)
                s0 = si * T
                t_cpbc = stageB.pop(("cpbc", ti))
                t_s2bc = stageB.pop(("s2bc", ti))
                k2_used = {}
                kc_tok = {}
                w2_prev = [None]
                zkm_tok = {}
                zki_tok = {}
                s2b = ps[6][:, 0:T].unsqueeze(1).broadcast_to([128, 2, T])
                zqA = lambda j: res2[:, 0:2 * T] if j == 0 else res2[:, MB0:MB0 + 2 * T]
                def filler(pr):
                    if wv.q:
                        wv.pull()
                    elif pr >= 5 and m1l:
                        m1l.pop(0)()
                for pr in range(CC // 2):
                    kb = pr % 2
                    def zkm(c, pr=pr, kb=kb):
                        h3 = hsil[:, 2 * pr * T:(2 * pr + 2) * T].rearrange("p (k t) -> p k t", k=2)
                        z3 = zqA(kb)[:, :].rearrange("p (k t) -> p k t", k=2)
                        return c.tensor_tensor(z3, h3, s2b, ALU.mult)
                    t_m = sch.op("v", zkm, waits=[t_s2bc, zki_tok.get(pr - 2)] +
                                 ([a2_prev[0], ln2row_tok[0]] if pr <= 1 else []))
                    zkm_tok[pr] = t_m
                    filler(pr)
                    def zki(c, pr=pr, kb=kb):
                        return c.tensor_scalar(k2i8[kb][:, 0:2 * T], zqA(kb)[:, 0:2 * T],
                                               MAGIC, -MAGIC, ALU.add, ALU.add)
                    t_i = sch.op("v", zki, waits=[t_m, kc_tok.get(pr - 2)])
                    zki_tok[pr] = t_i
                    filler(pr)
                    t_kc = sch.op("s", lambda c, kb=kb: c.activation(
                        k2bf[kb][:, :], k2i8[kb][:, 0:2 * T], AF.Identity),
                        waits=[t_i, k2_used.get(pr - 2)])
                    kc_tok[pr] = t_kc
                    def mmw2(c, pr=pr, kb=kb):
                        for half in range(2):
                            ch = 2 * pr + half
                            for m in range(KC):
                                c.matmul(ps[2 + m // 2][:, (m % 2) * 256:(m % 2) * 256 + T],
                                         kw2[:, ch * D + m * 128:ch * D + (m + 1) * 128],
                                         k2bf[kb][:, half * T:(half + 1) * T],
                                         start=(ch == 0 and m % 2 == 0),
                                         stop=(ch == CC - 1 and m % 2 == 1))
                        r = None
                        for half in range(2):
                            ch = 2 * pr + half
                            r = c.matmul(ps[7][0:1, 256:256 + T], mrow2[:, ch:ch + 1],
                                         k2bf[kb][:, half * T:(half + 1) * T],
                                         start=(ch == 0), stop=(ch == CC - 1))
                        return r
                    t_m2 = sch.op("t", mmw2, waits=[t_kc, w2_prev[0]])
                    w2_prev[0] = t_m2
                    k2_used[pr] = t_m2
                hsilfree[0] = zkm_tok[CC // 2 - 2]
                hsilfree[1] = zkm_tok[CC // 2 - 1]
                res2q_free[0] = zki_tok[CC // 2 - 2]
                res2q_free[1] = zki_tok[CC // 2 - 1]
                wv.drain()
                while m1l:
                    m1l.pop(0)()
                while qxl:
                    qxl.pop(0)()
                if nxt is not None:
                    def warm2(c):
                        r = None
                        for _ in range(22):
                            r = c.matmul(ps[0][:, 384:512], ident[:, :], ident[:, :],
                                         start=True, stop=True)
                        return r
                    sch.op("t", warm2, waits=[w2_prev[0], bsq_tok.get(nxt)])

                ev_m = {}
                ev_a = {}
                def mk_evm(mp):
                    def f(c, mp=mp):
                        pr3 = ps[2 + mp][:, 0:2 * T].rearrange("p (k t) -> p k t", k=2)
                        r3 = outb[:, 2 * mp * T:(2 * mp + 2) * T].rearrange("p (k t) -> p k t", k=2)
                        e3 = e2bc2[:, :].unsqueeze(1).broadcast_to([128, 2, T])
                        return c.tensor_tensor(r3, pr3, e3, ALU.mult)
                    return f
                def mk_eva(mp):
                    def f(c, mp=mp, bufl=bufl):
                        r3 = outb[:, 2 * mp * T:(2 * mp + 2) * T].rearrange("p (k t) -> p k t", k=2)
                        x3 = x1ld[bufl][:, :].rearrange("p (k t) -> p k t", k=KC)[:, 2 * mp:2 * mp + 2, 1:1 + T]
                        return c.tensor_tensor(r3, r3, x3, ALU.add)
                    return f
                ev_m[0] = sch.op("v", mk_evm(0), waits=[w2_prev[0], t_cpbc, out_prev[0]])
                ev_m[1] = sch.op("v", mk_evm(1), waits=[w2_prev[0], t_cpbc, out_prev[0]])
                ev_a[0] = sch.op("v", mk_eva(0), waits=[ev_m[0]])
                ev_m[2] = sch.op("v", mk_evm(2), waits=[w2_prev[0], t_cpbc, out_prev[0]])
                ev_a[1] = sch.op("v", mk_eva(1), waits=[ev_m[1], ev_a[0]])
                ev_m[3] = sch.op("v", mk_evm(3), waits=[w2_prev[0], t_cpbc, out_prev[0]])
                ev_a[2] = sch.op("v", mk_eva(2), waits=[ev_m[2], ev_a[1]])
                ev_a[3] = sch.op("v", mk_eva(3), waits=[ev_m[3], ev_a[2]])
                evm_prev[0] = ev_m[0]
                evm_prev[1] = ev_m[1]
                x1_free[bufl] = ev_a[3]
                t_rsq2 = sch.op("s", lambda c: c.activation(sqb[:, :], outb[:, :], AF.Square),
                                waits=[ev_a[3]])

                tl = {}
                def w_st2():
                    def st2(c):
                        r = None
                        for k in range(KC):
                            r = c.matmul(ps[7][0:1, 0:T], onescol[:, 0:1],
                                         sqb[:, k * T:(k + 1) * T],
                                         start=(k == 0), stop=(k == KC - 1))
                        return r
                    tl["st2"] = sch.op("t", st2, waits=[t_rsq2])
                def w_rw1():
                    def rw1(c):
                        c.tensor_copy(srowm[0:1, 0:T], ps[7][0:1, 256:256 + T])
                        return c.tensor_copy(srowm[0:1, TH:TH + T], ps[7][0:1, 0:T])
                    tl["rw1"] = sch.op("v", rw1, waits=[tl["st2"]])
                def w_rw2():
                    tl["rw2"] = sch.op("v", lambda c: c.tensor_tensor(
                        srowm[0:1, 0:T], srowm[0:1, 0:T], e2bc2[0:1, 0:T], ALU.mult),
                        waits=[tl["rw1"]])
                    ln2row_tok[0] = tl["rw2"]
                def w_rm1():
                    def rm1(c):
                        c.tensor_scalar(srowm[0:1, 0:T], srowm[0:1, 0:T], INV_D, None, ALU.mult)
                        return c.tensor_scalar(srowm[0:1, TH:TH + T], srowm[0:1, TH:TH + T],
                                               INV_D, None, ALU.mult)
                    tl["rm1"] = sch.op("v", rm1, waits=[tl["rw2"]])
                def w_rm2():
                    tl["rm2"] = sch.op("v", lambda c: c.tensor_tensor(
                        rtmp[0:1, 0:T], srowm[0:1, 0:T], srowm[0:1, 0:T], ALU.mult),
                        waits=[tl["rm1"]])
                def w_rm3():
                    tl["rm3"] = sch.op("v", lambda c: c.scalar_tensor_tensor(
                        srowm[0:1, TH:TH + T], srowm[0:1, TH:TH + T], EPS, rtmp[0:1, 0:T],
                        ALU.add, ALU.subtract), waits=[tl["rm2"]])
                def w_rm4():
                    tl["rm4"] = sch.op("v", lambda c: c.reciprocal(
                        rtmp[0:1, 0:T], srowm[0:1, TH:TH + T]), waits=[tl["rm3"]])
                def w_sq():
                    tl["sq"] = sch.op("s", lambda c: c.activation(
                        srowm[0:1, TH:TH + T], rtmp[0:1, 0:T], AF.Sqrt), waits=[tl["rm4"]])
                def w_nm():
                    tl["nm"] = sch.op("v", lambda c: c.scalar_tensor_tensor(
                        srowm[0:1, 0:T], srowm[0:1, 0:T], -1.0, srowm[0:1, TH:TH + T],
                        ALU.mult, ALU.mult), waits=[tl["sq"]])
                def w_bc():
                    def bcast(c):
                        c.matmul(ps[7][:, 0:T], ones1[0:1, :], srowm[0:1, TH:TH + T],
                                 start=True, stop=True)
                        return c.matmul(ps[7][:, 256:256 + T], ones1[0:1, :], srowm[0:1, 0:T],
                                        start=True, stop=True)
                    tl["bc"] = sch.op("t", bcast, waits=[tl["nm"]])
                def w_a2m():
                    def a2m(c):
                        abc = ps[7][:, 0:T].unsqueeze(1).broadcast_to([128, KC, T])
                        r3 = outb[:, :].rearrange("p (k t) -> p k t", k=KC)
                        return c.tensor_tensor(r3, r3, abc, ALU.mult)
                    tl["a2m"] = sch.op("v", a2m, waits=[tl["bc"]])
                def w_a2a():
                    def a2a(c):
                        bbc = ps[7][:, 256:256 + T].unsqueeze(1).broadcast_to([128, KC, T])
                        r3 = outb[:, :].rearrange("p (k t) -> p k t", k=KC)
                        return c.tensor_tensor(r3, r3, bbc, ALU.add)
                    tl["a2a"] = sch.op("v", a2a, waits=[tl["a2m"]])
                    a2_prev[0] = tl["a2a"]
                def w_sto():
                    def sto(e, b=b, s0=s0):
                        srcap = outb[:, :].rearrange("p (k t) -> p k t", k=KC)
                        dst = out_h[b, :, s0:s0 + T].rearrange("(k p) t -> p k t", p=128)
                        return e.dma_start(out=dst, in_=srcap)
                    out_prev[0] = sch.op("go0", sto, n_dma=1, waits=[tl["a2a"]])
                return [w_st2, w_rw1, w_rw2, w_rm1, w_rm2, w_rm3, w_rm4, w_sq, w_nm,
                        w_bc, w_a2m, w_a2a, w_sto]

            b_load(0)
            b_load(1)
            b_load(2)
            wv0 = Weaver()
            add_fold_emitters(wv0, 0)
            wv0.drain()
            emit_trq1(0)
            for f in r1_emitters(0):
                f()
            emit_btq1(0)
            emit_rowsq1(0)
            emit_bcq1(0)
            for f in qx_emitters(0):
                f()
            b_s2(0)
            for ti in range(ntiles_b):
                nxt = ti + 1 if ti + 1 < ntiles_b else None
                wv = Weaver()
                if nxt is not None:
                    add_fold_emitters(wv, nxt)
                m1l = ([(lambda n=nxt: emit_trq1(n))] + r1_emitters(nxt) +
                       [(lambda n=nxt: emit_btq1(n)), (lambda n=nxt: emit_rowsq1(n)),
                        (lambda n=nxt: emit_bcq1(n))]) if nxt is not None else []
                qxl = qx_emitters(nxt) if nxt is not None else []
                b_s3(ti, wv)
                tail = b_s4(ti, wv, m1l, qxl, nxt)
                if nxt is not None:
                    b_s2(nxt, weave=tail)
                else:
                    for f in tail:
                        f()
                b_load(ti + 3)
            final_toks = [out_prev[0]]

        else:
            def dbg(e):
                rs = []
                for b in range(BPC):
                    for c0 in range(KC):
                        rs.append(e.dma_start(out=out_h[b, c0 * 128:(c0 + 1) * 128, :],
                                              in_=x1s_h[b, c0 * 128:(c0 + 1) * 128, :]))
                return rs
            final_toks = [sch.op("qz", dbg, n_dma=BPC * KC, waits=pa_done)]

        sch.op("g", lambda c: c.engine_nop(), waits=final_toks)

        secfns = {"tensor": blk.tensor, "vector": blk.vector, "scalar": blk.scalar,
                  "gpsimd": blk.gpsimd, "sync": blk.sync}
        sem2sec = {"t": "tensor", "v": "vector", "s": "scalar", "g": "gpsimd",
                   "qp": "sync", "qx0": "sync", "qx1": "sync", "qx2": "sync", "qr0": "sync", "qr2": "sync",
                   "qr1": "sync", "pw0": "sync", "pw1": "sync", "pw2": "sync", "pw3": "sync", "qz": "sync",
                   "qw": "gpsimd", "gx0": "gpsimd", "gx1": "gpsimd",
                   "go0": "gpsimd", "go1": "gpsimd"}
        sec_ops = {}
        order = []
        for entry in sch.ops:
            secname = sem2sec[entry[1]]
            if secname not in sec_ops:
                sec_ops[secname] = (secfns[secname], [])
                order.append(secname)
            sec_ops[secname][1].append(entry)
        import os
        if os.environ.get("MK_DEBUG_CNT"):
            print("SCHED CNT:", {k: v for k, v in sch.cnt.items() if v})
            from collections import Counter
            print("OPS PER SEM:", Counter(e[1] for e in sch.ops))
        for sid in order:
            sec, entries = sec_ops[sid]
            def run(eng, entries=entries):
                for (kind, e, emitfn, narg, waits, base) in entries:
                    for (ws, wn) in waits:
                        eng.wait_ge(sch.sems[ws], wn)
                    if kind == "dma":
                        ins = emitfn(eng)
                        if not isinstance(ins, (list, tuple)):
                            ins = [ins]
                        assert len(ins) == narg, f"dma {e}: {len(ins)} != {narg}"
                        for i_ in ins:
                            i_.then_inc(sch.sems[e], 16)
                    else:
                        ctx = Ctx(eng, e, base, sch.sems)
                        last = emitfn(ctx)
                        assert ctx.nbar == narg, f"{e}: {ctx.nbar} barriers != {narg}"
                        assert last is not None, f"op on {e} returned None"
                        last.then_inc(sch.sems[e], 1)
            sec(run)
    return nc


_NC_CACHE = {}


def kernel(**inputs):
    from concourse.bass_utils import run_bass_kernel_spmd
    if "nc" not in _NC_CACHE:
        _NC_CACHE["nc"] = build_nc()
    nc = _NC_CACHE["nc"]
    in_maps = [host_prepare(inputs, core) for core in range(8)]
    res = run_bass_kernel_spmd(nc, in_maps, core_ids=list(range(8)))
    out = np.concatenate([np.asarray(r["out"]) for r in res.results], axis=0)
    return np.ascontiguousarray(out.transpose(0, 2, 1)).astype(np.float32)


# version 33 (final)
# speedup vs baseline: 1.0133x; 1.0070x over previous
import contextlib
import numpy as np
import concourse.bass as bass
import concourse.mybir as mybir

F32 = mybir.dt.float32
BF16 = mybir.dt.bfloat16
I8 = mybir.dt.int8
F8 = mybir.dt.float8e4
AF = mybir.ActivationFunctionType
ALU = mybir.AluOpType
AX = mybir.AxisListType

D = 1024
DH = 4096
S = 2048
BPC = 4
T = 256
NSEQT = S // T
NTILES = BPC * NSEQT
KC = D // 128
CC = DH // 128
NB = T // 128
TH = T + 2
EPS = 1e-5
INV_D = 1.0 / D
MAGIC = 12582912.0


class Ctx:

    def __init__(self, eng, sem, base, sems_h):
        self._eng = eng
        self._semname = sem
        self._val = base
        self._sems_h = sems_h
        self._last = None
        self.nbar = 0

    def __getattr__(self, name):
        fn = getattr(self._eng, name)
        if callable(fn):
            def wrap(*a, **k):
                ins = fn(*a, **k)
                self._last = ins
                return ins
            return wrap
        return fn

    def barrier(self):
        assert self._last is not None, "barrier with no prior instruction"
        self._val += 1
        self.nbar += 1
        self._last.then_inc(self._sems_h[self._semname], 1)
        self._eng.wait_ge(self._sems_h[self._semname], self._val)
        self._last = None


class Sched:
    def __init__(self, sems):
        self.ops = []
        self.cnt = {s: 0 for s in sems}
        self.sems = sems

    def op(self, eng, emitfn, waits=(), barriers=0, n_dma=0):
        waits = tuple(w for w in waits if w is not None)
        if n_dma:
            self.cnt[eng] += 16 * n_dma
            tok = (eng, self.cnt[eng])
            self.ops.append(("dma", eng, emitfn, n_dma, waits, None))
            return tok
        base = self.cnt[eng]
        self.cnt[eng] += barriers + 1
        tok = (eng, self.cnt[eng])
        self.ops.append(("cmp", eng, emitfn, barriers, waits, base))
        return tok


def host_prepare(inputs, core):
    x = np.asarray(inputs["L0_attn_x"])
    xT = np.ascontiguousarray(np.transpose(x[core * BPC:(core + 1) * BPC], (0, 2, 1)))
    cT = lambda a: np.ascontiguousarray(np.asarray(a).T).astype(np.float32)
    col = lambda a, n: np.ascontiguousarray(np.asarray(a).astype(np.float32).reshape(n, 128).T)
    return {
        "x": xT.astype(np.float32),
        "wq": cT(inputs["L0_attn_Wq"]), "wv": cT(inputs["L0_attn_Wv"]),
        "wg": cT(inputs["L0_attn_Wg"]), "wo": cT(inputs["L0_attn_Wo"]),
        "w1t": cT(inputs["L0_ffn_w1"]), "w2t": cT(inputs["L0_ffn_w2"]),
        "beta_col": col(np.repeat(np.asarray(inputs["L0_attn_beta"]).astype(np.float32), 64), KC),
        "lng_col": col(inputs["L0_attn_ln_gamma"], KC),
        "lnb_col": col(inputs["L0_attn_ln_beta"], KC),
        "cw_col": np.ascontiguousarray(
            np.asarray(inputs["L0_ffn_conv_w"]).astype(np.float32).reshape(CC, 128, 3)
            .transpose(1, 0, 2).reshape(128, CC * 3)),
        "cb_col": col(inputs["L0_ffn_conv_b"], CC),
        "ident": np.eye(128, dtype=np.float32),
        "ones1row": np.ones((1, 128), dtype=np.float32),
        "onescol_bf": np.ones((128, 1), dtype=np.float32),
        "onescol_f": np.ones((128, 1), dtype=np.float32),
        "onesT_h": np.ones((128, T), dtype=np.float32),
        "zpad_h": np.zeros((BPC, D, 2), dtype=np.float32),
        "zcol_h": np.zeros((128, KC), dtype=np.float32),
    }


def build_nc(phases="AB", ntiles_a=NTILES, ntiles_b=NTILES):
    nc = bass.Bass("TRN2", target_bir_lowering=False)
    x_h = nc.dram_tensor("x", [BPC, D, S], F32, kind="ExternalInput").ap()
    w_h = {nm: nc.dram_tensor(nm, [D, D], F32, kind="ExternalInput").ap()
           for nm in ("wq", "wv", "wg", "wo")}
    w1t_h = nc.dram_tensor("w1t", [D, DH], F32, kind="ExternalInput").ap()
    w2t_h = nc.dram_tensor("w2t", [DH, D], F32, kind="ExternalInput").ap()
    beta_h = nc.dram_tensor("beta_col", [128, KC], F32, kind="ExternalInput").ap()
    lng_h = nc.dram_tensor("lng_col", [128, KC], F32, kind="ExternalInput").ap()
    lnb_h = nc.dram_tensor("lnb_col", [128, KC], F32, kind="ExternalInput").ap()
    cw_h = nc.dram_tensor("cw_col", [128, CC * 3], F32, kind="ExternalInput").ap()
    cb_h = nc.dram_tensor("cb_col", [128, CC], F32, kind="ExternalInput").ap()
    id_h = nc.dram_tensor("ident", [128, 128], F32, kind="ExternalInput").ap()
    ones1_h = nc.dram_tensor("ones1row", [1, 128], F32, kind="ExternalInput").ap()
    onescol_h = nc.dram_tensor("onescol_bf", [128, 1], F32, kind="ExternalInput").ap()
    onescolf_h = nc.dram_tensor("onescol_f", [128, 1], F32, kind="ExternalInput").ap()
    onesT_hh = nc.dram_tensor("onesT_h", [128, T], F32, kind="ExternalInput").ap()
    zpad_hh = nc.dram_tensor("zpad_h", [BPC, D, 2], F32, kind="ExternalInput").ap()
    zcol_hh = nc.dram_tensor("zcol_h", [128, KC], F32, kind="ExternalInput").ap()
    out_h = nc.dram_tensor("out", [BPC, D, S], F32, kind="ExternalOutput").ap()
    x1s_h = nc.dram_tensor("x1s", [BPC, D, S], F32, kind="Internal").ap()

    es = contextlib.ExitStack()
    with es:
        def sb(name, shape, dt, stack=None):
            return (stack or es).enter_context(nc.sbuf_tensor(name, shape, dt))
        ident = sb("ident_sb", [128, 128], F32)
        ones1 = sb("ones1_sb", [1, 128], F32)
        onescol = sb("onescol_sb", [128, 1], BF16)
        lamcol = sb("lamcol", [128, KC], F32)
        betac = sb("betac", [128, KC], F32)
        lngc = sb("lngc", [128, KC], F32)
        lnbc = sb("lnbc", [128, KC], F32)
        cwc = sb("cwc", [128, CC * 3], F32)
        cbc = sb("cbc", [128, CC], F32)
        carry = sb("carry", [128, KC], F32)
        mrow_o = sb("mrow_o", [128, KC], BF16)
        mrow_f = sb("mrow_f", [128, CC], F32)
        mrow2 = sb("mrow2", [128, CC], BF16)
        srow = sb("srow", [1, 2 * TH], F32)
        stok = sb("stok", [128, 24], F32)
        stokm = sb("stokm", [128, 12], F32)
        srowm = sb("srowm", [1, 2 * TH], F32)
        srowmB = sb("srowmB", [1, 2 * TH], F32)
        rtmp = sb("rtmp", [1, T], F32)
        rtmpB = sb("rtmpB", [1, T], F32)
        sqb = sb("sqb", [128, KC * T], BF16)
        swcol = sb("swcol", [128, 4], F32)
        zsb = sb("zsb", [128, KC], F32)
        onescolf = sb("onescolf", [128, 1], F32)
        tinyrow = sb("tinyrow", [1, 8], F32)
        wtmp1 = [sb(f"wtmp1_{i}", [128, 1024], F32) for i in range(2)]
        wacc1 = sb("wacc1", [128, 2], F32)
        wsum1 = sb("wsum1", [128, 2], F32)

        semnames = ["t", "v", "s", "g", "qp", "qw", "qx0", "qx1", "qx2", "qr2", "gx0", "gx1",
                    "qr0", "qr1", "go0", "go1", "qz", "pw0", "pw1", "pw2", "pw3"]
        sems = {n: es.enter_context(nc.semaphore("sem_" + n)) for n in semnames}
        ps = [es.enter_context(nc.psum_tensor(f"ps{i}", [128, 512], F32)) for i in range(8)]
        blk = es.enter_context(nc.Block())
        sch = Sched(sems)

        def p0(e):
            return [e.dma_start(out=ident[:, :], in_=id_h[:, :]),
                    e.dma_start(out=ones1[:, :], in_=ones1_h[:, :]),
                    e.dma_start(out=betac[:, :], in_=beta_h[:, :]),
                    e.dma_start(out=lngc[:, :], in_=lng_h[:, :]),
                    e.dma_start(out=lnbc[:, :], in_=lnb_h[:, :]),
                    e.dma_start(out=cwc[:, :], in_=cw_h[:, :]),
                    e.dma_start(out=cbc[:, :], in_=cb_h[:, :]),
                    e.dma_start(out=carry[:, :], in_=zcol_hh[:, :]),
                    e.dma_start(out=zsb[:, :], in_=zcol_hh[:, :]),
                    e.dma_start(out=onescolf[:, :], in_=onescolf_h[:, :])]
        t_p0 = sch.op("qp", p0, n_dma=10)
        t_onescol = sch.op("qw", lambda e: e.dma_start(out=onescol[:, :], in_=onescol_h[:, :]),
                           n_dma=1)
        t_lam = sch.op("s", lambda c: c.activation(lamcol[:, :], betac[:, :], AF.Sigmoid),
                       waits=[t_p0])

        scopeA = contextlib.ExitStack()
        wsb = {nm: sb(nm + "_sb", [128, KC * D], BF16, scopeA) for nm in ("wq", "wv", "wg", "wo")}
        lamtile = sb("lamtile", [128, KC * T], F32, scopeA)
        onesT = sb("onesT", [128, T], F32, scopeA)
        t_onesT = sch.op("qp", lambda e: e.dma_start(out=onesT[:, :], in_=onesT_hh[:, :]),
                         n_dma=1, waits=[t_p0])

        def mk_lamtile(c):
            r = None
            for k in range(KC):
                r = c.tensor_scalar(lamtile[:, k * T:(k + 1) * T], onesT[:, :],
                                    lamcol[:, k:k + 1], None, ALU.mult)
            return r
        t_lamtile = sch.op("v", mk_lamtile, waits=[t_lam, t_onesT])

        def wload(e):
            out = []
            for nm in ("wq", "wv", "wg", "wo"):
                src = w_h[nm][:, :].rearrange("(k p) m -> p k m", p=128)
                dst = wsb[nm][:, :].rearrange("p (k m) -> p k m", k=KC)
                out.append(e.dma_start(out=dst, in_=src))
            return out
        t_w = sch.op("qw", wload, n_dma=4, waits=[t_onescol])

        def mk_mrow(c):
            for k in range(KC):
                c.reduce_sum(mrow_f[:, k:k + 1], wsb["wo"][:, k * D:(k + 1) * D], axis=AX.X)
            c.barrier()
            return c.tensor_copy(mrow_o[:, :], mrow_f[:, 0:KC])
        t_mrow = sch.op("v", mk_mrow, barriers=1, waits=[t_w])

        def ln_machinery(t_statrow, btr, bbc):
            def rowmath(c):
                c.tensor_scalar(srowm[0:1, 0:T], srowm[0:1, 0:T], INV_D, None, ALU.mult)
                c.tensor_scalar(srowm[0:1, TH:TH + T], srowm[0:1, TH:TH + T], INV_D, None, ALU.mult)
                c.barrier()
                c.tensor_tensor(srowm[0:1, TH + T:TH + T + T] if False else rtmp[0:1, 0:T],
                                srowm[0:1, 0:T], srowm[0:1, 0:T], ALU.mult)
                c.barrier()
                c.tensor_tensor(srowm[0:1, TH:TH + T], srowm[0:1, TH:TH + T], rtmp[0:1, 0:T],
                                ALU.subtract)
                c.barrier()
                c.tensor_scalar(srowm[0:1, TH:TH + T], srowm[0:1, TH:TH + T], EPS, None, ALU.add)
                c.barrier()
                return c.reciprocal(rtmp[0:1, 0:T], srowm[0:1, TH:TH + T])
            t_rm = sch.op("v", rowmath, barriers=4, waits=[t_statrow])
            t_sq = sch.op("s", lambda c: c.activation(srowm[0:1, TH:TH + T], rtmp[0:1, 0:T],
                                                      AF.Sqrt), waits=[t_rm])
            def negmu(c):
                c.tensor_tensor(srowm[0:1, 0:T], srowm[0:1, 0:T], srowm[0:1, TH:TH + T], ALU.mult)
                c.barrier()
                return c.tensor_scalar(srowm[0:1, 0:T], srowm[0:1, 0:T], -1.0, None, ALU.mult)
            t_nm = sch.op("v", negmu, barriers=1, waits=[t_sq])

            def bcast(c):
                c.matmul(ps[bbc][:, 0:T], ones1[0:1, :], srowm[0:1, TH:TH + T],
                         start=True, stop=True)
                return c.matmul(ps[bbc][:, 256:256 + T], ones1[0:1, :], srowm[0:1, 0:T],
                                start=True, stop=True)
            return sch.op("t", bcast, waits=[t_nm])

        xt = [sb(f"xt{i}", [128, KC * T], F32, scopeA) for i in range(3)]
        xb = [sb(f"xb{i}", [128, KC * T], BF16, scopeA) for i in range(2)]
        qbf = [sb(f"qbf{i}", [128, KC * T], BF16, scopeA) for i in range(2)]
        vfull = [sb(f"vfull{i}", [128, KC * T], F32, scopeA) for i in range(2)]
        state = [sb(f"state{i}", [128, KC * T], F32, scopeA) for i in range(2)]
        gate = [sb(f"gate{i}", [128, KC * T], BF16, scopeA) for i in range(2)]
        ybf = sb("ybf", [128, KC * T], BF16, scopeA)
        z1 = sb("z1", [128, KC * T], F32, scopeA)
        y1b = sb("y1b", [128, KC * T], BF16, scopeA)
        x1t = [sb(f"x1t{i}", [128, KC * T], F32, scopeA) for i in range(2)]

        xt_free = [None, None, None]
        xb_free = [None, None]
        x1_dma = [None, None]
        prev_scan = [None]
        bankA = {}
        bankWo = {}
        sty_tok = [None]
        st1_tok = [None]
        y1_par = [None, None]
        a1_prev = [None]
        row1_prev = [None]
        pend_tail = [[]]
        aft = {}

        def a_ld(ti):
            if ti >= ntiles_a:
                return
            buf3 = ti % 3
            qx = f"qx{ti % 3}"
            b, si = divmod(ti, NSEQT)
            s0 = si * T
            def ld(e, b=b, s0=s0, buf3=buf3):
                srcap = x_h[b, :, s0:s0 + T].rearrange("(k p) t -> p k t", p=128)
                dst = xt[buf3][:, :].rearrange("p (k t) -> p k t", k=KC)
                return e.dma_start(out=dst, in_=srcap)
            aft[("x", ti)] = sch.op(qx, ld, n_dma=1, waits=[xt_free[buf3]])

        def front_emitters(ti):
            if ti >= ntiles_a:
                return []
            buf = ti % 2
            fs = {}
            aft[ti] = fs
            out = []
            buf3 = ti % 3
            def e_xb():
                t_x = aft.pop(("x", ti))
                fs["xb"] = sch.op("s", lambda c, buf=buf, buf3=buf3: c.activation(
                    xb[buf][:, :], xt[buf3][:, :], AF.Identity),
                    waits=[t_x, xb_free[buf]])
            out.append(e_xb)
            evq, evv, evg = [], [], []
            for wname, dstl, evl in (("wq", qbf, evq), ("wv", vfull, evv), ("wg", gate, evg)):
                for mp in range(4):
                    def e_mm(wname=wname, mp=mp):
                        bank = ps[mp]
                        def mm(c, wname=wname, mp=mp, bank=bank, buf=buf):
                            r = None
                            for k in range(KC):
                                for half in range(2):
                                    m = 2 * mp + half
                                    r = c.matmul(bank[:, half * 256:half * 256 + T],
                                                 wsb[wname][:, k * D + m * 128:k * D + (m + 1) * 128],
                                                 xb[buf][:, k * T:(k + 1) * T],
                                                 start=(k == 0 and half == 0),
                                                 stop=(k == KC - 1 and half == 1))
                            return r
                        fs[("mm", wname, mp)] = sch.op("t", mm,
                                                       waits=[fs["xb"], t_w, bankA.get(mp)])
                    def e_ev(wname=wname, mp=mp, dstl=dstl, evl=evl):
                        bank = ps[mp]
                        if wname == "wg":
                            emitf = lambda c, dstl=dstl, mp=mp, bank=bank, buf=buf: c.activation(
                                dstl[buf][:, 2 * mp * T:(2 * mp + 2) * T], bank[:, 0:2 * T],
                                AF.Silu)
                            w2_ = [y1_par[buf]]
                        else:
                            emitf = lambda c, dstl=dstl, mp=mp, bank=bank, buf=buf: c.activation(
                                dstl[buf][:, 2 * mp * T:(2 * mp + 2) * T], bank[:, 0:2 * T],
                                AF.Identity)
                            w2_ = []
                        t = sch.op("s", emitf, waits=[fs[("mm", wname, mp)]] + w2_)
                        evl.append(t)
                        bankA[mp] = t
                    out += [e_mm, e_ev]
            def e_scan():
                b, si = divmod(ti, NSEQT)
                def scans(c, si=si, buf=buf):
                    if si == 0:
                        c.tensor_scalar(carry[:, :], carry[:, :], 0.0, None, ALU.mult)
                        c.barrier()
                    for k in range(KC):
                        c.tensor_tensor_scan(state[buf][:, k * T:(k + 1) * T],
                                             lamtile[:, k * T:(k + 1) * T],
                                             vfull[buf][:, k * T:(k + 1) * T],
                                             carry[:, k:k + 1], ALU.mult, ALU.add)
                    c.barrier()
                    r = None
                    for k in range(KC):
                        r = c.tensor_copy(carry[:, k:k + 1],
                                          state[buf][:, (k + 1) * T - 1:(k + 1) * T])
                    return r
                fs["scan"] = sch.op("v", scans, barriers=(2 if si == 0 else 1),
                                    waits=evv + [t_lamtile, prev_scan[0], t_p0])
                prev_scan[0] = fs["scan"]
            out.append(e_scan)
            fs["evq"] = evq
            fs["evg"] = evg
            return out

        def back_emitters(ti):
            buf = ti % 2
            buf3 = ti % 3
            b, si = divmod(ti, NSEQT)
            s0 = si * T
            ft = aft.pop(ti)
            bt = {}
            def w_ty():
                bt["ty"] = sch.op("v", lambda c, buf=buf: c.tensor_tensor(
                    ybf[:, :], qbf[buf][:, :], state[buf][:, :], ALU.mult),
                    waits=[ft["scan"], sty_tok[0]] + ft["evq"])
            def w_ysq():
                bt["ysq"] = sch.op("s", lambda c: c.activation(sqb[:, :], ybf[:, :], AF.Square),
                                   waits=[bt["ty"], st1_tok[0]])
            def w_statsy():
                def stats_y(c):
                    for k in range(KC):
                        c.matmul(ps[7][0:1, 0:T], onescol[:, 0:1], ybf[:, k * T:(k + 1) * T],
                                 start=(k == 0), stop=(k == KC - 1))
                    r = None
                    for k in range(KC):
                        r = c.matmul(ps[7][0:1, 256:256 + T], onescol[:, 0:1],
                                     sqb[:, k * T:(k + 1) * T],
                                     start=(k == 0), stop=(k == KC - 1))
                    return r
                bt["sty"] = sch.op("t", stats_y,
                                   waits=[bt["ysq"], t_onescol, bt["ty"], row1_prev[0]])
                sty_tok[0] = bt["sty"]
            def w_rowy():
                def rowy(c):
                    c.tensor_copy(srowm[0:1, 0:T], ps[7][0:1, 0:T])
                    return c.tensor_copy(srowm[0:1, TH:TH + T], ps[7][0:1, 256:256 + T])
                bt["rowy"] = sch.op("v", rowy, waits=[bt["sty"]])
            def mk_rm(px, prev_key, srow, rt):
                def w_rm1():
                    def rm1(c):
                        c.tensor_scalar(srow[0:1, 0:T], srow[0:1, 0:T], INV_D, None, ALU.mult)
                        return c.tensor_scalar(srow[0:1, TH:TH + T], srow[0:1, TH:TH + T],
                                               INV_D, None, ALU.mult)
                    bt[px + "rm1"] = sch.op("v", rm1, waits=[bt[prev_key]])
                def w_rm2():
                    bt[px + "rm2"] = sch.op("v", lambda c: c.tensor_tensor(
                        rt[0:1, 0:T], srow[0:1, 0:T], srow[0:1, 0:T], ALU.mult),
                        waits=[bt[px + "rm1"]])
                def w_rm3():
                    bt[px + "rm3"] = sch.op("v", lambda c: c.scalar_tensor_tensor(
                        srow[0:1, TH:TH + T], srow[0:1, TH:TH + T], EPS, rt[0:1, 0:T],
                        ALU.add, ALU.subtract), waits=[bt[px + "rm2"]])
                def w_rm4():
                    bt[px + "rm4"] = sch.op("v", lambda c: c.reciprocal(
                        rt[0:1, 0:T], srow[0:1, TH:TH + T]), waits=[bt[px + "rm3"]])
                def w_sq():
                    bt[px + "sq"] = sch.op("s", lambda c: c.activation(
                        srow[0:1, TH:TH + T], rt[0:1, 0:T], AF.Sqrt),
                        waits=[bt[px + "rm4"]])
                def w_nm():
                    bt[px + "nm"] = sch.op("v", lambda c: c.scalar_tensor_tensor(
                        srow[0:1, 0:T], srow[0:1, 0:T], -1.0, srow[0:1, TH:TH + T],
                        ALU.mult, ALU.mult), waits=[bt[px + "sq"]])
                return [w_rm1, w_rm2, w_rm3, w_rm4, w_sq, w_nm]
            def mk_bc(px, srow, extra=()):
                def w_bc():
                    def bcast(c):
                        c.matmul(ps[6][:, 0:T], ones1[0:1, :], srow[0:1, TH:TH + T],
                                 start=True, stop=True)
                        return c.matmul(ps[6][:, 256:256 + T], ones1[0:1, :], srow[0:1, 0:T],
                                        start=True, stop=True)
                    bt[px + "bc"] = sch.op("t", bcast,
                                           waits=[bt[px + "nm"]] + [x[0] for x in extra])
                return w_bc
            def w_apym():
                def apym(c):
                    abc = ps[6][:, 0:T].unsqueeze(1).broadcast_to([128, KC, T])
                    y3 = ybf[:, :].rearrange("p (k t) -> p k t", k=KC)
                    z3 = z1[:, :].rearrange("p (k t) -> p k t", k=KC)
                    return c.tensor_tensor(z3, y3, abc, ALU.mult)
                bt["apym"] = sch.op("v", apym, waits=[bt["ybc"]])
            def w_apya():
                def apya(c):
                    bbc = ps[6][:, 256:256 + T].unsqueeze(1).broadcast_to([128, KC, T])
                    z3 = z1[:, :].rearrange("p (k t) -> p k t", k=KC)
                    return c.tensor_tensor(z3, z3, bbc, ALU.add)
                bt["apya"] = sch.op("v", apya, waits=[bt["apym"]])
            def w_gb():
                def gb(c):
                    r = None
                    for k in range(KC):
                        r = c.tensor_scalar(z1[:, k * T:(k + 1) * T], z1[:, k * T:(k + 1) * T],
                                            lngc[:, k:k + 1], lnbc[:, k:k + 1], ALU.mult, ALU.add)
                    return r
                bt["gb"] = sch.op("v", gb, waits=[bt["apya"]])
            def w_y1():
                bt["y1"] = sch.op("v", lambda c, buf=buf: c.tensor_tensor(
                    y1b[:, :], z1[:, :], gate[buf][:, :], ALU.mult),
                    waits=[bt["gb"]] + ft["evg"])
                y1_par[buf] = bt["y1"]
            def mk_wo(mp):
                def w_wo(mp=mp):
                    bank = ps[4 + (mp % 2)]
                    def mm(c, mp=mp, bank=bank):
                        r = None
                        for k in range(KC):
                            for half in range(2):
                                m = 2 * mp + half
                                r = c.matmul(bank[:, half * 256:half * 256 + T],
                                             wsb["wo"][:, k * D + m * 128:k * D + (m + 1) * 128],
                                             y1b[:, k * T:(k + 1) * T],
                                             start=(k == 0 and half == 0),
                                             stop=(k == KC - 1 and half == 1))
                        return r
                    bt[("wo", mp)] = sch.op("t", mm, waits=[bt["y1"], bankWo.get(mp % 2)])
                return w_wo
            def mk_evr(mp):
                def w_evr(mp=mp):
                    bank = ps[4 + (mp % 2)]
                    t = sch.op("v", lambda c, mp=mp, bank=bank, buf3=buf3: c.tensor_tensor(
                        xt[buf3][:, 2 * mp * T:(2 * mp + 2) * T], bank[:, 0:2 * T],
                        xt[buf3][:, 2 * mp * T:(2 * mp + 2) * T], ALU.add),
                        waits=[bt[("wo", mp)], bt.get(("evr", mp - 1))])
                    bt[("evr", mp)] = t
                    bankWo[mp % 2] = t
                return w_evr
            def w_rsq():
                bt["rsq"] = sch.op("s", lambda c, buf3=buf3: c.activation(
                    sqb[:, :], xt[buf3][:, :], AF.Square), waits=[bt[("evr", 3)]])
            def w_stats1():
                def stats_1(c, buf=buf):
                    for k in range(KC):
                        c.matmul(ps[7][0:1, 0:T], onescol[:, 0:1], xb[buf][:, k * T:(k + 1) * T],
                                 start=(k == 0), stop=False)
                    for k in range(KC):
                        c.matmul(ps[7][0:1, 0:T], mrow_o[:, k:k + 1], y1b[:, k * T:(k + 1) * T],
                                 start=False, stop=(k == KC - 1))
                    r = None
                    for k in range(KC):
                        r = c.matmul(ps[7][0:1, 256:256 + T], onescol[:, 0:1],
                                     sqb[:, k * T:(k + 1) * T],
                                     start=(k == 0), stop=(k == KC - 1))
                    return r
                bt["st1"] = sch.op("t", stats_1, waits=[bt["rsq"], t_mrow])
                st1_tok[0] = bt["st1"]
                xb_free[buf] = bt["st1"]
            def w_row1():
                def row1(c):
                    c.tensor_copy(srowmB[0:1, 0:T], ps[7][0:1, 0:T])
                    return c.tensor_copy(srowmB[0:1, TH:TH + T], ps[7][0:1, 256:256 + T])
                bt["row1"] = sch.op("v", row1, waits=[bt["st1"]])
                row1_prev[0] = bt["row1"]
            def w_a1m():
                def a1m(c, buf=buf, buf3=buf3):
                    abc = ps[6][:, 0:T].unsqueeze(1).broadcast_to([128, KC, T])
                    xt3 = xt[buf3][:, :].rearrange("p (k t) -> p k t", k=KC)
                    x13 = x1t[buf][:, :].rearrange("p (k t) -> p k t", k=KC)
                    return c.tensor_tensor(x13, xt3, abc, ALU.mult)
                bt["a1m"] = sch.op("v", a1m, waits=[bt["1bc"], x1_dma[buf]])
            def w_a1a():
                def a1a(c, buf=buf):
                    bbc = ps[6][:, 256:256 + T].unsqueeze(1).broadcast_to([128, KC, T])
                    x13 = x1t[buf][:, :].rearrange("p (k t) -> p k t", k=KC)
                    return c.tensor_tensor(x13, x13, bbc, ALU.add)
                bt["a1a"] = sch.op("v", a1a, waits=[bt["a1m"]])
                xt_free[buf3] = bt["a1a"]
                a1_prev[0] = bt["a1a"]
            def w_stx1():
                gx = "gx0" if buf == 0 else "gx1"
                def stx1(e, b=b, s0=s0, buf=buf):
                    srcap = x1t[buf][:, :].rearrange("p (k t) -> p k t", k=KC)
                    dst = x1s_h[b, :, s0:s0 + T].rearrange("(k p) t -> p k t", p=128)
                    return e.dma_start(out=dst, in_=srcap)
                x1_dma[buf] = sch.op(gx, stx1, n_dma=1, waits=[bt["a1a"]])
            yrm = mk_rm("y", "rowy", srowm, rtmp)
            orm = mk_rm("1", "row1", srowmB, rtmpB)
            bp = dict(ty=w_ty, ysq=w_ysq, statsy=w_statsy, rowy=w_rowy,
                      yrm1=yrm[0], yrm2=yrm[1], yrm3=yrm[2], yrm4=yrm[3],
                      ysqrt=yrm[4], ynm=yrm[5],
                      ybc=mk_bc("y", srowm, extra=[a1_prev]),
                      apym=w_apym, apya=w_apya, gb=w_gb, y1=w_y1,
                      wo0=mk_wo(0), evr0=mk_evr(0), wo1=mk_wo(1), evr1=mk_evr(1),
                      wo2=mk_wo(2), evr2=mk_evr(2), wo3=mk_wo(3), evr3=mk_evr(3))
            tail = ([w_rsq, w_stats1, w_row1] + orm +
                    [mk_bc("1", srowmB), w_a1m, w_a1a, w_stx1])
            return bp, tail

        pp1l = []
        if "B" in phases:
            t_z1 = sch.op("v", lambda c: c.tensor_scalar(
                wsum1[:, 0:2], lamcol[:, 0:2], 0.0, None, ALU.mult), waits=[t_p0])
            halves = []
            for ch in range(KC):
                for h in range(4):
                    halves.append((w1t_h, ch, h, 0))
            for ch in range(CC):
                halves.append((w2t_h, ch, 0, 1))
            pp_ld = [None, None]
            pp_abs = [None]
            pp_add = [t_z1]
            def mk_pp(i):
                hh, ch, h, col_i = halves[i]
                pb = i % 2
                wid = 1024
                def e_ld():
                    def ldw(e, hh=hh, ch=ch, h=h, pb=pb, wid=wid):
                        return e.dma_start(out=wtmp1[pb][:, 0:wid],
                                           in_=hh[ch * 128:(ch + 1) * 128, h * wid:(h + 1) * wid])
                    tok = sch.op(f"pw{pb}", ldw, n_dma=1, waits=[pp_ld[pb]])
                    pp_ld[pb] = tok
                    mk_pp.lt[i] = tok
                def e_abs():
                    t = sch.op("s", lambda c, pb=pb, wid=wid: c.activation(
                        wtmp1[pb][:, 0:wid], wtmp1[pb][:, 0:wid], AF.Abs,
                        accum_out=wacc1[:, pb:pb + 1]),
                        waits=[mk_pp.lt[i], pp_abs[0], pp_add[0]])
                    pp_abs[0] = t
                    pp_ld[pb] = t
                    mk_pp.at[i] = t
                def e_add():
                    t = sch.op("v", lambda c, col_i=col_i, pb=pb: c.tensor_tensor(
                        wsum1[:, col_i:col_i + 1], wsum1[:, col_i:col_i + 1],
                        wacc1[:, pb:pb + 1], ALU.add), waits=[mk_pp.at[i], pp_add[0]])
                    pp_add[0] = t
                return [e_ld, e_abs, e_add]
            mk_pp.lt = {}
            mk_pp.at = {}
            for i in range(len(halves)):
                pp1l.extend(mk_pp(i))
        if "A" in phases:
            a_ld(0)
            for f in front_emitters(0):
                f()
            for ti in range(ntiles_a):
                a_ld(ti + 1)
                bp, tail = back_emitters(ti)
                fq = front_emitters(ti + 1)
                tl = pend_tail[0]
                def tpull(n=1, tl=tl):
                    for _ in range(n):
                        if tl:
                            tl.pop(0)()
                def fpull(n=1, fq=fq):
                    for _ in range(n):
                        if fq:
                            fq.pop(0)()
                tpull(2)
                if ti >= 1:
                    def warmA(c):
                        r = None
                        for _ in range(12):
                            r = c.matmul(ps[4][:, 0:128], ident[:, :], ident[:, :],
                                         start=True, stop=True)
                        return r
                    sch.op("t", warmA, waits=[bankWo.get(0), bankWo.get(1)])
                bp["ty"](); fpull()
                tpull()
                bp["ysq"](); fpull()
                tpull()
                bp["statsy"](); fpull()
                tpull()
                bp["rowy"](); fpull()
                tpull(); bp["yrm1"](); fpull()
                tpull(); bp["yrm2"](); fpull()
                tpull(); bp["yrm3"](); fpull()
                tpull(); bp["yrm4"](); fpull()
                tpull(); bp["ysqrt"](); fpull()
                tpull(); bp["ynm"](); fpull()
                tpull()
                bp["ybc"](); fpull()
                tpull(99)
                bp["apym"](); fpull(2)
                bp["apya"](); fpull(2)
                bp["gb"](); fpull(2)
                bp["y1"](); fpull(2)
                bp["wo0"](); fpull(2); bp["evr0"](); fpull(2)
                bp["wo1"](); fpull(2); bp["evr1"](); fpull(2)
                bp["wo2"](); fpull(2); bp["evr2"](); fpull(2)
                bp["wo3"](); fpull(99); bp["evr3"]()
                for _ in range(10):
                    if pp1l:
                        pp1l.pop(0)()
                pend_tail[0] = tail
            for f in pend_tail[0]:
                f()
        while pp1l:
            pp1l.pop(0)()
        pa_done = [x1_dma[0], x1_dma[1], prev_scan[0], xt_free[0], xt_free[1]]

        scopeA.close()

        kw1 = sb("kw1", [128, KC * DH], F8)
        kw2 = sb("kw2", [128, CC * D], F8)
        t_kw = t_mrow2 = None
        if "B" in phases:
            scopeP = contextlib.ExitStack()
            wtmp = [sb(f"wtmp{i}", [128, DH], F32, scopeP) for i in range(4)]
            wi8 = [sb(f"wi8{i}", [128, DH], I8, scopeP) for i in range(2)]

            def total_mm(c):
                c.matmul(ps[6][0:1, 0:1], wsum1[:, 0:1], onescolf[:, 0:1], start=True, stop=True)
                return c.matmul(ps[6][0:1, 1:2], wsum1[:, 1:2], onescolf[:, 0:1], start=True, stop=True)
            t_tot = sch.op("t", total_mm, waits=[pp_add[0], t_p0] + pa_done)

            def scalemath(c):
                c.tensor_copy(tinyrow[0:1, 0:2], ps[6][0:1, 0:2])
                c.barrier()
                c.tensor_scalar(tinyrow[0:1, 0:2], tinyrow[0:1, 0:2], 1.0 / (DH * D), None, ALU.mult)
                c.barrier()
                c.tensor_scalar(tinyrow[0:1, 0:2], tinyrow[0:1, 0:2], 1e-5, None, ALU.max)
                c.barrier()
                return c.reciprocal(tinyrow[0:1, 2:4], tinyrow[0:1, 0:2])
            t_sm = sch.op("v", scalemath, barriers=3, waits=[t_tot])

            def bc_scales(c):
                c.matmul(ps[6][:, 4:5], ones1[0:1, :], tinyrow[0:1, 2:3], start=True, stop=True)
                c.matmul(ps[6][:, 5:6], ones1[0:1, :], tinyrow[0:1, 3:4], start=True, stop=True)
                c.matmul(ps[6][:, 6:7], ones1[0:1, :], tinyrow[0:1, 1:2], start=True, stop=True)
                return c.matmul(ps[6][:, 7:8], ones1[0:1, :], tinyrow[0:1, 0:1], start=True, stop=True)
            t_bs = sch.op("t", bc_scales, waits=[t_sm])
            t_swc = sch.op("v", lambda c: c.tensor_copy(swcol[:, 0:4], ps[6][:, 4:8]), waits=[t_bs])
            t_cw = sch.op("v", lambda c: c.tensor_scalar(cwc[:, :], cwc[:, :],
                                                         swcol[:, 3:4], None, ALU.mult),
                          waits=[t_swc, t_p0])

            chunks2 = [("w1", w1t_h, ch, DH, 0, kw1) for ch in range(KC)] + \
                      [("w2", w2t_h, ch, D, 1, kw2) for ch in range(CC)]
            ld2 = [None, None, None, None]
            qa = {}
            qb = {}
            qc = {}
            qd = {}
            def eld(i):
                if i >= len(chunks2):
                    return
                nm, hh, ch, chw, scol, dst = chunks2[i]
                pb = i % 4
                pw = f"pw{pb}"
                def ldw(e, hh=hh, ch=ch, chw=chw, pb=pb):
                    return e.dma_start(out=wtmp[pb][:, 0:chw], in_=hh[ch * 128:(ch + 1) * 128, :])
                qa[("l", i)] = sch.op(pw, ldw, n_dma=1,
                                      waits=[ld2[pb], t_swc, t_cw] +
                                            (pa_done if i < 4 else []))
            def eq1(i):
                nm, hh, ch, chw, scol, dst = chunks2[i]
                pb = i % 4
                qa[i] = sch.op("v", lambda c, pb=pb, chw=chw, scol=scol: c.tensor_scalar(
                    wtmp[pb][:, 0:chw], wtmp[pb][:, 0:chw],
                    swcol[:, scol:scol + 1], 1.0, ALU.mult, ALU.min),
                    waits=[qa[("l", i)], qd.get(i - 4)])
            def eq2(i):
                nm, hh, ch, chw, scol, dst = chunks2[i]
                pb = i % 4
                qb[i] = sch.op("v", lambda c, pb=pb, chw=chw: c.tensor_scalar(
                    wtmp[pb][:, 0:chw], wtmp[pb][:, 0:chw], -1.0, None, ALU.max),
                    waits=[qa[i]])
            def eq3(i):
                nm, hh, ch, chw, scol, dst = chunks2[i]
                pb = i % 4
                wb = i % 2
                qc[i] = sch.op("v", lambda c, pb=pb, chw=chw, wb=wb: c.tensor_scalar(
                    wi8[wb][:, 0:chw], wtmp[pb][:, 0:chw], MAGIC, -MAGIC, ALU.add, ALU.add),
                    waits=[qb[i], qd.get(i - 2)])
            def eq4(i):
                nm, hh, ch, chw, scol, dst = chunks2[i]
                wb = i % 2
                qd[i] = sch.op("v", lambda c, chw=chw, wb=wb, dst=dst, ch=ch: c.tensor_copy(
                    dst[:, ch * chw:(ch + 1) * chw], wi8[wb][:, 0:chw]), waits=[qc[i]])
                ld2[i % 4] = qd[i]
            for i in range(4):
                eld(i)
            eq1(0)
            eq2(0)
            eq1(1)
            for i in range(len(chunks2)):
                eq3(i)
                if i + 2 < len(chunks2):
                    eq1(i + 2)
                eq4(i)
                eld(i + 4)
                if i + 1 < len(chunks2):
                    eq2(i + 1)
            t_kw = qd[len(chunks2) - 1]

            def mr2(c):
                for ch in range(CC):
                    c.reduce_sum(mrow_f[:, ch:ch + 1], kw2[:, ch * D:(ch + 1) * D], axis=AX.X)
                c.barrier()
                return c.tensor_copy(mrow2[:, :], mrow_f[:, 0:CC])
            t_mrow2 = sch.op("v", mr2, barriers=1, waits=[t_kw])
            scopeP.close()

        if "B" in phases:
            x1ld = [sb(f"x1ld{i}", [128, KC * TH], F32) for i in range(3)]
            xq = sb("xq", [128, KC * TH], BF16)
            convA = [sb(f"convA{i}", [128, T], F32) for i in range(2)]
            convB1 = sb("convB1", [128, T], F32)
            hsil = sb("hsil", [128, CC * T], F32)
            k2bf = [sb(f"k2bf{i}", [128, 2 * T], BF16) for i in range(2)]
            k2i8 = [sb(f"k2i8{i}", [128, 2 * T], I8) for i in range(2)]
            res2 = sb("res2", [128, KC * T], F32)
            habs1 = sb("habs1", [128, 2 * T], F32)
            outb = sb("outb", [128, KC * T], F32)
            q1m = sb("q1m", [128, 3 * TH], F32)
            zq1 = sb("zq1", [128, 4 * TH], F32)
            zqi1 = sb("zqi1", [128, 4 * TH], I8)
            srowq1 = sb("srowq1", [1, 2 * TH], F32)
            e2bc2 = sb("e2bc2", [128, T], F32)
            MB0 = 2 * TH
            HB0 = 2 * TH + 2 * TH
            CB0 = HB0 + 2 * T
            mbuf = res2
            convB = res2

            x1_free = [None, None, None]
            out_prev = [None]
            a2_prev = [None]
            fin_prev = [None]
            m1q_free = [None]
            stok1_free = [None]
            stok2_free = [None]
            srowq_free = [None]
            xqfree = [None, None]
            ln2row_tok = [None]
            res2q_free = [None, None]
            evm_prev = [None, None]
            xq_free_warm = [None]
            hsilfree = [None, None]
            mf1_tok = {}
            qx_tok = {}
            trq1_tok = {}
            r1_tok = {}
            bsq_tok = {}
            bcq1_tok = {}
            stageB = {}

            class Weaver:
                def __init__(self):
                    self.q = []
                def add(self, fn):
                    self.q.append(fn)
                def pull(self, n=1):
                    for _ in range(n):
                        if self.q:
                            self.q.pop(0)()
                def drain(self):
                    while self.q:
                        self.q.pop(0)()

            def b_load(ti):
                if ti >= ntiles_b:
                    return
                b, si = divmod(ti, NSEQT)
                s0 = si * T
                bufl = ti % 3
                qr = f"qr{bufl}"
                lo = max(s0 - 1, 0)
                hi = min(s0 - 1 + TH, S)
                doff = lo - (s0 - 1)
                ncols = hi - lo
                def ld(e, b=b, lo=lo, hi=hi, doff=doff, ncols=ncols, bufl=bufl):
                    srcap = x1s_h[b, :, lo:hi].rearrange("(k p) t -> p k t", p=128)
                    dst = x1ld[bufl][:, :].rearrange("p (k t) -> p k t", k=KC)[:, :, doff:doff + ncols]
                    return e.dma_start(out=dst, in_=srcap)
                wx = [x1_free[bufl]] + ([x1_dma[0], x1_dma[1], t_p0, t_kw, t_mrow2] if ti <= 2 else [])
                t_x1 = sch.op(qr, ld, n_dma=1, waits=wx)
                if si == 0 or si == NSEQT - 1:
                    zc = 0 if si == 0 else TH - 1
                    def zhalo(c, zc=zc, bufl=bufl):
                        r = None
                        for k in range(KC):
                            r = c.tensor_copy(x1ld[bufl][:, k * TH + zc:k * TH + zc + 1], zsb[:, 0:1])
                        return r
                    t_x1 = sch.op("v", zhalo, waits=[t_x1])
                stageB[("ld", ti)] = t_x1

            def add_fold_emitters(wv, ti):
                xl = x1ld[ti % 3]
                m1v = q1m[:, 0:TH]
                tmpX = [q1m[:, TH:2 * TH], q1m[:, 2 * TH:3 * TH]]
                st = {}
                def e_a0():
                    st["x1"] = stageB.pop(("ld", ti))
                    base = [st["x1"], m1q_free[0], mf1_tok.get(ti - 1)]
                    st[("f", 0)] = sch.op("v", lambda c: c.scalar_tensor_tensor(
                        m1v, xl[:, 0:TH], -1.0, xl[:, 0:TH], ALU.mult, ALU.max), waits=base)
                wv.add(e_a0)
                def mk_abs(k, wkey):
                    def e_abs(k=k, wkey=wkey):
                        tm = tmpX[k % 2]
                        st[("a", k)] = sch.op("v", lambda c, tm=tm, k=k: c.scalar_tensor_tensor(
                            tm, xl[:, k * TH:(k + 1) * TH], -1.0, xl[:, k * TH:(k + 1) * TH],
                            ALU.mult, ALU.max),
                            waits=[st["x1"], st.get(wkey),
                                   mf1_tok.get(ti - 1) if k <= 2 else None])
                    return e_abs
                def mk_fold(k):
                    def e_fold(k=k):
                        tm = tmpX[k % 2]
                        st[("f", k)] = sch.op("v", lambda c, tm=tm: c.tensor_tensor(
                            m1v, m1v, tm, ALU.max), waits=[st[("a", k)], st[("f", k - 1)]])
                        if k == KC - 1:
                            mf1_tok[ti] = st[("f", k)]
                    return e_fold
                wv.add(mk_abs(1, None))
                wv.add(mk_abs(2, None))
                for k in range(1, KC):
                    wv.add(mk_fold(k))
                    if k + 2 < KC:
                        wv.add(mk_abs(k + 2, ("f", k)))

            def emit_trq1(ti):
                def trq(c):
                    c.transpose(ps[0][:, 0:128], q1m[:, 0:128], ident[:, :])
                    c.transpose(ps[0][:, 128:256], q1m[:, 128:256], ident[:, :])
                    return c.transpose(ps[0][:, 256:384], q1m[:, 130:258], ident[:, :])
                trq1_tok[ti] = sch.op("t", trq, waits=[mf1_tok[ti], fin_prev[0]])
                m1q_free[0] = trq1_tok[ti]

            def r1_emitters(ti):
                st = {}
                def r1a():
                    def f(c):
                        r = None
                        for j in range(3):
                            r = c.reduce_max(stok[:, j:j + 1], ps[0][:, j * 128:(j + 1) * 128],
                                             axis=AX.X)
                        return r
                    st["a"] = sch.op("v", f, waits=[trq1_tok[ti], stok1_free[0]])
                def r1b():
                    st["b"] = sch.op("v", lambda c: c.tensor_scalar(
                        stok[:, 0:3], stok[:, 0:3], 1e-5, None, ALU.max), waits=[st["a"]])
                def r1c():
                    st["c"] = sch.op("v", lambda c: c.reciprocal(stok[:, 4:7], stok[:, 0:3]),
                                     waits=[st["b"]])
                def r1d():
                    st["d"] = sch.op("v", lambda c: c.tensor_scalar(
                        stok[:, 4:7], stok[:, 4:7], 127.0, None, ALU.mult), waits=[st["c"]])
                def r1e():
                    st["e"] = sch.op("v", lambda c: c.tensor_scalar(
                        stok[:, 8:11], stok[:, 0:3], 1.0 / 127.0, None, ALU.mult),
                        waits=[st["b"]])
                    r1_tok[ti] = (st["d"], st["e"])
                return [r1a, r1b, r1c, r1d, r1e]

            def emit_btq1(ti):
                def btq(c):
                    for j in range(3):
                        c.transpose(ps[1][0:1, j * 128:(j + 1) * 128], stok[:, 4 + j:5 + j],
                                    ident[:, :])
                    c.transpose(ps[1][0:1, 384:512], stok[:, 8:9], ident[:, :])
                    c.transpose(ps[0][0:1, 384:512], stok[:, 9:10], ident[:, :])
                    return c.transpose(ps[0][0:1, 0:128], stok[:, 10:11], ident[:, :])
                d, e = r1_tok[ti]
                t = sch.op("t", btq, waits=[d, e, fin_prev[0]])
                stok1_free[0] = t
                bsq_tok[ti] = t

            def emit_rowsq1(ti):
                def rowsq(c):
                    c.tensor_copy(srowq1[0:1, 0:256], ps[1][0:1, 0:256])
                    c.tensor_copy(srowq1[0:1, 256:258], ps[1][0:1, 382:384])
                    c.tensor_copy(srowq1[0:1, TH:TH + 128], ps[1][0:1, 384:512])
                    c.tensor_copy(srowq1[0:1, TH + 128:TH + 256], ps[0][0:1, 384:512])
                    return c.tensor_copy(srowq1[0:1, TH + 256:TH + 258], ps[0][0:1, 126:128])
                bsq_tok[ti] = sch.op("v", rowsq, waits=[bsq_tok[ti], srowq_free[0]])

            def emit_bcq1(ti):
                def bcq(c):
                    c.matmul(ps[0][:, 0:TH], ones1[0:1, :], srowq1[0:1, 0:TH],
                             start=True, stop=True)
                    return c.matmul(ps[1][:, 0:TH], ones1[0:1, :], srowq1[0:1, TH:2 * TH],
                                    start=True, stop=True)
                t = sch.op("t", bcq, waits=[bsq_tok[ti]])
                srowq_free[0] = t
                bcq1_tok[ti] = t

            def qx_emitters(ti):
                bufl = ti % 3
                s1bb = ps[0][:, 0:TH].unsqueeze(1).broadcast_to([128, 2, TH])
                s1ib = ps[1][:, 0:TH].unsqueeze(1).broadcast_to([128, 2, TH])
                zqQ = lambda j: zq1[:, j * 2 * TH:(j + 1) * 2 * TH]
                ziQ = lambda j: zqi1[:, j * 2 * TH:(j + 1) * 2 * TH]
                st = {}
                out = []
                for kp in range(KC // 2):
                    jj = kp % 2
                    def e_qm(kp=kp, jj=jj):
                        def f(c, kp=kp, jj=jj, bufl=bufl):
                            x3 = x1ld[bufl][:, 2 * kp * TH:(2 * kp + 2) * TH].rearrange(
                                "p (k t) -> p k t", k=2)
                            z3 = zqQ(jj)[:, :].rearrange("p (k t) -> p k t", k=2)
                            return c.tensor_tensor(z3, x3, s1bb, ALU.mult)
                        st[("m", kp)] = sch.op("v", f, waits=[bcq1_tok[ti], st.get(("i", kp - 2))] +
                                               ([xqfree[0], xqfree[1]] if kp <= 1 else []))
                    def e_qi(kp=kp, jj=jj):
                        def f(c, jj=jj):
                            return c.tensor_scalar(ziQ(jj), zqQ(jj), MAGIC, -MAGIC,
                                                   ALU.add, ALU.add)
                        st[("i", kp)] = sch.op("v", f, waits=[st[("m", kp)], st.get(("q", kp - 2))])
                    def e_qq(kp=kp, jj=jj):
                        def f(c, kp=kp, jj=jj):
                            zi3 = ziQ(jj).rearrange("p (k t) -> p k t", k=2)
                            xq3 = xq[:, 2 * kp * TH:(2 * kp + 2) * TH].rearrange(
                                "p (k t) -> p k t", k=2)
                            return c.tensor_tensor(xq3, zi3, s1ib, ALU.mult)
                        st[("q", kp)] = sch.op("v", f, waits=[st[("i", kp)]])
                        if kp == KC // 2 - 1:
                            qx_tok[ti] = st[("q", kp)]
                    out += [e_qm, e_qi, e_qq]
                return out

            def b_s2(ti, weave=()):
                buf = ti % 2
                t_qx = qx_tok[ti]
                silu_toks = []
                sil_prev = {}
                cB_prev = {}
                conv_done = {}
                wq = list(weave)
                def wpull(n=1):
                    for _ in range(n):
                        if wq:
                            wq.pop(0)()
                cAap = lambda pb: convA[pb][:, 0:T]
                cBap = lambda pb: convB[:, CB0:CB0 + T] if pb == 0 else convB1[:, 0:T]
                habs_free = {}
                def habs_ap(gb):
                    return mbuf[:, HB0:HB0 + 2 * T] if gb == 0 else habs1[:, :]
                mAB = [lambda: mbuf[:, MB0:MB0 + 2 * T], lambda: res2[:, 0:2 * T]]
                acc_t = {0: None, 1: None}
                def habsfold(pr):
                    gb = pr % 2
                    t_ab = sch.op("s", lambda c, pr=pr, gb=gb: c.activation(
                        habs_ap(gb), hsil[:, 2 * pr * T:(2 * pr + 2) * T], AF.Abs),
                        waits=[silu_toks[2 * pr], silu_toks[2 * pr + 1], habs_free.get(pr - 2)])
                    lane = pr % 2
                    if acc_t[lane] is None:
                        t_fo = sch.op("v", lambda c, gb=gb, lane=lane: c.tensor_copy(
                            mAB[lane](), habs_ap(gb)),
                            waits=[t_ab, res2q_free[1] if lane == 0 else res2q_free[0]])
                    else:
                        t_fo = sch.op("v", lambda c, gb=gb, lane=lane: c.tensor_tensor(
                            mAB[lane](), mAB[lane](), habs_ap(gb), ALU.max),
                            waits=[t_ab, acc_t[lane]])
                    acc_t[lane] = t_fo
                    habs_free[pr] = t_fo
                for pr in range(CC // 2):
                    mmt = {}
                    for half in range(2):
                        ch = 2 * pr + half
                        bank = ps[ch % 4]
                        def mmw1(c, ch=ch, bank=bank):
                            r = None
                            for k in range(KC):
                                r = c.matmul(bank[:, 0:TH],
                                             kw1[:, k * DH + ch * 128:k * DH + (ch + 1) * 128],
                                             xq[:, k * TH:(k + 1) * TH],
                                             start=(k == 0), stop=(k == KC - 1))
                            return r
                        mmt[ch] = sch.op("t", mmw1, waits=[t_qx, conv_done.get(ch - 4)] +
                                         ([evm_prev[ch - 2]] if ch in (2, 3) else []))
                    cAt = {}
                    for half in range(2):
                        ch = 2 * pr + half
                        bank = ps[ch % 4]
                        pb = ch % 2
                        cAt[ch] = sch.op("s", lambda c, ch=ch, bank=bank, pb=pb: c.activation(
                            cAap(pb), bank[:, 2:2 + T], AF.Identity,
                            bias=cbc[:, ch:ch + 1], scale=cwc[:, ch * 3 + 2:ch * 3 + 3]),
                            waits=[mmt[ch], cB_prev.get(ch - 2)])
                    s1t = {}
                    for half in range(2):
                        ch = 2 * pr + half
                        bank = ps[ch % 4]
                        pb = ch % 2
                        s1t[ch] = sch.op("v", lambda c, ch=ch, bank=bank, pb=pb:
                            c.scalar_tensor_tensor(cBap(pb), bank[:, 0:T],
                                                   cwc[:, ch * 3 + 0:ch * 3 + 1], cAap(pb),
                                                   ALU.mult, ALU.add),
                            waits=[cAt[ch], sil_prev.get(ch - 2)])
                    s2t = {}
                    for half in range(2):
                        ch = 2 * pr + half
                        bank = ps[ch % 4]
                        pb = ch % 2
                        s2t[ch] = sch.op("v", lambda c, ch=ch, bank=bank, pb=pb:
                            c.scalar_tensor_tensor(cBap(pb), bank[:, 1:1 + T],
                                                   cwc[:, ch * 3 + 1:ch * 3 + 2], cBap(pb),
                                                   ALU.mult, ALU.add),
                            waits=[s1t[ch]])
                        conv_done[ch] = s2t[ch]
                        cB_prev[ch] = s2t[ch]
                    for half in range(2):
                        ch = 2 * pr + half
                        pb = ch % 2
                        t_sil = sch.op("s", lambda c, ch=ch, pb=pb: c.activation(
                            hsil[:, ch * T:(ch + 1) * T], cBap(pb), AF.Silu),
                            waits=[s2t[ch]] +
                                  ([hsilfree[0], hsilfree[1]] if ch <= 1 else []))
                        sil_prev[ch] = t_sil
                        silu_toks.append(t_sil)
                    wpull(1)
                    if pr >= 1:
                        habsfold(pr - 1)
                wpull(99)
                habsfold(CC // 2 - 1)
                xqfree[0] = conv_done[CC - 2]
                xqfree[1] = conv_done[CC - 1]
                xq_free_warm[0] = mmt[CC - 1]
                t_j = sch.op("v", lambda c: c.tensor_tensor(
                    mAB[0](), mAB[0](), mAB[1](), ALU.max), waits=[acc_t[0], acc_t[1]])
                t_fin = sch.op("v", lambda c: c.tensor_tensor(
                    mbuf[:, MB0:MB0 + T], mbuf[:, MB0:MB0 + T], mbuf[:, MB0 + T:MB0 + 2 * T],
                    ALU.max), waits=[t_j])
                stageB[("mf2", ti)] = t_fin
                fin_prev[0] = t_fin

            def b_s3(ti, wv):
                t_fin = stageB.pop(("mf2", ti))
                def warm(c):
                    r = None
                    for _ in range(26):
                        r = c.matmul(ps[2][:, 0:256], kw1[:, 0:128], xq[:, 0:256],
                                     start=True, stop=True)
                    return r
                sch.op("t", warm, waits=[a2_prev[0], xq_free_warm[0], xqfree[0], xqfree[1]])
                def trq2(c):
                    c.transpose(ps[6][:, 0:128], mbuf[:, MB0:MB0 + 128], ident[:, :])
                    return c.transpose(ps[6][:, 128:256], mbuf[:, MB0 + 128:MB0 + 256], ident[:, :])
                t_tq2 = sch.op("t", trq2, waits=[t_fin, hsilfree[0], hsilfree[1]])
                wv.pull()
                def r2a(c):
                    r = None
                    for j in range(NB):
                        r = c.reduce_max(stok[:, 12 + j:13 + j], ps[6][:, j * 128:(j + 1) * 128],
                                         axis=AX.X)
                    return r
                t_a = sch.op("v", r2a, waits=[t_tq2, stok2_free[0]])
                wv.pull()
                t_b = sch.op("v", lambda c: c.tensor_scalar(
                    stok[:, 12:12 + NB], stok[:, 12:12 + NB], 1e-5, None, ALU.max), waits=[t_a])
                wv.pull()
                t_c = sch.op("v", lambda c: c.reciprocal(stok[:, 16:16 + NB], stok[:, 12:12 + NB]),
                             waits=[t_b])
                wv.pull()
                t_d = sch.op("v", lambda c: c.tensor_scalar(
                    stok[:, 16:16 + NB], stok[:, 16:16 + NB], 127.0, None, ALU.mult), waits=[t_c])
                wv.pull()
                t_e = sch.op("v", lambda c: c.tensor_scalar(
                    stok[:, 20:20 + NB], stok[:, 12:12 + NB], 1.0 / 127.0, swcol[:, 2:3],
                    ALU.mult, ALU.mult), waits=[t_b])
                wv.pull()
                def btq2(c):
                    for j in range(NB):
                        c.transpose(ps[6][0:1, 256 + j * 128:256 + (j + 1) * 128],
                                    stok[:, 16 + j:17 + j], ident[:, :])
                    r = None
                    for j in range(NB):
                        r = c.transpose(ps[7][0:1, j * 128:(j + 1) * 128],
                                        stok[:, 20 + j:21 + j], ident[:, :])
                    return r
                t_bq2 = sch.op("t", btq2, waits=[t_d, t_e, a2_prev[0]])
                stok2_free[0] = t_bq2
                wv.pull()
                def rowsq2(c):
                    c.tensor_copy(srowq1[0:1, 0:T], ps[6][0:1, 256:256 + T])
                    return c.tensor_copy(srowq1[0:1, TH:TH + T], ps[7][0:1, 0:T])
                t_rowq2 = sch.op("v", rowsq2, waits=[t_bq2, srowq_free[0]])
                wv.pull()
                def bcq2(c):
                    c.matmul(ps[6][:, 0:T], ones1[0:1, :], srowq1[0:1, 0:T], start=True, stop=True)
                    return c.matmul(ps[6][:, 256:256 + T], ones1[0:1, :], srowq1[0:1, TH:TH + T],
                                    start=True, stop=True)
                t_bcq2 = sch.op("t", bcq2, waits=[t_rowq2])
                srowq_free[0] = t_bcq2
                wv.pull()
                t_e2c = sch.op("v", lambda c: c.tensor_copy(e2bc2[:, :], ps[6][:, 256:256 + T]),
                               waits=[t_bcq2, ln2row_tok[0]])
                stageB[("cpbc", ti)] = t_e2c
                stageB[("s2bc", ti)] = t_bcq2

            def b_s4(ti, wv, m1l, qxl, nxt):
                buf = ti % 2
                bufl = ti % 3
                b, si = divmod(ti, NSEQT)
                s0 = si * T
                t_cpbc = stageB.pop(("cpbc", ti))
                t_s2bc = stageB.pop(("s2bc", ti))
                k2_used = {}
                kc_tok = {}
                w2_prev = [None]
                zkm_tok = {}
                zki_tok = {}
                s2b = ps[6][:, 0:T].unsqueeze(1).broadcast_to([128, 2, T])
                zqA = lambda j: res2[:, 0:2 * T] if j == 0 else res2[:, MB0:MB0 + 2 * T]
                def filler(pr):
                    if wv.q:
                        wv.pull()
                    elif pr >= 5 and m1l:
                        m1l.pop(0)()
                for pr in range(CC // 2):
                    kb = pr % 2
                    def zkm(c, pr=pr, kb=kb):
                        h3 = hsil[:, 2 * pr * T:(2 * pr + 2) * T].rearrange("p (k t) -> p k t", k=2)
                        z3 = zqA(kb)[:, :].rearrange("p (k t) -> p k t", k=2)
                        return c.tensor_tensor(z3, h3, s2b, ALU.mult)
                    t_m = sch.op("v", zkm, waits=[t_s2bc, zki_tok.get(pr - 2)] +
                                 ([a2_prev[0], ln2row_tok[0]] if pr <= 1 else []))
                    zkm_tok[pr] = t_m
                    filler(pr)
                    def zki(c, pr=pr, kb=kb):
                        return c.tensor_scalar(k2i8[kb][:, 0:2 * T], zqA(kb)[:, 0:2 * T],
                                               MAGIC, -MAGIC, ALU.add, ALU.add)
                    t_i = sch.op("v", zki, waits=[t_m, kc_tok.get(pr - 2)])
                    zki_tok[pr] = t_i
                    filler(pr)
                    t_kc = sch.op("s", lambda c, kb=kb: c.activation(
                        k2bf[kb][:, :], k2i8[kb][:, 0:2 * T], AF.Identity),
                        waits=[t_i, k2_used.get(pr - 2)])
                    kc_tok[pr] = t_kc
                    def mmw2(c, pr=pr, kb=kb):
                        for half in range(2):
                            ch = 2 * pr + half
                            for m in range(KC):
                                c.matmul(ps[2 + m // 2][:, (m % 2) * 256:(m % 2) * 256 + T],
                                         kw2[:, ch * D + m * 128:ch * D + (m + 1) * 128],
                                         k2bf[kb][:, half * T:(half + 1) * T],
                                         start=(ch == 0 and m % 2 == 0),
                                         stop=(ch == CC - 1 and m % 2 == 1))
                        r = None
                        for half in range(2):
                            ch = 2 * pr + half
                            r = c.matmul(ps[7][0:1, 256:256 + T], mrow2[:, ch:ch + 1],
                                         k2bf[kb][:, half * T:(half + 1) * T],
                                         start=(ch == 0), stop=(ch == CC - 1))
                        return r
                    t_m2 = sch.op("t", mmw2, waits=[t_kc, w2_prev[0]])
                    w2_prev[0] = t_m2
                    k2_used[pr] = t_m2
                hsilfree[0] = zkm_tok[CC // 2 - 2]
                hsilfree[1] = zkm_tok[CC // 2 - 1]
                res2q_free[0] = zki_tok[CC // 2 - 2]
                res2q_free[1] = zki_tok[CC // 2 - 1]
                wv.drain()
                while m1l:
                    m1l.pop(0)()
                while qxl:
                    qxl.pop(0)()
                if nxt is not None:
                    def warm2(c):
                        r = None
                        for _ in range(22):
                            r = c.matmul(ps[0][:, 384:512], ident[:, :], ident[:, :],
                                         start=True, stop=True)
                        return r
                    sch.op("t", warm2, waits=[w2_prev[0], bsq_tok.get(nxt)])

                ev_m = {}
                ev_a = {}
                def mk_evm(mp):
                    def f(c, mp=mp):
                        pr3 = ps[2 + mp][:, 0:2 * T].rearrange("p (k t) -> p k t", k=2)
                        r3 = outb[:, 2 * mp * T:(2 * mp + 2) * T].rearrange("p (k t) -> p k t", k=2)
                        e3 = e2bc2[:, :].unsqueeze(1).broadcast_to([128, 2, T])
                        return c.tensor_tensor(r3, pr3, e3, ALU.mult)
                    return f
                def mk_eva(mp):
                    def f(c, mp=mp, bufl=bufl):
                        r3 = outb[:, 2 * mp * T:(2 * mp + 2) * T].rearrange("p (k t) -> p k t", k=2)
                        x3 = x1ld[bufl][:, :].rearrange("p (k t) -> p k t", k=KC)[:, 2 * mp:2 * mp + 2, 1:1 + T]
                        return c.tensor_tensor(r3, r3, x3, ALU.add)
                    return f
                ev_m[0] = sch.op("v", mk_evm(0), waits=[w2_prev[0], t_cpbc, out_prev[0]])
                ev_m[1] = sch.op("v", mk_evm(1), waits=[w2_prev[0], t_cpbc, out_prev[0]])
                ev_a[0] = sch.op("v", mk_eva(0), waits=[ev_m[0]])
                ev_m[2] = sch.op("v", mk_evm(2), waits=[w2_prev[0], t_cpbc, out_prev[0]])
                ev_a[1] = sch.op("v", mk_eva(1), waits=[ev_m[1], ev_a[0]])
                ev_m[3] = sch.op("v", mk_evm(3), waits=[w2_prev[0], t_cpbc, out_prev[0]])
                ev_a[2] = sch.op("v", mk_eva(2), waits=[ev_m[2], ev_a[1]])
                ev_a[3] = sch.op("v", mk_eva(3), waits=[ev_m[3], ev_a[2]])
                evm_prev[0] = ev_m[0]
                evm_prev[1] = ev_m[1]
                x1_free[bufl] = ev_a[3]
                t_rsq2 = sch.op("s", lambda c: c.activation(sqb[:, :], outb[:, :], AF.Square),
                                waits=[ev_a[3]])

                tl = {}
                def w_st2():
                    def st2(c):
                        r = None
                        for k in range(KC):
                            r = c.matmul(ps[7][0:1, 0:T], onescol[:, 0:1],
                                         sqb[:, k * T:(k + 1) * T],
                                         start=(k == 0), stop=(k == KC - 1))
                        return r
                    tl["st2"] = sch.op("t", st2, waits=[t_rsq2])
                def w_rw1():
                    def rw1(c):
                        c.tensor_copy(srowm[0:1, 0:T], ps[7][0:1, 256:256 + T])
                        return c.tensor_copy(srowm[0:1, TH:TH + T], ps[7][0:1, 0:T])
                    tl["rw1"] = sch.op("v", rw1, waits=[tl["st2"]])
                def w_rw2():
                    tl["rw2"] = sch.op("v", lambda c: c.tensor_tensor(
                        srowm[0:1, 0:T], srowm[0:1, 0:T], e2bc2[0:1, 0:T], ALU.mult),
                        waits=[tl["rw1"]])
                    ln2row_tok[0] = tl["rw2"]
                def w_rm1():
                    def rm1(c):
                        c.tensor_scalar(srowm[0:1, 0:T], srowm[0:1, 0:T], INV_D, None, ALU.mult)
                        return c.tensor_scalar(srowm[0:1, TH:TH + T], srowm[0:1, TH:TH + T],
                                               INV_D, None, ALU.mult)
                    tl["rm1"] = sch.op("v", rm1, waits=[tl["rw2"]])
                def w_rm2():
                    tl["rm2"] = sch.op("v", lambda c: c.tensor_tensor(
                        rtmp[0:1, 0:T], srowm[0:1, 0:T], srowm[0:1, 0:T], ALU.mult),
                        waits=[tl["rm1"]])
                def w_rm3():
                    tl["rm3"] = sch.op("v", lambda c: c.scalar_tensor_tensor(
                        srowm[0:1, TH:TH + T], srowm[0:1, TH:TH + T], EPS, rtmp[0:1, 0:T],
                        ALU.add, ALU.subtract), waits=[tl["rm2"]])
                def w_rm4():
                    tl["rm4"] = sch.op("v", lambda c: c.reciprocal(
                        rtmp[0:1, 0:T], srowm[0:1, TH:TH + T]), waits=[tl["rm3"]])
                def w_sq():
                    tl["sq"] = sch.op("s", lambda c: c.activation(
                        srowm[0:1, TH:TH + T], rtmp[0:1, 0:T], AF.Sqrt), waits=[tl["rm4"]])
                def w_nm():
                    tl["nm"] = sch.op("v", lambda c: c.scalar_tensor_tensor(
                        srowm[0:1, 0:T], srowm[0:1, 0:T], -1.0, srowm[0:1, TH:TH + T],
                        ALU.mult, ALU.mult), waits=[tl["sq"]])
                def w_bc():
                    def bcast(c):
                        c.matmul(ps[7][:, 0:T], ones1[0:1, :], srowm[0:1, TH:TH + T],
                                 start=True, stop=True)
                        return c.matmul(ps[7][:, 256:256 + T], ones1[0:1, :], srowm[0:1, 0:T],
                                        start=True, stop=True)
                    tl["bc"] = sch.op("t", bcast, waits=[tl["nm"]])
                def w_a2m():
                    def a2m(c):
                        abc = ps[7][:, 0:T].unsqueeze(1).broadcast_to([128, KC, T])
                        r3 = outb[:, :].rearrange("p (k t) -> p k t", k=KC)
                        return c.tensor_tensor(r3, r3, abc, ALU.mult)
                    tl["a2m"] = sch.op("v", a2m, waits=[tl["bc"]])
                def w_a2a():
                    def a2a(c):
                        bbc = ps[7][:, 256:256 + T].unsqueeze(1).broadcast_to([128, KC, T])
                        r3 = outb[:, :].rearrange("p (k t) -> p k t", k=KC)
                        return c.tensor_tensor(r3, r3, bbc, ALU.add)
                    tl["a2a"] = sch.op("v", a2a, waits=[tl["a2m"]])
                    a2_prev[0] = tl["a2a"]
                def w_sto():
                    def sto(e, b=b, s0=s0):
                        srcap = outb[:, :].rearrange("p (k t) -> p k t", k=KC)
                        dst = out_h[b, :, s0:s0 + T].rearrange("(k p) t -> p k t", p=128)
                        return e.dma_start(out=dst, in_=srcap)
                    out_prev[0] = sch.op("go0", sto, n_dma=1, waits=[tl["a2a"]])
                return [w_st2, w_rw1, w_rw2, w_rm1, w_rm2, w_rm3, w_rm4, w_sq, w_nm,
                        w_bc, w_a2m, w_a2a, w_sto]

            b_load(0)
            b_load(1)
            b_load(2)
            wv0 = Weaver()
            add_fold_emitters(wv0, 0)
            wv0.drain()
            emit_trq1(0)
            for f in r1_emitters(0):
                f()
            emit_btq1(0)
            emit_rowsq1(0)
            emit_bcq1(0)
            for f in qx_emitters(0):
                f()
            b_s2(0)
            for ti in range(ntiles_b):
                nxt = ti + 1 if ti + 1 < ntiles_b else None
                wv = Weaver()
                if nxt is not None:
                    add_fold_emitters(wv, nxt)
                m1l = ([(lambda n=nxt: emit_trq1(n))] + r1_emitters(nxt) +
                       [(lambda n=nxt: emit_btq1(n)), (lambda n=nxt: emit_rowsq1(n)),
                        (lambda n=nxt: emit_bcq1(n))]) if nxt is not None else []
                qxl = qx_emitters(nxt) if nxt is not None else []
                b_s3(ti, wv)
                tail = b_s4(ti, wv, m1l, qxl, nxt)
                if nxt is not None:
                    b_s2(nxt, weave=tail)
                else:
                    for f in tail:
                        f()
                b_load(ti + 3)
            final_toks = [out_prev[0]]

        else:
            def dbg(e):
                rs = []
                for b in range(BPC):
                    for c0 in range(KC):
                        rs.append(e.dma_start(out=out_h[b, c0 * 128:(c0 + 1) * 128, :],
                                              in_=x1s_h[b, c0 * 128:(c0 + 1) * 128, :]))
                return rs
            final_toks = [sch.op("qz", dbg, n_dma=BPC * KC, waits=pa_done)]

        sch.op("g", lambda c: c.engine_nop(), waits=final_toks)

        secfns = {"tensor": blk.tensor, "vector": blk.vector, "scalar": blk.scalar,
                  "gpsimd": blk.gpsimd, "sync": blk.sync}
        sem2sec = {"t": "tensor", "v": "vector", "s": "scalar", "g": "gpsimd",
                   "qp": "sync", "qx0": "sync", "qx1": "sync", "qx2": "sync", "qr0": "sync", "qr2": "sync",
                   "qr1": "sync", "pw0": "sync", "pw1": "sync", "pw2": "sync", "pw3": "sync", "qz": "sync",
                   "qw": "gpsimd", "gx0": "gpsimd", "gx1": "gpsimd",
                   "go0": "gpsimd", "go1": "gpsimd"}
        sec_ops = {}
        order = []
        for entry in sch.ops:
            secname = sem2sec[entry[1]]
            if secname not in sec_ops:
                sec_ops[secname] = (secfns[secname], [])
                order.append(secname)
            sec_ops[secname][1].append(entry)
        import os
        if os.environ.get("MK_DEBUG_CNT"):
            print("SCHED CNT:", {k: v for k, v in sch.cnt.items() if v})
            from collections import Counter
            print("OPS PER SEM:", Counter(e[1] for e in sch.ops))
        for sid in order:
            sec, entries = sec_ops[sid]
            def run(eng, entries=entries):
                for (kind, e, emitfn, narg, waits, base) in entries:
                    for (ws, wn) in waits:
                        eng.wait_ge(sch.sems[ws], wn)
                    if kind == "dma":
                        ins = emitfn(eng)
                        if not isinstance(ins, (list, tuple)):
                            ins = [ins]
                        assert len(ins) == narg, f"dma {e}: {len(ins)} != {narg}"
                        for i_ in ins:
                            i_.then_inc(sch.sems[e], 16)
                    else:
                        ctx = Ctx(eng, e, base, sch.sems)
                        last = emitfn(ctx)
                        assert ctx.nbar == narg, f"{e}: {ctx.nbar} barriers != {narg}"
                        assert last is not None, f"op on {e} returned None"
                        last.then_inc(sch.sems[e], 1)
            sec(run)
    return nc


_NC_CACHE = {}


def kernel(**inputs):
    from concourse.bass_utils import run_bass_kernel_spmd
    if "nc" not in _NC_CACHE:
        _NC_CACHE["nc"] = build_nc()
    nc = _NC_CACHE["nc"]
    in_maps = [host_prepare(inputs, core) for core in range(8)]
    res = run_bass_kernel_spmd(nc, in_maps, core_ids=list(range(8)))
    out = np.concatenate([np.asarray(r["out"]) for r in res.results], axis=0)
    return np.ascontiguousarray(out.transpose(0, 2, 1)).astype(np.float32)
